# version 03
# baseline (speedup 1.0000x reference)
import contextlib
import math
import numpy as np
import ml_dtypes
import concourse.bass as bass
import concourse.mybir as mybir
from concourse.bass_utils import run_bass_kernel_spmd

F32 = mybir.dt.float32
BF16 = mybir.dt.bfloat16
AF = mybir.ActivationFunctionType
ALU = mybir.AluOpType
AX = mybir.AxisListType

T = 4096
D = 1024
NCH = D // 128
NTT = T // 512
NQT = T // 128
NEG = -1.0e30
LN_EPS = 1e-5

ENGINES = ("pe", "dve", "act", "pool", "sp")

GPOS = [int(v) for v in np.linspace(0, T - 1, 8).astype(np.int64)]
GSTEP = GPOS[1] - GPOS[0]


class Tile:
    __slots__ = ("ap", "name", "lw", "rd")

    def __init__(self, ap, name):
        self.ap = ap
        self.name = name
        self.lw = None
        self.rd = []

    def __getitem__(self, key):
        return self.ap[key]


class K:

    NDQ = 12

    def __init__(self, nc, ctx):
        self.nc = nc
        self.ctx = ctx
        self.prog = {e: [] for e in ENGINES}
        self.chans = list(ENGINES)
        self.sem = {e: ctx.enter_context(nc.semaphore("sem_" + e))
                    for e in ENGINES}
        self.dma_rr = {}
        for e in ("sp", "pool", "act"):
            for j in range(self.NDQ):
                ch = "dq_%s_%d" % (e, j)
                self.chans.append(ch)
                self.sem[ch] = ctx.enter_context(nc.semaphore(ch))
            self.dma_rr[e] = 0
        self.cnt = {c: 0 for c in self.chans}
        self.waited = {e1: {c: 0 for c in self.chans} for e1 in ENGINES}
        self.dump_names = set()
        self.dbg = {}

    def _uniq(self, name):
        self._serial = getattr(self, "_serial", 0) + 1
        return "%s__%d" % (name, self._serial)

    def sb(self, name, shape, dt=F32, ctx=None):
        h = (ctx or self.ctx).enter_context(
            self.nc.sbuf_tensor(self._uniq(name), list(shape), dt))
        return Tile(h, name)

    def psum(self, name, shape, dt=F32, ctx=None):
        h = (ctx or self.ctx).enter_context(
            self.nc.psum_tensor(self._uniq(name), list(shape), dt))
        return Tile(h, name)

    def dram(self, name, shape, dt=F32, kind="Internal"):
        if kind == "Internal":
            name = self._uniq(name)
        h = self.nc.dram_tensor(name, list(shape), dt, kind=kind)
        return Tile(h.ap(), name)

    @contextlib.contextmanager
    def scope(self):
        with contextlib.ExitStack() as st:
            yield st
            self.barrier()

    def _need(self, eng, dep):
        if dep is None:
            return
        e2, c = dep
        if c <= 0:
            return
        if e2 == eng and eng == "pe":
            return
        if self.waited[eng][e2] >= c:
            return
        self.waited[eng][e2] = c
        sem = self.sem[e2]
        self.prog[eng].append(lambda e, s=sem, v=c: e.wait_ge(s, v))

    def _dma_issue(self, eng, fn, reads, writes):
        slot = self.dma_rr[eng] % self.NDQ
        self.dma_rr[eng] += 1
        ch = "dq_%s_%d" % (eng, slot)
        for t in reads:
            self._need(eng, t.lw)
        for t in writes:
            self._need(eng, t.lw)
            for r in t.rd:
                self._need(eng, r)
        self._need(eng, (ch, self.cnt[ch]))
        self.cnt[ch] += 16
        sem = self.sem[ch]
        th = lambda e, f=fn, s=sem: f(e).then_inc(s, 16)
        self.prog[eng].append(th)
        pos = (ch, self.cnt[ch])
        for t in reads:
            t.rd.append(pos)
        for t in writes:
            t.lw = pos
            t.rd = []
        return pos

    def op(self, eng, fn, reads=(), writes=(), inc=1):
        for t in reads:
            self._need(eng, t.lw)
        for t in writes:
            self._need(eng, t.lw)
            for r in t.rd:
                self._need(eng, r)
        if inc:
            self.cnt[eng] += inc
            sem = self.sem[eng]
            th = lambda e, f=fn, s=sem, v=inc: f(e).then_inc(s, v)
            import traceback
            frs = traceback.extract_stack(limit=5)[:-1]
            th._src = ",".join("%d" % f.lineno for f in frs)
            self.prog[eng].append(th)
            pos = (eng, self.cnt[eng])
        else:
            self.prog[eng].append(lambda e, f=fn: f(e))
            pos = (eng, self.cnt[eng] + 1)
        for t in reads:
            t.rd.append(pos)
        for t in writes:
            t.lw = pos
            t.rd = []
        return pos

    def dma(self, eng, out, in_, reads=(), writes=(), **kw):
        return self._dma_issue(
            eng, lambda e: e.dma_start(out=out, in_=in_, **kw),
            list(reads), list(writes))

    def dmat(self, out, in_, reads=(), writes=()):
        return self._dma_issue(
            "sp", lambda e: e.dma_start_transpose(out, in_),
            list(reads), list(writes))

    def mm(self, ps_ap, lhsT_ap, rhs_ap, reads=(), writes=(), start=True,
           stop=True, inc=None, skip=False):
        if inc is None:
            inc = 1
        return self.op(
            "pe",
            lambda e: e.matmul(ps_ap, lhsT_ap, rhs_ap, start=start, stop=stop,
                               skip_group_check=skip),
            reads=reads, writes=writes, inc=inc)

    def tr(self, ps_ap, in_ap, ident_ap, reads=(), writes=()):
        return self.op(
            "pe", lambda e: e.transpose(ps_ap, in_ap, ident_ap),
            reads=reads, writes=writes, inc=1)

    def barrier(self):
        snap = dict(self.cnt)
        for e1 in ENGINES:
            for c in self.chans:
                if c != e1:
                    self._need(e1, (c, snap[c]))

    def dump(self, name, src_ap, shape, reads, dt=F32):
        if name not in self.dump_names:
            return
        d = self.dram("dbg_" + name, list(shape), dt, kind="ExternalOutput")
        self.dbg[name] = d
        self.dma("pool", d.ap[:, :], src_ap, reads=reads, writes=[d])

    def finalize(self):
        for e1 in ENGINES:
            for c in self.chans:
                assert self.waited[e1][c] <= self.cnt[c], (
                    "dangling wait on %s@%d > %d (inc=0 op at program end?)"
                    % (c, self.waited[e1][c], self.cnt[c]))
        block = self.ctx.enter_context(self.nc.Block())
        section = dict(pe=block.tensor, dve=block.vector, act=block.scalar,
                       pool=block.gpsimd, sp=block.sync)
        for eng in ENGINES:
            thunks = self.prog[eng]

            def body(e, thunks=thunks):
                for t in thunks:
                    try:
                        t(e)
                    except Exception as ex:
                        loc = getattr(t, "_src", None)
                        raise RuntimeError(
                            "emission failed at %s: %s" % (loc, ex)) from ex
            section[eng](body)



def _rel_bucket_np(dist):
    num_buckets, max_distance = 32, 128
    rp = np.asarray(dist, dtype=np.int64)
    max_exact = num_buckets // 2
    is_small = rp < max_exact
    rp_f = rp.astype(np.float64)
    with np.errstate(divide="ignore"):
        rp_large = max_exact + (
            np.log(rp_f / max_exact + 1e-10)
            / math.log(max_distance / max_exact)
            * (num_buckets - max_exact)).astype(np.int32)
    rp_large = np.minimum(rp_large, num_buckets - 1)
    return np.where(is_small, rp, rp_large)


def host_constants():
    c = {}
    c["c_ident"] = np.eye(128, dtype=ml_dtypes.bfloat16)
    c["c_identf"] = np.eye(128, dtype=np.float32)

    masks = np.zeros((32, 128, 128), dtype=np.float32)
    base = np.full((128, 128), NEG, dtype=np.float32)
    base[0:64, 0:64] = 0.0
    base[64:128, 64:128] = 0.0
    for i in range(32):
        m = base.copy()
        for g in GPOS:
            if i * 128 <= g < (i + 1) * 128:
                m[:, g - i * 128] = NEG
        masks[i] = m
    c["c_l0_mask"] = masks

    cm = np.zeros((4, 128, 512), dtype=np.float32)
    for r in range(4):
        q = r * 128 + np.arange(128)[:, None]
        kcol = np.arange(512)[None, :]
        cm[r] = np.where(kcol <= q, 0.0, NEG)
    c["c_causal"] = cm

    hs = 128
    pos = np.arange(0, hs, 2, dtype=np.float64)
    theta = 10000.0 ** (-2.0 * pos / hs)
    ang = np.arange(T, dtype=np.float64)[:, None] * theta[None, :]
    cos = np.cos(ang).T
    sin = np.sin(ang).T
    cfull = np.empty((hs, T), dtype=np.float32)
    sfull = np.empty((hs, T), dtype=np.float32)
    cfull[0::2] = cos
    cfull[1::2] = cos
    sfull[0::2] = -sin
    sfull[1::2] = sin
    c["c_rope_cos"] = cfull
    c["c_rope_sin"] = sfull

    P = np.zeros((128, 128), dtype=np.float32)
    for i in range(64):
        P[2 * i + 1, 2 * i] = 1
        P[2 * i, 2 * i + 1] = 1
    c["c_rope_swap"] = P

    M = np.zeros((384, 32), dtype=np.float32)
    for i in range(384):
        dist = i - 128
        if dist >= 0:
            b = int(_rel_bucket_np(dist))
            M[i, b] += 1.0
            M[i, 31] -= 1.0
    c["c_relmap"] = np.ascontiguousarray(M[::-1, :].T)
    R = np.zeros((128, 128), dtype=np.float32)
    for i in range(128):
        R[127 - i, i] = 1.0
    c["c_prev"] = R
    return c


CONST_SHAPES = {
    "c_ident": ([128, 128], BF16),
    "c_identf": ([128, 128], F32),
    "c_l0_mask": ([32, 128, 128], F32),
    "c_causal": ([4, 128, 512], F32),
    "c_rope_cos": ([128, T], F32),
    "c_rope_sin": ([128, T], F32),
    "c_rope_swap": ([128, 128], F32),
    "c_relmap": ([32, 384], F32),
    "c_prev": ([128, 128], F32),
}



class Common:
    def __init__(self, k, fin):
        self.ps0 = k.psum("ps0", [128, 512], F32)
        self.ps1 = k.psum("ps1", [128, 512], F32)
        self.ps2 = k.psum("ps2", [128, 512], F32)
        self.psS = k.psum("psS", [128, 512], F32)
        self.psT = k.psum("psT", [128, 1024], BF16)
        self.psR1 = k.psum("psR1", [1, 512], F32)
        self.psR2 = k.psum("psR2", [1, 512], F32)
        self.ident = k.sb("ident", [128, 128], BF16)
        self.identf = k.sb("identf", [128, 128], F32)
        self.ones1 = k.sb("ones1", [1, 512], BF16)
        self.ones1f = k.sb("ones1f", [1, 512], F32)
        k.dma("sp", self.ident[:, :], fin["c_ident"][:, :],
              writes=[self.ident])
        k.dma("sp", self.identf[:, :], fin["c_identf"][:, :],
              writes=[self.identf])
        k.op("dve", lambda e: e.memset(self.ones1[:, :], 1.0),
             writes=[self.ones1])
        k.op("dve", lambda e: e.memset(self.ones1f[:, :], 1.0),
             writes=[self.ones1f])


def load_row_bf16(k, row_tile, src_1d_f32, n):
    k.dma("pool", row_tile[0:1, 0:n], src_1d_f32[0:n].unsqueeze(0),
          writes=[row_tile])


def emit_cast(k, dst_tile, src_ap, max_bytes=8 << 20):
    shape = src_ap.shape
    rows = int(shape[0])
    row_bytes = int(np.prod(shape[1:])) * 4 if len(shape) > 1 else 4
    step = max(1, max_bytes // max(row_bytes, 1))
    r = 0
    while r < rows:
        r1 = min(rows, r + step)
        k.dma("pool", dst_tile.ap[r:r1], src_ap[r:r1], writes=[dst_tile])
        r = r1


def emit_ln_ff(k, cm, xnT, xin_dram=None, x_sb=None, gamma_col=None,
               beta_col=None, nt=T, routers=(), in_bf16=False,
               dst0=None, on_chunk=None):
    if dst0 is None:
        dst0 = lambda c: c * nt
    """Feature-major layernorm over d: input either DRAM [1024, nt] f32 or
    SBUF tile [128, NCH*nt] f32; output xnT [128, NCH*nt] bf16."""
    ntt = nt // 512
    with k.scope() as st:
        xbf = k.sb("ln_xbf", [128, 512], BF16, ctx=st)
        sq = k.sb("ln_sq", [128, 512], BF16, ctx=st)
        srow = k.sb("ln_srow", [1, nt], F32, ctx=st)
        s2row = k.sb("ln_s2row", [1, nt], F32, ctx=st)
        mean_r, rstd_r = srow, s2row
        mb = k.sb("ln_mb", [128, nt], F32, ctx=st)
        rb = k.sb("ln_rb", [128, nt], F32, ctx=st)
        xc = k.sb("ln_xc", [128, nt], F32, ctx=st) if x_sb is not None \
            else mb
        onesc = k.sb("ln_ones", [128, 1], BF16, ctx=st)
        k.op("dve", lambda e: e.memset(onesc[:, :], 1.0), writes=[onesc])
        epsc = k.sb("ln_eps", [1, 1], F32, ctx=st)
        k.op("dve", lambda e: e.memset(epsc[:, :], float(LN_EPS)),
             writes=[epsc])

        for tt in range(ntt):
            for c in range(NCH):
                if x_sb is None:
                    k.dma("pool" if not in_bf16 else "sp", xbf[:, :],
                          xin_dram[c * 128:(c + 1) * 128,
                                   tt * 512:(tt + 1) * 512],
                          writes=[xbf])
                else:
                    k.op("dve", lambda e, c=c, tt=tt: e.tensor_copy(
                        xbf[:, :],
                        x_sb[:, c * nt + tt * 512: c * nt + (tt + 1) * 512]),
                        reads=[x_sb], writes=[xbf])
                k.op("act", lambda e: e.activation(sq[:, :], xbf[:, :],
                                                   AF.Square),
                     reads=[xbf], writes=[sq])
                k.mm(cm.psR1[0:1, :], onesc[:, 0:1], xbf[:, :],
                     reads=[onesc, xbf], writes=[cm.psR1],
                     start=(c == 0), stop=(c == NCH - 1))
                k.mm(cm.psR2[0:1, :], onesc[:, 0:1], sq[:, :],
                     reads=[onesc, sq], writes=[cm.psR2],
                     start=(c == 0), stop=(c == NCH - 1))
            k.op("dve", lambda e, tt=tt: e.tensor_copy(
                srow[0:1, tt * 512:(tt + 1) * 512], cm.psR1[0:1, :]),
                reads=[cm.psR1], writes=[srow])
            k.op("dve", lambda e, tt=tt: e.tensor_copy(
                s2row[0:1, tt * 512:(tt + 1) * 512], cm.psR2[0:1, :]),
                reads=[cm.psR2], writes=[s2row])

        k.op("dve", lambda e: e.tensor_scalar_mul(
            srow[0:1, :], srow[0:1, :], 1.0 / D),
            reads=[srow], writes=[srow])
        k.op("dve", lambda e: e.tensor_scalar_mul(
            s2row[0:1, :], s2row[0:1, :], 1.0 / D),
            reads=[s2row], writes=[s2row])
        k.op("dve", lambda e: e.tensor_mul(
            xc[0:1, :], srow[0:1, :], srow[0:1, :]),
            reads=[srow], writes=[xc])
        k.op("dve", lambda e: e.tensor_sub(
            s2row[0:1, :], s2row[0:1, :], xc[0:1, :]),
            reads=[xc, s2row], writes=[s2row])
        k.op("act", lambda e: e.activation(
            s2row[0:1, :], s2row[0:1, :], AF.Sqrt, bias=epsc[0:1, 0:1]),
            reads=[s2row, epsc], writes=[s2row])
        k.op("dve", lambda e: e.reciprocal(s2row[0:1, :], s2row[0:1, :]),
             reads=[s2row], writes=[s2row])

        for tt in range(ntt):
            sl = slice(tt * 512, (tt + 1) * 512)
            k.mm(cm.ps0[:, 0:512], cm.ones1f[0:1, 0:128], mean_r[0:1, sl],
                 reads=[cm.ones1f, mean_r], writes=[cm.ps0])
            k.op("dve", lambda e, sl=sl: e.tensor_copy(
                mb[:, sl], cm.ps0[:, 0:512]), reads=[cm.ps0], writes=[mb])
            k.mm(cm.ps1[:, 0:512], cm.ones1f[0:1, 0:128], rstd_r[0:1, sl],
                 reads=[cm.ones1f, rstd_r], writes=[cm.ps1])
            k.op("dve", lambda e, sl=sl: e.tensor_copy(
                rb[:, sl], cm.ps1[:, 0:512]), reads=[cm.ps1], writes=[rb])

        if x_sb is None:
            xnf = k.sb("ln_xnf", [128, 512], F32, ctx=st)
            for tt in range(ntt):
                tsl = slice(tt * 512, (tt + 1) * 512)
                for c in range(NCH):
                    k.dma("pool" if in_bf16 else "sp", xnf[:, :],
                          xin_dram[c * 128:(c + 1) * 128, tsl], writes=[xnf])
                    k.op("dve", lambda e, tsl=tsl: e.tensor_sub(
                        xnf[:, :], xnf[:, :], mb[:, tsl]),
                        reads=[xnf, mb], writes=[xnf])
                    k.op("dve", lambda e, tsl=tsl: e.tensor_mul(
                        xnf[:, :], xnf[:, :], rb[:, tsl]),
                        reads=[xnf, rb], writes=[xnf])
                    if gamma_col is None:
                        k.op("dve", lambda e, c=c, tt=tt: e.tensor_copy(
                            xnT[:, dst0(c) + tt * 512:
                                dst0(c) + (tt + 1) * 512],
                            xnf[:, :]),
                            reads=[xnf], writes=[xnT])
                    else:
                        k.op("dve", lambda e, c=c, tt=tt: e.tensor_scalar(
                            xnT[:, dst0(c) + tt * 512:
                                dst0(c) + (tt + 1) * 512],
                            xnf[:, :], gamma_col[:, c:c + 1],
                            beta_col[:, c:c + 1], ALU.mult, ALU.add),
                            reads=[xnf, gamma_col, beta_col], writes=[xnT])
                    for ri, r in enumerate(routers):
                        ne = r["ne"]
                        for sub in range(4):
                            k.mm(cm.psS[:, ri * 32 + sub * ne:
                                        ri * 32 + (sub + 1) * ne],
                                 xnf[:, sub * 128:(sub + 1) * 128],
                                 r["rw"].ap[:, c * ne:(c + 1) * ne],
                                 reads=[xnf, r["rw"]], writes=[cm.psS],
                                 start=(c == 0 and ri == 0 and sub == 0),
                                 stop=False, skip=True)
                for ri, r in enumerate(routers):
                    ne = r["ne"]
                    for sub in range(4):
                        psl = slice(ri * 32 + sub * ne,
                                    ri * 32 + (sub + 1) * ne)
                        if r.get("rb") is not None:
                            k.mm(cm.psS[:, psl], cm.ones1f[0:1, 0:128],
                                 r["rb"].ap[0:1, 0:ne],
                                 reads=[cm.ones1f, r["rb"]],
                                 writes=[cm.psS],
                                 start=False, stop=True, skip=True)
                        tsg = tt * 4 + sub
                        k.op("dve", lambda e, psl=psl, tsg=tsg, r=r, ne=ne:
                             e.tensor_copy(
                                 r["out"].ap[:, tsg * ne:(tsg + 1) * ne],
                                 cm.psS[:, psl]),
                             reads=[cm.psS], writes=[r["out"]])
        else:
            for c in range(NCH):
                k.op("dve", lambda e, c=c: e.tensor_sub(
                    xc[:, :], x_sb[:, c * nt:(c + 1) * nt], mb[:, :]),
                    reads=[x_sb, mb], writes=[xc])
                if gamma_col is None:
                    k.op("dve", lambda e, c=c: e.tensor_mul(
                        xnT[:, dst0(c):dst0(c) + nt], xc[:, :], rb[:, :]),
                        reads=[xc, rb], writes=[xnT])
                else:
                    k.op("dve", lambda e: e.tensor_mul(xc[:, :], xc[:, :],
                                                       rb[:, :]),
                         reads=[xc, rb], writes=[xc])
                    k.op("dve", lambda e, c=c: e.tensor_scalar(
                        xnT[:, dst0(c):dst0(c) + nt], xc[:, :],
                        gamma_col[:, c:c + 1], beta_col[:, c:c + 1],
                        ALU.mult, ALU.add),
                        reads=[xc, gamma_col, beta_col], writes=[xnT])
                if on_chunk is not None:
                    on_chunk(c)




def load_cols(k, out_cols, src_1d, nchunk):
    for c in range(nchunk):
        k.dma("pool", out_cols.ap[:, c:c + 1],
              src_1d[c * 128:(c + 1) * 128].unsqueeze(1), writes=[out_cols])


def build_transposed_dram(k, st, dst_dram, src_bf, R, C):
    tmp = k.sb("btd_tmp_%s" % dst_dram.name, [128, R], BF16, ctx=st)
    for cc in range(C // 128):
        k.dmat(tmp[:, :], src_bf.ap[0:R, cc * 128:(cc + 1) * 128],
               reads=[src_bf], writes=[tmp])
        k.dma("pool", dst_dram.ap[cc * 128:(cc + 1) * 128, 0:R], tmp[:, :],
              reads=[tmp], writes=[dst_dram])



def emit_l0_attn(k, cm, xnT, s_dram, sG, wbf, fin, v_scratch, sg_dram,
                 png_dram):
    with k.scope() as st:
        wq = k.sb("l0_wq", [128, NCH * D], BF16, ctx=st)
        wk = k.sb("l0_wk", [128, NCH * D], BF16, ctx=st)
        wv = k.sb("l0_wv", [128, NCH * D], BF16, ctx=st)
        wo = k.sb("l0_wo", [128, NCH * D], BF16, ctx=st)
        for (w, nm) in ((wq, "L0_attn_Wq"), (wk, "L0_attn_Wk"),
                        (wv, "L0_attn_Wv"), (wo, "L0_attn_Wo")):
            k.dma("sp", w[:, :].rearrange("p (c m) -> p c m", c=NCH),
                  wbf[nm].ap.rearrange("(c p) m -> p c m", p=128),
                  reads=[wbf[nm]], writes=[w])
        bq = k.sb("l0_bq", [1, D], BF16, ctx=st)
        bk = k.sb("l0_bk", [1, D], BF16, ctx=st)
        bv = k.sb("l0_bv", [1, D], BF16, ctx=st)
        bo = k.sb("l0_bo", [1, D], BF16, ctx=st)
        load_row_bf16(k, bq, fin["L0_attn_bq"], D)
        load_row_bf16(k, bk, fin["L0_attn_bk"], D)
        load_row_bf16(k, bv, fin["L0_attn_bv"], D)
        load_row_bf16(k, bo, fin["L0_attn_bo"], D)
        onesw = k.sb("l0_onesw", [1, 512], BF16, ctx=st)
        k.op("dve", lambda e: e.memset(onesw[0:1, :], 1.0), writes=[onesw])

        def proj_ff(dst, dst_sl, w, b, src, src_cols, n, scale=None,
                    mc_range=range(NCH)):
            for mc in mc_range:
                for c in range(NCH):
                    k.mm(cm.ps0[:, 0:n],
                         w[:, c * D + mc * 128: c * D + (mc + 1) * 128],
                         src.ap[:, src_cols(c)],
                         reads=[w, src], writes=[cm.ps0],
                         start=(c == 0), stop=False)
                k.mm(cm.ps0[:, 0:n], b[0:1, mc * 128:(mc + 1) * 128],
                     onesw[0:1, 0:n],
                     reads=[b, onesw], writes=[cm.ps0],
                     start=False, stop=True)
                k.op("act", lambda e, mc=mc: e.activation(
                    dst.ap[:, dst_sl(mc)], cm.ps0[:, 0:n], AF.Copy,
                    scale=(scale or 1.0)),
                    reads=[cm.ps0], writes=[dst])

        xnG = k.sb("l0_xnG", [128, NCH * 8], BF16, ctx=st)
        qG = k.sb("l0_qG", [128, NCH * 8], BF16, ctx=st)
        kG = k.sb("l0_kG", [128, NCH * 8], BF16, ctx=st)
        vGf = k.sb("l0_vGf", [128, NCH * 8], BF16, ctx=st)
        vG = k.sb("l0_vG", [8, D], BF16, ctx=st)
        for c in range(NCH):
            k.op("dve", lambda e, c=c: e.tensor_copy(
                xnG[:, c * 8:(c + 1) * 8],
                xnT.ap[:, c * T:(c + 1) * T][:, GPOS[0]::GSTEP]),
                reads=[xnT], writes=[xnG])
        for (dst, w, b, scale) in ((qG, wq, bq, 0.125), (kG, wk, bk, None),
                                   (vGf, wv, bv, None)):
            proj_ff(dst, lambda mc: slice(mc * 8, (mc + 1) * 8), w, b,
                    xnG, lambda c: slice(c * 8, (c + 1) * 8), 8, scale)
        for c in range(NCH):
            k.tr(cm.psT[0:8, 0:128], vGf[:, c * 8:(c + 1) * 8],
                 cm.ident[:, :], reads=[vGf, cm.ident], writes=[cm.psT])
            k.op("dve", lambda e, c=c: e.tensor_copy(
                vG[0:8, c * 128:(c + 1) * 128], cm.psT[0:8, 0:128]),
                reads=[cm.psT], writes=[vG])

        msk = k.sb("l0_msk", [128, 128], F32, ctx=st)
        sgst = k.sb("l0_sgst", [8, 512], F32, ctx=st)
        sstage = k.sb("l0_sst", [128, 512], F32, ctx=st)
        st2 = contextlib.ExitStack()
        qT = k.sb("l0_qT", [128, NCH * 512], BF16, ctx=st2)
        kT = k.sb("l0_kT", [128, NCH * 512], BF16, ctx=st2)
        vg = k.sb("l0_vg", [128, 4 * D], BF16, ctx=st2)
        aoT = k.sb("l0_aoT", [128, NCH * 512], BF16, ctx=st2)
        Ssb = k.sb("l0_S", [128, 136], F32, ctx=st2)
        Psb = k.sb("l0_P", [128, 136], BF16, ctx=st2)
        PT = k.sb("l0_PT", [128, 128], BF16, ctx=st2)
        PTg = k.sb("l0_PTg", [8, 128], BF16, ctx=st2)
        nmax = k.sb("l0_nmax", [128, 1], F32, ctx=st2)
        zrow = k.sb("l0_z", [128, 1], F32, ctx=st2)
        rz = k.sb("l0_rz", [128, 1], F32, ctx=st2)

        for g in range(NTT):
            gsl = slice(g * 512, (g + 1) * 512)
            proj_ff(qT, lambda mc: slice(mc * 512, (mc + 1) * 512), wq, bq,
                    xnT, lambda c, g=g: slice(c * T + g * 512,
                                              c * T + (g + 1) * 512),
                    512, 0.125)
            proj_ff(kT, lambda mc: slice(mc * 512, (mc + 1) * 512), wk, bk,
                    xnT, lambda c, g=g: slice(c * T + g * 512,
                                              c * T + (g + 1) * 512),
                    512, None)
            for ts in range(4):
                t0 = g * 512 + ts * 128
                for mt in range(2):
                    for c in range(NCH):
                        k.mm(cm.ps1[:, :],
                             xnT[:, c * T + t0: c * T + t0 + 128],
                             wv[:, c * D + mt * 512: c * D + (mt + 1) * 512],
                             reads=[xnT, wv], writes=[cm.ps1],
                             start=(c == 0), stop=False)
                    k.mm(cm.ps1[:, :], cm.ones1[0:1, 0:128],
                         bv[0:1, mt * 512:(mt + 1) * 512],
                         reads=[cm.ones1, bv], writes=[cm.ps1],
                         start=False, stop=True)
                    k.op("dve", lambda e, ts=ts, mt=mt: e.tensor_copy(
                        vg[:, ts * D + mt * 512: ts * D + (mt + 1) * 512],
                        cm.ps1[:, :]),
                        reads=[cm.ps1], writes=[vg])
            k.dma("pool",
                  v_scratch.ap[g * 512:(g + 1) * 512, :].rearrange(
                      "(ts p) m -> p ts m", p=128),
                  vg[:, :].rearrange("p (ts m) -> p ts m", ts=4),
                  reads=[vg], writes=[v_scratch])

            for h in range(16):
                c0, r0 = h // 2, (h % 2) * 64
                k.mm(cm.ps2[0:8, :], qG[r0:r0 + 64, c0 * 8:(c0 + 1) * 8],
                     kT[r0:r0 + 64, c0 * 512:(c0 + 1) * 512],
                     reads=[qG, kT], writes=[cm.ps2])
                k.op("dve", lambda e: e.tensor_copy(
                    sgst[0:8, :], cm.ps2[0:8, :]),
                    reads=[cm.ps2], writes=[sgst])
                k.dma("pool", sg_dram.ap[h, :, g * 512:(g + 1) * 512],
                      sgst[0:8, :], reads=[sgst], writes=[sg_dram])

            for qt in range(4):
                qtile = g * 4 + qt
                k.dma("sp", msk[:, :], fin["c_l0_mask"][qtile, :, :],
                      writes=[msk])
                for h in range(16):
                    c0, r0 = h // 2, (h % 2) * 64
                    hs = slice(c0 * 512 + qt * 128, c0 * 512 + (qt + 1) * 128)
                    k.mm(cm.psS[:, 0:128], qT[r0:r0 + 64, hs],
                         kT[r0:r0 + 64, hs],
                         reads=[qT, kT], writes=[cm.psS])
                    k.mm(cm.psS[:, 128:136], qT[r0:r0 + 64, hs],
                         kG[r0:r0 + 64, c0 * 8:(c0 + 1) * 8],
                         reads=[qT, kG], writes=[cm.psS])
                    k.op("dve", lambda e: e.tensor_add(
                        Ssb[:, 0:128], cm.psS[:, 0:128], msk[:, :]),
                        reads=[cm.psS, msk], writes=[Ssb])
                    k.op("dve", lambda e: e.tensor_copy(
                        Ssb[:, 128:136], cm.psS[:, 128:136]),
                        reads=[cm.psS], writes=[Ssb])
                    k.op("dve", lambda e: e.tensor_reduce(
                        nmax[:, :], Ssb[:, :], AX.X, ALU.max, negate=True),
                        reads=[Ssb], writes=[nmax])
                    k.op("act", lambda e: e.activation(
                        Psb[:, :], Ssb[:, :], AF.Exp, bias=nmax[:, 0:1],
                        accum_out=zrow[:, :]),
                        reads=[Ssb, nmax], writes=[Psb, zrow])
                    k.op("dve", lambda e: e.reciprocal(rz[:, :], zrow[:, :]),
                         reads=[zrow], writes=[rz])
                    k.op("pool", lambda e: e.tensor_scalar_mul(
                        Psb[:, :], Psb[:, :], rz[:, 0:1]),
                        reads=[Psb, rz], writes=[Psb])
                    k.tr(cm.psT[:, 0:128], Psb[:, 0:128], cm.ident[:, :],
                         reads=[Psb, cm.ident], writes=[cm.psT])
                    k.tr(cm.psT[0:8, 128:256], Psb[:, 128:136],
                         cm.ident[:, :],
                         reads=[Psb, cm.ident], writes=[cm.psT])
                    k.op("dve", lambda e: e.tensor_copy(
                        PT[:, :], cm.psT[:, 0:128]),
                        reads=[cm.psT], writes=[PT])
                    k.op("dve", lambda e: e.tensor_copy(
                        PTg[0:8, :], cm.psT[0:8, 128:256]),
                        reads=[cm.psT], writes=[PTg])
                    k.mm(cm.ps2[r0:r0 + 64, 0:128],
                         vg[:, qt * D + h * 64: qt * D + (h + 1) * 64],
                         PT[:, :], reads=[vg, PT], writes=[cm.ps2],
                         start=True, stop=False)
                    k.mm(cm.ps2[r0:r0 + 64, 0:128],
                         vG[0:8, h * 64:(h + 1) * 64], PTg[0:8, :],
                         reads=[vG, PTg], writes=[cm.ps2],
                         start=False, stop=True)
                    k.op("dve", lambda e, c0=c0, r0=r0, qt=qt: e.tensor_copy(
                        aoT[r0:r0 + 64,
                            c0 * 512 + qt * 128: c0 * 512 + (qt + 1) * 128],
                        cm.ps2[r0:r0 + 64, 0:128]),
                        reads=[cm.ps2], writes=[aoT])

            for mc in range(NCH):
                for c in range(NCH):
                    k.mm(cm.ps0[:, :],
                         wo[:, c * D + mc * 128: c * D + (mc + 1) * 128],
                         aoT[:, c * 512:(c + 1) * 512],
                         reads=[wo, aoT], writes=[cm.ps0],
                         start=(c == 0), stop=False)
                k.mm(cm.ps0[:, :], bo[0:1, mc * 128:(mc + 1) * 128],
                     onesw[0:1, 0:512],
                     reads=[bo, onesw], writes=[cm.ps0],
                     start=False, stop=True)
                k.op("dve", lambda e, mc=mc: e.tensor_copy(
                    sstage[:, :], cm.ps0[:, :]),
                    reads=[cm.ps0], writes=[sstage])
                k.dma("pool", s_dram.ap[mc * 128:(mc + 1) * 128, gsl],
                      sstage[:, :], reads=[sstage], writes=[s_dram])

        k.barrier()
        st2.close()
        SgP = k.sb("l0_SgP", [128, T], F32, ctx=st)
        PnG = k.sb("l0_PnG", [128, T], BF16, ctx=st)
        gng = k.sb("l0_gng", [128, 1], F32, ctx=st)
        gz = k.sb("l0_gz", [128, 1], F32, ctx=st)
        grz = k.sb("l0_grz", [128, 1], F32, ctx=st)
        vtile = k.sb("l0_vt", [128, D], BF16, ctx=st)
        PGT = k.sb("l0_PGT", [128, 128], BF16, ctx=st)
        aoG = k.sb("l0_aoG", [128, 64], BF16, ctx=st)
        psAVG = k.psum("l0_psavg", [128, 512], F32, ctx=st)
        k.dma("sp", SgP[:, :],
              sg_dram.ap.rearrange("h g t -> (h g) t"),
              reads=[sg_dram], writes=[SgP])
        k.op("dve", lambda e: e.tensor_reduce(
            gng[:, :], SgP[:, :], AX.X, ALU.max, negate=True),
            reads=[SgP], writes=[gng])
        k.op("act", lambda e: e.activation(
            PnG[:, :], SgP[:, :], AF.Exp, bias=gng[:, 0:1],
            accum_out=gz[:, :]),
            reads=[SgP, gng], writes=[PnG, gz])
        k.op("dve", lambda e: e.reciprocal(grz[:, :], gz[:, :]),
             reads=[gz], writes=[grz])
        k.op("pool", lambda e: e.tensor_scalar_mul(
            PnG[:, :], PnG[:, :], grz[:, 0:1]),
            reads=[PnG, grz], writes=[PnG])
        k.dma("pool", png_dram.ap[:, :], PnG[:, :],
              reads=[PnG], writes=[png_dram])
        for kt in range(NQT):
            k.dma("sp", vtile[:, :], v_scratch.ap[kt * 128:(kt + 1) * 128, :],
                  reads=[v_scratch], writes=[vtile])
            k.dmat(PGT[:, :], png_dram.ap[:, kt * 128:(kt + 1) * 128],
                   reads=[png_dram], writes=[PGT])
            for h in range(16):
                c0, r0 = h // 2, (h % 2) * 64
                k.mm(psAVG[r0:r0 + 64, c0 * 8:(c0 + 1) * 8],
                     vtile[:, h * 64:(h + 1) * 64],
                     PGT[:, h * 8:(h + 1) * 8],
                     reads=[vtile, PGT], writes=[psAVG],
                     start=(kt == 0 and h == 0), stop=(kt == NQT - 1),
                     skip=True, inc=1)
        k.op("dve", lambda e: e.tensor_copy(aoG[:, :], psAVG[:, 0:64]),
             reads=[psAVG], writes=[aoG])
        for mc in range(NCH):
            for c in range(NCH):
                k.mm(cm.psS[:, 0:8],
                     wo[:, c * D + mc * 128: c * D + (mc + 1) * 128],
                     aoG[:, c * 8:(c + 1) * 8],
                     reads=[wo, aoG], writes=[cm.psS],
                     start=(c == 0), stop=False)
            k.mm(cm.psS[:, 0:8], bo[0:1, mc * 128:(mc + 1) * 128],
                 onesw[0:1, 0:8],
                 reads=[bo, onesw], writes=[cm.psS],
                 start=False, stop=True)
            k.op("dve", lambda e, mc=mc: e.tensor_copy(
                sG[:, mc * 8:(mc + 1) * 8], cm.psS[:, 0:8]),
                reads=[cm.psS], writes=[sG])



def emit_l0_ffn(k, cm, xnT, g_dram, wbf, fin, gates):
    H = 2048
    NHC = H // 128
    with k.scope() as st:
        b1r = k.sb("l0_b1r", [1, 10 * H], BF16, ctx=st)
        b2r = k.sb("l0_b2r", [1, 10 * D], BF16, ctx=st)
        k.dma("pool", b1r[0:1, 0:8 * H],
              fin["L0_ffn_b1"].rearrange("e h -> (e h)").unsqueeze(0),
              writes=[b1r])
        k.dma("pool", b2r[0:1, 0:8 * D],
              fin["L0_ffn_b2"].rearrange("e d -> (e d)").unsqueeze(0),
              writes=[b2r])
        k.dma("pool", b1r[0:1, 8 * H:10 * H],
              fin["L0_ffn_sb1"].rearrange("e h -> (e h)").unsqueeze(0),
              writes=[b1r])
        k.dma("pool", b2r[0:1, 8 * D:10 * D],
              fin["L0_ffn_sb2"].rearrange("e d -> (e d)").unsqueeze(0),
              writes=[b2r])

        comb = k.sb("l0_comb", [128, NQT * 10], F32, ctx=st)
        gts = k.sb("l0_gts", [128, 8], F32, ctx=st)
        m1 = k.sb("l0_m1", [128, 1], F32, ctx=st)
        m2 = k.sb("l0_m2", [128, 1], F32, ctx=st)
        mk1 = k.sb("l0_mk1", [128, 8], F32, ctx=st)
        mk2 = k.sb("l0_mk2", [128, 8], F32, ctx=st)
        gg2 = k.sb("l0_gg2", [128, 8], F32, ctx=st)
        p1 = k.sb("l0_p1", [128, 1], F32, ctx=st)
        p1m = k.sb("l0_p1m", [128, 1], F32, ctx=st)
        dd = k.sb("l0_dd", [128, 1], F32, ctx=st)
        k.op("dve", lambda e: e.memset(
            comb[:, :].rearrange("p (t e) -> p t e", e=10)[:, :, 8:10], 0.5),
            writes=[comb])
        for tt in range(NQT):
            k.op("dve", lambda e, tt=tt: e.tensor_copy(
                gts[:, :], gates.ap[:, tt * 8:(tt + 1) * 8]),
                reads=[gates], writes=[gts])
            k.op("dve", lambda e: e.tensor_reduce(
                m1[:, :], gts[:, :], AX.X, ALU.max), reads=[gts], writes=[m1])
            k.op("dve", lambda e: e.tensor_scalar(
                mk1[:, :], gts[:, :], m1[:, 0:1], None, ALU.is_equal),
                reads=[gts, m1], writes=[mk1])
            k.op("dve", lambda e: e.scalar_tensor_tensor(
                gg2[:, :], mk1[:, :], NEG, gts[:, :], ALU.mult, ALU.add),
                reads=[mk1, gts], writes=[gg2])
            k.op("dve", lambda e: e.tensor_reduce(
                m2[:, :], gg2[:, :], AX.X, ALU.max), reads=[gg2], writes=[m2])
            k.op("dve", lambda e: e.tensor_scalar(
                mk2[:, :], gg2[:, :], m2[:, 0:1], None, ALU.is_equal),
                reads=[gg2, m2], writes=[mk2])
            k.op("dve", lambda e: e.tensor_sub(dd[:, :], m1[:, :], m2[:, :]),
                 reads=[m1, m2], writes=[dd])
            k.op("act", lambda e: e.activation(p1[:, :], dd[:, :],
                                               AF.Sigmoid),
                 reads=[dd], writes=[p1])
            k.op("dve", lambda e: e.tensor_scalar(
                p1m[:, :], p1[:, :], -1.0, 1.0, ALU.mult, ALU.add),
                reads=[p1], writes=[p1m])
            k.op("dve", lambda e: e.tensor_scalar(
                mk1[:, :], mk1[:, :], p1[:, 0:1], None, ALU.mult),
                reads=[mk1, p1], writes=[mk1])
            k.op("dve", lambda e, tt=tt: e.scalar_tensor_tensor(
                comb[:, tt * 10: tt * 10 + 8], mk2[:, :], p1m[:, 0:1],
                mk1[:, :], ALU.mult, ALU.add),
                reads=[mk1, mk2, p1m], writes=[comb])

        k.dump("l0_comb", comb.ap[:, :], [128, NQT * 10], [comb])
        h_sb = k.sb("l0_h", [128, NHC * 512], BF16, ctx=st)
        w1a = k.sb("l0_w1a", [128, NCH * 128], BF16, ctx=st)
        w1b = k.sb("l0_w1b", [128, NCH * 128], BF16, ctx=st)
        w2a = k.sb("l0_w2a", [128, NHC * 512], BF16, ctx=st)
        w2b = k.sb("l0_w2b", [128, NHC * 512], BF16, ctx=st)
        acc = k.sb("l0_acc", [128, 4 * D], F32, ctx=st)
        sil = k.sb("l0_sil", [128, 4 * D], BF16, ctx=st)
        gst = k.sb("l0_gst", [128, 512], BF16, ctx=st)

        def w1_ap(e, hc):
            src = (wbf["L0_ffn_w1"].ap[e] if e < 8
                   else wbf["L0_ffn_sw1"].ap[e - 8])
            return src[:, hc * 128:(hc + 1) * 128].rearrange(
                "(kc p) h -> p kc h", p=128)

        def w2_ap(e, mt):
            src = (wbf["L0_ffn_w2"].ap[e] if e < 8
                   else wbf["L0_ffn_sw2"].ap[e - 8])
            return src[:, mt * 512:(mt + 1) * 512].rearrange(
                "(hc p) m -> p hc m", p=128)

        for tt in range(NTT):
            tsl = slice(tt * 512, (tt + 1) * 512)
            for e in range(10):
                for hc in range(NHC):
                    wbuf = w1a if hc % 2 == 0 else w1b
                    k.dma("sp", wbuf[:, :].rearrange("p (kc h) -> p kc h",
                                                     kc=NCH),
                          w1_ap(e, hc),
                          reads=[wbf["L0_ffn_w1"] if e < 8
                                 else wbf["L0_ffn_sw1"]], writes=[wbuf])
                    for c in range(NCH):
                        k.mm(cm.ps0[:, :],
                             wbuf[:, c * 128:(c + 1) * 128],
                             xnT[:, c * T + tt * 512: c * T + (tt + 1) * 512],
                             reads=[wbuf, xnT], writes=[cm.ps0],
                             start=(c == 0), stop=False)
                    k.mm(cm.ps0[:, :],
                         b1r[0:1, e * H + hc * 128: e * H + (hc + 1) * 128],
                         cm.ones1[0:1, 0:512],
                         reads=[b1r, cm.ones1], writes=[cm.ps0],
                         start=False, stop=True)
                    k.op("act", lambda e2, hc=hc: e2.activation(
                        h_sb[:, hc * 512:(hc + 1) * 512], cm.ps0[:, :],
                        AF.Gelu),
                        reads=[cm.ps0], writes=[h_sb])
                if tt == 0 and e == 0:
                    k.dump("l0_h00", h_sb.ap[:, :], [128, NHC * 512],
                           [h_sb], dt=BF16)
                for mt in range(2):
                    wbuf = w2a if mt == 0 else w2b
                    k.dma("sp", wbuf[:, :].rearrange("p (hc m) -> p hc m",
                                                     hc=NHC),
                          w2_ap(e, mt),
                          reads=[wbf["L0_ffn_w2"] if e < 8
                                 else wbf["L0_ffn_sw2"]], writes=[wbuf])
                    for ts in range(4):
                        tsg = tt * 4 + ts
                        for hc in range(NHC):
                            k.mm(cm.ps1[:, :],
                                 h_sb[:, hc * 512 + ts * 128:
                                      hc * 512 + (ts + 1) * 128],
                                 wbuf[:, hc * 512:(hc + 1) * 512],
                                 reads=[h_sb, wbuf], writes=[cm.ps1],
                                 start=(hc == 0), stop=False)
                        k.mm(cm.ps1[:, :], cm.ones1[0:1, 0:128],
                             b2r[0:1, e * D + mt * 512: e * D + (mt + 1) * 512],
                             reads=[cm.ones1, b2r], writes=[cm.ps1],
                             start=False, stop=True)
                        out_sl = slice(ts * D + mt * 512,
                                       ts * D + (mt + 1) * 512)
                        if e == 0:
                            k.op("dve", lambda e2, out_sl=out_sl, tsg=tsg,
                                 e=e: e2.tensor_scalar(
                                     acc[:, out_sl], cm.ps1[:, :],
                                     comb[:, tsg * 10 + e: tsg * 10 + e + 1],
                                     None, ALU.mult),
                                 reads=[cm.ps1, comb], writes=[acc])
                        else:
                            k.op("dve", lambda e2, out_sl=out_sl, tsg=tsg,
                                 e=e: e2.scalar_tensor_tensor(
                                     acc[:, out_sl], cm.ps1[:, :],
                                     comb[:, tsg * 10 + e: tsg * 10 + e + 1],
                                     acc[:, out_sl], ALU.mult, ALU.add),
                                 reads=[cm.ps1, comb, acc], writes=[acc])
            if tt == 0:
                k.dump("l0_acc0", acc.ap[:, :], [128, 4 * D], [acc])
            k.op("act", lambda e: e.activation(sil[:, :], acc[:, :], AF.Silu),
                 reads=[acc], writes=[sil])
            for c in range(NCH):
                for ts in range(4):
                    k.tr(cm.psT[:, 0:128],
                         sil[:, ts * D + c * 128: ts * D + (c + 1) * 128],
                         cm.ident[:, :],
                         reads=[sil, cm.ident], writes=[cm.psT])
                    k.op("dve", lambda e, ts=ts: e.tensor_copy(
                        gst[:, ts * 128:(ts + 1) * 128], cm.psT[:, 0:128]),
                        reads=[cm.psT], writes=[gst])
                k.dma("pool",
                      g_dram.ap[c * 128:(c + 1) * 128,
                                tt * 512:(tt + 1) * 512],
                      gst[:, :], reads=[gst], writes=[g_dram])





TBLK = 256
NBLK = T // TBLK


def _mix_coeff_cols(k, st, fin, pre, tm_name, cm_name):
    tm = k.sb(pre + "_tm", [128, NCH], F32, ctx=st)
    cmx = k.sb(pre + "_cm", [128, NCH], F32, ctx=st)
    a1 = k.sb(pre + "_a1", [128, NCH], F32, ctx=st)
    a2 = k.sb(pre + "_a2", [128, NCH], F32, ctx=st)
    load_cols(k, tm, fin[tm_name].rearrange("a b c -> (a b c)"), NCH)
    load_cols(k, cmx, fin[cm_name].rearrange("a b c -> (a b c)"), NCH)
    k.op("dve", lambda e: e.tensor_scalar(a1[:, :], tm[:, :], -1.0, 1.0,
                                          ALU.mult, ALU.add),
         reads=[tm], writes=[a1])
    k.op("dve", lambda e: e.tensor_add(a1[:, 0:NCH // 2], a1[:, 0:NCH // 2],
                                       cmx[:, 0:NCH // 2]),
         reads=[a1, cmx], writes=[a1])
    k.op("dve", lambda e: e.memset(a2[:, 0:NCH // 2], 0.0), writes=[a2])
    k.op("dve", lambda e: e.tensor_copy(a2[:, NCH // 2:NCH],
                                        cmx[:, NCH // 2:NCH]),
         reads=[cmx], writes=[a2])
    return tm, a1, a2


def _emit_mix_block(k, xsrc_pad, xm_out, tm, a1, a2, ncols, src0, dst_sl,
                    chunks=None):
    for c in (chunks if chunks is not None else range(NCH)):
        s0 = src0(c)
        x_ap = xsrc_pad.ap[:, s0:s0 + ncols]
        xm1_ap = xsrc_pad.ap[:, s0 - 1:s0 - 1 + ncols]
        xp1_ap = xsrc_pad.ap[:, s0 + 1:s0 + 1 + ncols]
        d = dst_sl(c)
        k.op("dve", lambda e, x_ap=x_ap, d=d, c=c: e.tensor_scalar(
            xm_out.ap[:, d], x_ap, tm[:, c:c + 1], None, ALU.mult),
            reads=[xsrc_pad, tm], writes=[xm_out])
        k.op("dve", lambda e, xm1_ap=xm1_ap, d=d, c=c:
             e.scalar_tensor_tensor(
                 xm_out.ap[:, d], xm1_ap, a1[:, c:c + 1], xm_out.ap[:, d],
                 ALU.mult, ALU.add),
             reads=[xsrc_pad, a1, xm_out], writes=[xm_out])
        k.op("dve", lambda e, xp1_ap=xp1_ap, d=d, c=c:
             e.scalar_tensor_tensor(
                 xm_out.ap[:, d], xp1_ap, a2[:, c:c + 1], xm_out.ap[:, d],
                 ALU.mult, ALU.add),
             reads=[xsrc_pad, a2, xm_out], writes=[xm_out])


def emit_l1_attn(k, cm, xnT, s_dram, wbf, fin, scratch):
    x0_dram = scratch["x0_dram"]
    xa_dram = scratch["xa_dram"]
    xb_dram = scratch["xb_dram"]

    with k.scope() as st:
        g1 = k.sb("l1_g1", [128, NCH], F32, ctx=st)
        b1 = k.sb("l1_b1", [128, NCH], F32, ctx=st)
        g2 = k.sb("l1_g2", [128, NCH], F32, ctx=st)
        b2 = k.sb("l1_b2", [128, NCH], F32, ctx=st)
        load_cols(k, g1, fin["L1_attn_ln1_g"], NCH)
        load_cols(k, b1, fin["L1_attn_ln1_b"], NCH)
        load_cols(k, g2, fin["L1_attn_ln2_g"], NCH)
        load_cols(k, b2, fin["L1_attn_ln2_b"], NCH)
        tmA, a1A, a2A = _mix_coeff_cols(k, st, fin, "l1a",
                                        "L1_attn_att_time_mix",
                                        "L1_attn_att_combined_mix")
        tmF, a1F, a2F = _mix_coeff_cols(k, st, fin, "l1f",
                                        "L1_attn_ffn_time_mix",
                                        "L1_attn_ffn_combined_mix")
        edc = k.sb("l1_ed", [128, NCH], F32, ctx=st)
        euc = k.sb("l1_eu", [128, NCH], F32, ctx=st)
        load_cols(k, edc, fin["L1_attn_time_decay"], NCH)
        load_cols(k, euc, fin["L1_attn_time_first"], NCH)
        k.op("act", lambda e: e.activation(edc[:, :], edc[:, :], AF.Exp),
             reads=[edc], writes=[edc])
        k.op("act", lambda e: e.activation(edc[:, :], edc[:, :], AF.Exp,
                                           scale=-1.0),
             reads=[edc], writes=[edc])
        k.op("act", lambda e: e.activation(euc[:, :], euc[:, :], AF.Exp),
             reads=[euc], writes=[euc])

        st_ln = contextlib.ExitStack()
        x0buf = k.sb("l1_x0buf", [128, T], BF16, ctx=st_ln)

        def _x0_out(c):
            k.dma("pool", x0_dram.ap[c * 128:(c + 1) * 128, :],
                  x0buf[:, :], reads=[x0buf], writes=[x0_dram])
        emit_ln_ff(k, cm, x0buf, x_sb=xnT, gamma_col=g1, beta_col=b1,
                   dst0=lambda c: 0, on_chunk=_x0_out)
        k.barrier()
        st_ln.close()

        st_tm = contextlib.ExitStack()
        wkT = k.sb("l1_wkT", [128, NCH * D], BF16, ctx=st_tm)
        wvT = k.sb("l1_wvT", [128, NCH * D], BF16, ctx=st_tm)
        wrT = k.sb("l1_wrT", [128, NCH * D], BF16, ctx=st_tm)
        woT = k.sb("l1_woT", [128, NCH * D], BF16, ctx=st_tm)
        for w, nm in ((wkT, "L1_attn_Wk"), (wvT, "L1_attn_Wv"),
                      (wrT, "L1_attn_Wr"), (woT, "L1_attn_Wo")):
            for c in range(NCH):
                k.dmat(w[:, c * D:(c + 1) * D],
                       wbf[nm].ap[0:D, c * 128:(c + 1) * 128],
                       reads=[wbf[nm]], writes=[w])
        x0b = k.sb("l1_x0b", [128, NCH * (TBLK + 2)], BF16, ctx=st_tm)
        xmb = k.sb("l1_xmb", [128, NCH * TBLK], BF16, ctx=st_tm)
        kb = k.sb("l1_kb", [128, NCH * TBLK], F32, ctx=st_tm)
        kvb = k.sb("l1_kvb", [128, NCH * TBLK], F32, ctx=st_tm)
        Ab = k.sb("l1_Ab", [128, NCH * (TBLK + 1)], F32, ctx=st_tm)
        Bb = k.sb("l1_Bb", [128, NCH * (TBLK + 1)], F32, ctx=st_tm)
        sigr = k.sb("l1_sigr", [128, NCH * TBLK], BF16, ctx=st_tm)
        syb = k.sb("l1_syb", [128, NCH * TBLK], BF16, ctx=st_tm)
        edb = k.sb("l1_edb", [128, NCH * TBLK], F32, ctx=st_tm)
        carA = k.sb("l1_carA", [128, NCH], F32, ctx=st_tm)
        carB = k.sb("l1_carB", [128, NCH], F32, ctx=st_tm)
        xab = k.sb("l1_xab", [128, NCH * TBLK], BF16, ctx=st_tm)
        k.op("dve", lambda e: e.memset(carA[:, :], 0.0), writes=[carA])
        k.op("dve", lambda e: e.memset(carB[:, :], 0.0), writes=[carB])
        k.op("dve", lambda e: e.memset(edb[:, :], 0.0), writes=[edb])
        for c in range(NCH):
            k.op("dve", lambda e, c=c: e.tensor_scalar(
                edb[:, c * TBLK:(c + 1) * TBLK],
                edb[:, c * TBLK:(c + 1) * TBLK], edc[:, c:c + 1], None,
                ALU.add),
                reads=[edb, edc], writes=[edb])

        for blk in range(NBLK):
            t0 = blk * TBLK
            if blk == 0:
                k.op("dve", lambda e: e.memset(
                    x0b[:, :].rearrange("p (c w) -> p c w", c=NCH)[:, :, 0:1],
                    0.0), writes=[x0b])
            if blk == NBLK - 1:
                k.op("dve", lambda e: e.memset(
                    x0b[:, :].rearrange("p (c w) -> p c w", c=NCH)
                    [:, :, TBLK + 1:TBLK + 2], 0.0), writes=[x0b])
            lo = max(t0 - 1, 0)
            hi = min(t0 + TBLK + 1, T)
            off = 0 if blk > 0 else 1
            k.dma("sp",
                  x0b[:, :].rearrange("p (c w) -> p c w", c=NCH)
                  [:, :, off:off + (hi - lo)],
                  x0_dram.ap.rearrange("(c p) t -> p c t", p=128)[:, :, lo:hi],
                  reads=[x0_dram], writes=[x0b])
            _emit_mix_block(k, x0b, xmb, tmA, a1A, a2A, TBLK,
                            lambda c: c * (TBLK + 2) + 1,
                            lambda c: slice(c * TBLK, (c + 1) * TBLK))
            for mc in range(NCH):
                for c in range(NCH):
                    k.mm(cm.ps0[:, 0:TBLK],
                         wkT[:, c * D + mc * 128: c * D + (mc + 1) * 128],
                         xmb[:, c * TBLK:(c + 1) * TBLK],
                         reads=[wkT, xmb], writes=[cm.ps0],
                         start=(c == 0), stop=(c == NCH - 1))
                k.op("dve", lambda e, mc=mc: e.tensor_scalar_min(
                    kb[:, mc * TBLK:(mc + 1) * TBLK], cm.ps0[:, 0:TBLK],
                    60.0), reads=[cm.ps0], writes=[kb])
            k.op("act", lambda e: e.activation(kb[:, :], kb[:, :], AF.Exp),
                 reads=[kb], writes=[kb])
            for mc in range(NCH):
                for c in range(NCH):
                    k.mm(cm.ps1[:, 0:TBLK],
                         wvT[:, c * D + mc * 128: c * D + (mc + 1) * 128],
                         xmb[:, c * TBLK:(c + 1) * TBLK],
                         reads=[wvT, xmb], writes=[cm.ps1],
                         start=(c == 0), stop=(c == NCH - 1))
                k.op("dve", lambda e, mc=mc: e.tensor_mul(
                    kvb[:, mc * TBLK:(mc + 1) * TBLK],
                    kb[:, mc * TBLK:(mc + 1) * TBLK], cm.ps1[:, 0:TBLK]),
                    reads=[kb, cm.ps1], writes=[kvb])
            for mc in range(NCH):
                for c in range(NCH):
                    k.mm(cm.ps2[:, 0:TBLK],
                         wrT[:, c * D + mc * 128: c * D + (mc + 1) * 128],
                         xmb[:, c * TBLK:(c + 1) * TBLK],
                         reads=[wrT, xmb], writes=[cm.ps2],
                         start=(c == 0), stop=(c == NCH - 1))
                k.op("act", lambda e, mc=mc: e.activation(
                    sigr[:, mc * TBLK:(mc + 1) * TBLK], cm.ps2[:, 0:TBLK],
                    AF.Sigmoid), reads=[cm.ps2], writes=[sigr])
            for c in range(NCH):
                k.op("dve", lambda e, c=c: e.tensor_copy(
                    Ab[:, c * (TBLK + 1): c * (TBLK + 1) + 1],
                    carA[:, c:c + 1]), reads=[carA], writes=[Ab])
                k.op("dve", lambda e, c=c: e.tensor_copy(
                    Bb[:, c * (TBLK + 1): c * (TBLK + 1) + 1],
                    carB[:, c:c + 1]), reads=[carB], writes=[Bb])
                k.op("dve", lambda e, c=c: e.tensor_tensor_scan(
                    Ab[:, c * (TBLK + 1) + 1: (c + 1) * (TBLK + 1)],
                    edb[:, c * TBLK:(c + 1) * TBLK],
                    kvb[:, c * TBLK:(c + 1) * TBLK],
                    carA[:, c:c + 1], ALU.mult, ALU.add),
                    reads=[edb, kvb, carA], writes=[Ab])
                k.op("dve", lambda e, c=c: e.tensor_tensor_scan(
                    Bb[:, c * (TBLK + 1) + 1: (c + 1) * (TBLK + 1)],
                    edb[:, c * TBLK:(c + 1) * TBLK],
                    kb[:, c * TBLK:(c + 1) * TBLK],
                    carB[:, c:c + 1], ALU.mult, ALU.add),
                    reads=[edb, kb, carB], writes=[Bb])
                k.op("dve", lambda e, c=c: e.tensor_copy(
                    carA[:, c:c + 1],
                    Ab[:, (c + 1) * (TBLK + 1) - 1: (c + 1) * (TBLK + 1)]),
                    reads=[Ab], writes=[carA])
                k.op("dve", lambda e, c=c: e.tensor_copy(
                    carB[:, c:c + 1],
                    Bb[:, (c + 1) * (TBLK + 1) - 1: (c + 1) * (TBLK + 1)]),
                    reads=[Bb], writes=[carB])
                ash = Ab.ap[:, c * (TBLK + 1): c * (TBLK + 1) + TBLK]
                bsh = Bb.ap[:, c * (TBLK + 1): c * (TBLK + 1) + TBLK]
                csl = slice(c * TBLK, (c + 1) * TBLK)
                k.op("dve", lambda e, c=c, ash=ash, csl=csl:
                     e.scalar_tensor_tensor(
                         kvb[:, csl], kvb[:, csl], euc[:, c:c + 1], ash,
                         ALU.mult, ALU.add),
                     reads=[kvb, euc, Ab], writes=[kvb])
                k.op("dve", lambda e, c=c, bsh=bsh, csl=csl:
                     e.scalar_tensor_tensor(
                         kb[:, csl], kb[:, csl], euc[:, c:c + 1], bsh,
                         ALU.mult, ALU.add),
                     reads=[kb, euc, Bb], writes=[kb])
                k.op("dve", lambda e, csl=csl: e.reciprocal(
                    kb[:, csl], kb[:, csl]), reads=[kb], writes=[kb])
                k.op("dve", lambda e, csl=csl: e.tensor_mul(
                    kvb[:, csl], kvb[:, csl], kb[:, csl]),
                    reads=[kvb, kb], writes=[kvb])
                k.op("pool", lambda e, csl=csl: e.tensor_mul(
                    syb[:, csl], sigr[:, csl], kvb[:, csl]),
                    reads=[sigr, kvb], writes=[syb])
            for mc in range(NCH):
                for c in range(NCH):
                    k.mm(cm.ps0[:, 0:TBLK],
                         woT[:, c * D + mc * 128: c * D + (mc + 1) * 128],
                         syb[:, c * TBLK:(c + 1) * TBLK],
                         reads=[woT, syb], writes=[cm.ps0],
                         start=(c == 0), stop=(c == NCH - 1))
                x0sl = x0b.ap[:, mc * (TBLK + 2) + 1:
                              mc * (TBLK + 2) + 1 + TBLK]
                k.op("dve", lambda e, mc=mc, x0sl=x0sl: e.tensor_add(
                    xab[:, mc * TBLK:(mc + 1) * TBLK], cm.ps0[:, 0:TBLK],
                    x0sl), reads=[cm.ps0, x0b], writes=[xab])
            k.dma("pool",
                  xa_dram.ap.rearrange("(c p) t -> p c t", p=128)
                  [:, :, t0:t0 + TBLK],
                  xab[:, :].rearrange("p (c w) -> p c w", c=NCH),
                  reads=[xab], writes=[xa_dram])
        k.barrier()
        st_tm.close()

        st_cm = contextlib.ExitStack()
        mixbuf = k.sb("l1_mix", [128, NCH * T], BF16, ctx=st_cm)
        emit_ln_ff(k, cm, mixbuf, xin_dram=xa_dram.ap, in_bf16=True,
                   gamma_col=g2, beta_col=b2)
        for c in range(NCH):
            k.dma("pool", xb_dram.ap[c * 128:(c + 1) * 128, :],
                  mixbuf[:, c * T:(c + 1) * T],
                  reads=[mixbuf], writes=[xb_dram])
        xbt = k.sb("l1_xbt", [128, T + 2], BF16, ctx=st_cm)
        k.op("dve", lambda e: e.memset(xbt[:, 0:1], 0.0), writes=[xbt])
        k.op("dve", lambda e: e.memset(xbt[:, T + 1:T + 2], 0.0),
             writes=[xbt])
        for c in range(NCH):
            k.op("dve", lambda e, c=c: e.tensor_copy(
                xbt[:, 1:T + 1], mixbuf[:, c * T:(c + 1) * T]),
                reads=[mixbuf], writes=[xbt])
            _emit_mix_block(k, xbt, mixbuf, tmF, a1F, a2F, T,
                            lambda cc: 1,
                            lambda cc: slice(cc * T, (cc + 1) * T),
                            chunks=[c])
        H2 = 4096
        NH2 = H2 // 128
        fkbA = k.sb("l1_fkA", [128, NCH * 128], BF16, ctx=st_cm)
        fkbB = k.sb("l1_fkB", [128, NCH * 128], BF16, ctx=st_cm)
        k2t = k.sb("l1_k2", [128, NH2 * 512], BF16, ctx=st_cm)
        fvbA = k.sb("l1_fvA", [128, 128], BF16, ctx=st_cm)
        fvbB = k.sb("l1_fvB", [128, 128], BF16, ctx=st_cm)
        frbA = k.sb("l1_frA", [128, NCH * 128], BF16, ctx=st_cm)
        kvt = k.sb("l1_kvt", [128, 512], F32, ctx=st_cm)
        xbl = k.sb("l1_xbl", [128, 512], BF16, ctx=st_cm)
        sst = k.sb("l1_sst", [128, 512], F32, ctx=st_cm)
        for tt in range(NTT):
            tsl = slice(tt * 512, (tt + 1) * 512)
            for hc in range(NH2):
                fkb = fkbA if hc % 2 == 0 else fkbB
                k.dma("sp", fkb[:, :].rearrange("p (c m) -> p c m", c=NCH),
                      wbf["fkT"].ap[:, hc * 128:(hc + 1) * 128].rearrange(
                          "(c p) m -> p c m", p=128),
                      reads=[wbf["fkT"]], writes=[fkb])
                for c in range(NCH):
                    k.mm(cm.ps0[:, :],
                         fkb[:, c * 128:(c + 1) * 128],
                         mixbuf[:, c * T + tt * 512: c * T + (tt + 1) * 512],
                         reads=[fkb, mixbuf], writes=[cm.ps0],
                         start=(c == 0), stop=(c == NCH - 1))
                k.op("dve", lambda e, hc=hc: e.tensor_scalar_max(
                    cm.ps0[:, :], cm.ps0[:, :], 0.0),
                    reads=[cm.ps0], writes=[cm.ps0])
                k.op("act", lambda e, hc=hc: e.activation(
                    k2t[:, hc * 512:(hc + 1) * 512], cm.ps0[:, :],
                    AF.Square), reads=[cm.ps0], writes=[k2t])
            for mc in range(NCH):
                k.dma("sp", frbA[:, :].rearrange("p (c m) -> p c m", c=NCH),
                      wbf["frT"].ap[:, mc * 128:(mc + 1) * 128].rearrange(
                          "(c p) m -> p c m", p=128),
                      reads=[wbf["frT"]], writes=[frbA])
                for c in range(NCH):
                    k.mm(cm.ps1[:, :],
                         frbA[:, c * 128:(c + 1) * 128],
                         mixbuf[:, c * T + tt * 512: c * T + (tt + 1) * 512],
                         reads=[frbA, mixbuf], writes=[cm.ps1],
                         start=(c == 0), stop=(c == NCH - 1))
                for hc in range(NH2):
                    fvb = fvbA if hc % 2 == 0 else fvbB
                    k.dma("act", fvb[:, :],
                          wbf["fvT"].ap[hc * 128:(hc + 1) * 128,
                                        mc * 128:(mc + 1) * 128],
                          reads=[wbf["fvT"]], writes=[fvb])
                    k.mm(cm.ps2[:, :], fvb[:, :],
                         k2t[:, hc * 512:(hc + 1) * 512],
                         reads=[fvb, k2t], writes=[cm.ps2],
                         start=(hc == 0), stop=(hc == NH2 - 1))
                k.op("dve", lambda e: e.tensor_copy(kvt[:, :], cm.ps2[:, :]),
                     reads=[cm.ps2], writes=[kvt])
                k.dma("sp", xbl[:, :],
                      xb_dram.ap[mc * 128:(mc + 1) * 128, tsl],
                      reads=[xb_dram], writes=[xbl])
                k.op("act", lambda e: e.activation(
                    sst[:, :], cm.ps1[:, :], AF.Sigmoid),
                    reads=[cm.ps1], writes=[sst])
                k.op("dve", lambda e: e.tensor_mul(sst[:, :], sst[:, :],
                                                   kvt[:, :]),
                     reads=[sst, kvt], writes=[sst])
                k.op("dve", lambda e: e.tensor_add(sst[:, :], sst[:, :],
                                                   xbl[:, :]),
                     reads=[sst, xbl], writes=[sst])
                k.dma("pool", s_dram.ap[mc * 128:(mc + 1) * 128, tsl],
                      sst[:, :], reads=[sst], writes=[s_dram])
        k.barrier()
        st_cm.close()


def emit_l1_ffn(k, cm, xnT, g_dram, wbf, fin, gates, scratch):
    HF = 4096
    NHT = HF // 512
    with k.scope() as st:
        gcol = k.sb("l1f_gcol", [128, NQT * 8], F32, ctx=st)
        sc = k.sb("l1f_sc", [128, 8], F32, ctx=st)
        nmax = k.sb("l1f_nmax", [128, 1], F32, ctx=st)
        zz = k.sb("l1f_z", [128, 1], F32, ctx=st)
        rz = k.sb("l1f_rz", [128, 1], F32, ctx=st)
        m1 = k.sb("l1f_m1", [128, 1], F32, ctx=st)
        m2 = k.sb("l1f_m2", [128, 1], F32, ctx=st)
        mk1 = k.sb("l1f_mk1", [128, 8], F32, ctx=st)
        mk2 = k.sb("l1f_mk2", [128, 8], F32, ctx=st)
        gg2 = k.sb("l1f_gg2", [128, 8], F32, ctx=st)
        vs = k.sb("l1f_vs", [128, 1], F32, ctx=st)
        vs2 = k.sb("l1f_vs2", [128, NQT], F32, ctx=st)
        vs2r_d = scratch["row_dram"]
        for tt in range(NQT):
            gsl = slice(tt * 8, (tt + 1) * 8)
            k.op("dve", lambda e, gsl=gsl: e.tensor_reduce(
                nmax[:, :], gates.ap[:, gsl], AX.X, ALU.max, negate=True),
                reads=[gates], writes=[nmax])
            k.op("act", lambda e, gsl=gsl: e.activation(
                sc[:, :], gates.ap[:, gsl], AF.Exp, bias=nmax[:, 0:1],
                accum_out=zz[:, :]),
                reads=[gates, nmax], writes=[sc, zz])
            k.op("dve", lambda e: e.reciprocal(rz[:, :], zz[:, :]),
                 reads=[zz], writes=[rz])
            k.op("dve", lambda e: e.tensor_scalar(
                sc[:, :], sc[:, :], rz[:, 0:1], None, ALU.mult),
                reads=[sc, rz], writes=[sc])
            k.op("dve", lambda e: e.tensor_reduce(
                m1[:, :], sc[:, :], AX.X, ALU.max), reads=[sc], writes=[m1])
            k.op("dve", lambda e: e.tensor_scalar(
                mk1[:, :], sc[:, :], m1[:, 0:1], None, ALU.is_equal),
                reads=[sc, m1], writes=[mk1])
            k.op("dve", lambda e: e.scalar_tensor_tensor(
                gg2[:, :], mk1[:, :], NEG, sc[:, :], ALU.mult, ALU.add),
                reads=[mk1, sc], writes=[gg2])
            k.op("dve", lambda e: e.tensor_reduce(
                m2[:, :], gg2[:, :], AX.X, ALU.max), reads=[gg2], writes=[m2])
            k.op("dve", lambda e: e.tensor_scalar(
                mk2[:, :], gg2[:, :], m2[:, 0:1], None, ALU.is_equal),
                reads=[gg2, m2], writes=[mk2])
            k.op("dve", lambda e: e.tensor_add(vs[:, :], m1[:, :], m2[:, :]),
                 reads=[m1, m2], writes=[vs])
            k.op("dve", lambda e: e.tensor_add(mk1[:, :], mk1[:, :],
                                               mk2[:, :]),
                 reads=[mk1, mk2], writes=[mk1])
            k.op("dve", lambda e, tt=tt: e.tensor_scalar(
                gcol[:, tt * 8:(tt + 1) * 8], mk1[:, :], vs[:, 0:1], None,
                ALU.mult), reads=[mk1, vs], writes=[gcol])
            k.op("dve", lambda e, tt=tt: e.tensor_scalar(
                vs2[:, tt:tt + 1], vs[:, :], 2.0, None, ALU.mult),
                reads=[vs], writes=[vs2])
        for tb in range(NQT):
            k.dma("pool", vs2r_d.ap[tb, :].unsqueeze(0).rearrange(
                "a p -> (a p)").unsqueeze(1),
                vs2[:, tb:tb + 1], reads=[vs2], writes=[vs2r_d])
        vsrow_b = k.sb("l1f_vsrb", [1, T], BF16, ctx=st)
        k.dma("pool", vsrow_b[0:1, :],
              vs2r_d.ap.rearrange("t p -> (t p)").unsqueeze(0),
              reads=[vs2r_d], writes=[vsrow_b])

        k.dump("l1f_gcol", gcol.ap[:, :], [128, NQT * 8], [gcol])
        k.dump("l1f_vsrow", vsrow_b.ap[0:1, :], [1, T], [vsrow_b], dt=BF16)
        aeT = k.sb("l1f_aeT", [128, 8 * NCH * 16], BF16, ctx=st)
        atmp = k.sb("l1f_atmp", [16, D], BF16, ctx=st)
        for e_ in range(8):
            k.dma("sp", atmp[0:16, :], wbf["L1_ffn_lora_A"].ap[e_],
                  reads=[wbf["L1_ffn_lora_A"]], writes=[atmp])
            for c in range(NCH):
                k.tr(cm.psT[0:128, 0:16],
                     atmp[0:16, c * 128:(c + 1) * 128], cm.ident[0:16, 0:16],
                     reads=[atmp, cm.ident], writes=[cm.psT])
                k.op("dve", lambda e, e_=e_, c=c: e.tensor_copy(
                    aeT[:, (e_ * NCH + c) * 16:(e_ * NCH + c + 1) * 16],
                    cm.psT[0:128, 0:16]),
                    reads=[cm.psT], writes=[aeT])
        beT_d = scratch["beT_dram"]
        btmp = k.sb("l1f_btmp", [128, 16], BF16, ctx=st)
        btr = k.sb("l1f_btr", [16, 128], BF16, ctx=st)
        for e_ in range(8):
            for hc in range(32):
                k.dma("sp", btmp[:, :],
                      wbf["L1_ffn_lora_B"].ap[e_][hc * 128:(hc + 1) * 128, :],
                      reads=[wbf["L1_ffn_lora_B"]], writes=[btmp])
                k.tr(cm.psT[0:16, 0:128], btmp[:, :], cm.ident[:, :],
                     reads=[btmp, cm.ident], writes=[cm.psT])
                k.op("dve", lambda e: e.tensor_copy(
                    btr[0:16, :], cm.psT[0:16, 0:128]),
                    reads=[cm.psT], writes=[btr])
                k.dma("pool", beT_d.ap[e_][:, hc * 128:(hc + 1) * 128],
                      btr[0:16, :], reads=[btr], writes=[beT_d])
        birow = k.sb("l1f_bi", [1, HF], BF16, ctx=st)
        borow = k.sb("l1f_bo", [1, D], BF16, ctx=st)
        load_row_bf16(k, birow, fin["L1_ffn_bi"], HF)
        load_row_bf16(k, borow, fin["L1_ffn_bo"], D)

        wit = k.sb("l1f_wit", [128, NCH * 512], BF16, ctx=st)
        wot = k.sb("l1f_wot", [128, 32 * 128], BF16, ctx=st)
        base_all = k.sb("l1f_base", [128, 4 * HF], BF16, ctx=st)
        h_e = k.sb("l1f_he", [128, 512], BF16, ctx=st)
        acc = k.sb("l1f_acc", [128, 4 * HF], BF16, ctx=st)
        hsT = base_all
        rA = k.sb("l1f_rA", [16, 8 * 512], BF16, ctx=st)
        beT = k.sb("l1f_beT", [16, HF], BF16, ctx=st)
        gtmp = k.sb("l1f_gtmp", [128, 512], F32, ctx=st)
        for tt in range(NTT):
            tsl = slice(tt * 512, (tt + 1) * 512)
            for e_ in range(8):
                for c in range(NCH):
                    k.mm(cm.psS[0:16, 0:512],
                         aeT[:, (e_ * NCH + c) * 16:(e_ * NCH + c + 1) * 16],
                         xnT[:, c * T + tt * 512: c * T + (tt + 1) * 512],
                         reads=[aeT, xnT], writes=[cm.psS],
                         start=(c == 0), stop=(c == NCH - 1))
                k.op("dve", lambda e, e_=e_: e.tensor_copy(
                    rA[0:16, e_ * 512:(e_ + 1) * 512], cm.psS[0:16, 0:512]),
                    reads=[cm.psS], writes=[rA])
            for ts4 in range(4):
                tb = tt * 4 + ts4
                for ht in range(NHT):
                    k.dma("sp",
                          wit[:, :].rearrange("p (c m) -> p c m", c=NCH),
                          wbf["wiT"].ap[:, ht * 512:(ht + 1) * 512]
                          .rearrange("(c p) m -> p c m", p=128),
                          reads=[wbf["wiT"]], writes=[wit])
                    for c in range(NCH):
                        k.mm(cm.ps0[:, :],
                             xnT[:, c * T + tb * 128: c * T + (tb + 1) * 128],
                             wit[:, c * 512:(c + 1) * 512],
                             reads=[xnT, wit], writes=[cm.ps0],
                             start=(c == 0), stop=False)
                    k.mm(cm.ps0[:, :], cm.ones1[0:1, 0:128],
                         birow[0:1, ht * 512:(ht + 1) * 512],
                         reads=[cm.ones1, birow], writes=[cm.ps0],
                         start=False, stop=True)
                    k.op("act", lambda e, ts4=ts4, ht=ht: e.activation(
                        base_all[:, ts4 * HF + ht * 512:
                                 ts4 * HF + (ht + 1) * 512],
                        cm.ps0[:, :], AF.Copy),
                        reads=[cm.ps0], writes=[base_all])
            for e_ in range(8):
                k.dma("sp", beT[0:16, :], beT_d.ap[e_],
                      reads=[beT_d], writes=[beT])
                for ts4 in range(4):
                    tb = tt * 4 + ts4
                    for ht in range(NHT):
                        bsl = slice(ts4 * HF + ht * 512,
                                    ts4 * HF + (ht + 1) * 512)
                        k.mm(cm.ps1[:, :],
                             rA[0:16, e_ * 512 + ts4 * 128:
                                e_ * 512 + (ts4 + 1) * 128],
                             beT[0:16, ht * 512:(ht + 1) * 512],
                             reads=[rA, beT], writes=[cm.ps1],
                             start=True, stop=False)
                        k.mm(cm.ps1[:, :], cm.ident[:, :],
                             base_all[:, bsl],
                             reads=[cm.ident, base_all], writes=[cm.ps1],
                             start=False, stop=True)
                        k.op("act", lambda e: e.activation(
                            h_e[:, :], cm.ps1[:, :], AF.Relu),
                            reads=[cm.ps1], writes=[h_e])
                        eng = "dve"
                        if e_ == 0:
                            k.op(eng, lambda e, bsl=bsl, tb=tb, e_=e_:
                                 e.tensor_scalar(
                                     acc[:, bsl], h_e[:, :],
                                     gcol[:, tb * 8 + e_:tb * 8 + e_ + 1],
                                     None, ALU.mult),
                                 reads=[h_e, gcol], writes=[acc])
                        else:
                            k.op(eng, lambda e, bsl=bsl, tb=tb, e_=e_:
                                 e.scalar_tensor_tensor(
                                     acc[:, bsl], h_e[:, :],
                                     gcol[:, tb * 8 + e_:tb * 8 + e_ + 1],
                                     acc[:, bsl], ALU.mult, ALU.add),
                                 reads=[h_e, gcol, acc], writes=[acc])
            if tt == 0:
                k.dump("l1f_acc0", acc.ap[:, :], [128, 4 * HF], [acc],
                       dt=BF16)
                k.dump("l1f_base0", base_all.ap[:, :], [128, 4 * HF],
                       [base_all], dt=BF16)
                k.dump("l1f_rA0", rA.ap[0:16, :], [16, 8 * 512], [rA],
                       dt=BF16)
            for ts4 in range(4):
                for hc in range(32):
                    k.tr(cm.psT[:, 0:128],
                         acc[:, ts4 * HF + hc * 128:
                             ts4 * HF + (hc + 1) * 128],
                         cm.ident[:, :],
                         reads=[acc, cm.ident], writes=[cm.psT])
                    k.op("dve", lambda e, hc=hc, ts4=ts4: e.tensor_copy(
                        hsT[:, hc * 512 + ts4 * 128:
                            hc * 512 + (ts4 + 1) * 128],
                        cm.psT[:, 0:128]),
                        reads=[cm.psT], writes=[hsT])
            for mc in range(NCH):
                k.dma("sp",
                      wot[:, :].rearrange("p (hc m) -> p hc m", hc=32),
                      wbf["woT"].ap[:, mc * 128:(mc + 1) * 128]
                      .rearrange("(hc p) m -> p hc m", p=128),
                      reads=[wbf["woT"]], writes=[wot])
                for hc in range(32):
                    k.mm(cm.ps2[:, :],
                         wot[:, hc * 128:(hc + 1) * 128],
                         hsT[:, hc * 512:(hc + 1) * 512],
                         reads=[wot, hsT], writes=[cm.ps2],
                         start=(hc == 0), stop=False)
                k.mm(cm.ps2[:, :], borow[0:1, mc * 128:(mc + 1) * 128],
                     vsrow_b[0:1, tsl],
                     reads=[borow, vsrow_b], writes=[cm.ps2],
                     start=False, stop=True)
                k.op("act", lambda e: e.activation(gtmp[:, :],
                                                   cm.ps2[:, :], AF.Silu),
                     reads=[cm.ps2], writes=[gtmp])
                k.dma("pool", g_dram.ap[mc * 128:(mc + 1) * 128, tsl],
                      gtmp[:, :], reads=[gtmp], writes=[g_dram])





def emit_l2_attn(k, cm, xnT, s_dram, wbf, fin):
    HS = 128
    with k.scope() as st:
        wqT = k.sb("l2_wqT", [128, 2 * NCH * 128], BF16, ctx=st)
        wkT = k.sb("l2_wkT", [128, NCH * 128], BF16, ctx=st)
        wvT = k.sb("l2_wvT", [128, NCH * 128], BF16, ctx=st)
        woT = k.sb("l2_woT", [128, 2 * NCH * 128], BF16, ctx=st)
        for h in range(2):
            for c in range(NCH):
                k.dmat(wqT[:, (h * NCH + c) * 128:(h * NCH + c + 1) * 128],
                       wbf["L2_attn_Wq"].ap[h][:, c * 128:(c + 1) * 128],
                       reads=[wbf["L2_attn_Wq"]], writes=[wqT])
        for c in range(NCH):
            k.dmat(wkT[:, c * 128:(c + 1) * 128],
                   wbf["L2_attn_Wk"].ap[:, c * 128:(c + 1) * 128],
                   reads=[wbf["L2_attn_Wk"]], writes=[wkT])
            k.dmat(wvT[:, c * 128:(c + 1) * 128],
                   wbf["L2_attn_Wv"].ap[:, c * 128:(c + 1) * 128],
                   reads=[wbf["L2_attn_Wv"]], writes=[wvT])
        for j in range(2):
            for c in range(NCH):
                k.dmat(woT[:, (j * NCH + c) * 128:(j * NCH + c + 1) * 128],
                       wbf["L2_attn_Wo"].ap[c * 128:(c + 1) * 128,
                                            j * 128:(j + 1) * 128],
                       reads=[wbf["L2_attn_Wo"]], writes=[woT])
        qrT = k.sb("l2_qrT", [128, 2 * T], BF16, ctx=st)
        krT = k.sb("l2_krT", [128, T], BF16, ctx=st)
        v_sb = k.sb("l2_v", [128, NQT * 128], BF16, ctx=st)
        aoT = k.sb("l2_aoT", [128, 2 * T], BF16, ctx=st)
        st2 = contextlib.ExitStack()
        ropeC = k.sb("l2_ropc", [128, T], F32, ctx=st2)
        ropeS = k.sb("l2_rops", [128, T], F32, ctx=st2)
        swp = k.sb("l2_swp", [128, 128], F32, ctx=st2)
        k.dma("sp", ropeC[:, :], fin["c_rope_cos"][:, :], writes=[ropeC])
        k.dma("sp", ropeS[:, :], fin["c_rope_sin"][:, :], writes=[ropeS])
        k.dma("sp", swp[:, :], fin["c_rope_swap"][:, :], writes=[swp])
        raw = k.sb("l2_raw", [128, T], F32, ctx=st2)
        tr1 = k.sb("l2_tr1", [128, T], F32, ctx=st2)

        def qk_proj(wT, w_off, dst, dst_off, scale):
            for tt in range(NTT):
                tsl = slice(tt * 512, (tt + 1) * 512)
                for c in range(NCH):
                    k.mm(cm.ps0[:, :],
                         wT.ap[:, w_off + c * 128: w_off + (c + 1) * 128],
                         xnT[:, c * T + tt * 512: c * T + (tt + 1) * 512],
                         reads=[wT, xnT], writes=[cm.ps0],
                         start=(c == 0), stop=(c == NCH - 1))
                k.op("dve", lambda e, tsl=tsl: e.tensor_copy(
                    raw[:, tsl], cm.ps0[:, :]), reads=[cm.ps0], writes=[raw])
            for tt in range(NTT):
                tsl = slice(tt * 512, (tt + 1) * 512)
                for q4 in range(4):
                    ssl = slice(tt * 512 + q4 * 128, tt * 512 + (q4 + 1) * 128)
                    k.mm(cm.ps1[:, 0:128], swp[:, :], raw.ap[:, ssl],
                         reads=[swp, raw], writes=[cm.ps1],
                         start=True, stop=True)
                    k.op("dve", lambda e, ssl=ssl: e.tensor_mul(
                        tr1[:, ssl], cm.ps1[:, 0:128], ropeS[:, ssl]),
                        reads=[cm.ps1, ropeS], writes=[tr1])
                k.op("dve", lambda e, tsl=tsl: e.tensor_mul(
                    raw[:, tsl], raw[:, tsl], ropeC[:, tsl]),
                    reads=[raw, ropeC], writes=[raw])
                k.op("dve", lambda e, tsl=tsl: e.tensor_add(
                    raw[:, tsl], raw[:, tsl], tr1[:, tsl]),
                    reads=[raw, tr1], writes=[raw])
                k.op("act", lambda e, tsl=tsl, dst=dst, dst_off=dst_off:
                     e.activation(dst.ap[:, dst_off + tsl.start:
                                         dst_off + tsl.stop],
                                  raw[:, tsl], AF.Copy, scale=scale),
                     reads=[raw], writes=[dst])

        qk_proj(wqT, 0, qrT, 0, HS ** -0.5)
        qk_proj(wqT, NCH * 128, qrT, T, HS ** -0.5)
        qk_proj(wkT, 0, krT, 0, 1.0)
        for kt in range(NQT):
            for c in range(NCH):
                k.mm(cm.ps1[:, 0:128],
                     xnT[:, c * T + kt * 128: c * T + (kt + 1) * 128],
                     wvT[:, c * 128:(c + 1) * 128],
                     reads=[xnT, wvT], writes=[cm.ps1],
                     start=(c == 0), stop=(c == NCH - 1))
            k.op("dve", lambda e, kt=kt: e.tensor_copy(
                v_sb[:, kt * 128:(kt + 1) * 128], cm.ps1[:, 0:128]),
                reads=[cm.ps1], writes=[v_sb])

        k.barrier()
        st2.close()
        Ssb = k.sb("l2_S", [128, T], F32, ctx=st)
        Psb = k.sb("l2_P", [128, T], BF16, ctx=st)
        PTs = k.sb("l2_PT", [128, 512], BF16, ctx=st)
        nmax = k.sb("l2_nmax", [128, 1], F32, ctx=st)
        zrow = k.sb("l2_zrow", [128, 1], F32, ctx=st)
        rz = k.sb("l2_rz", [128, 1], F32, ctx=st)
        cmask = k.sb("l2_cmask", [128, 4 * 512], F32, ctx=st)
        k.dma("sp", cmask[:, :].rearrange("p (r m) -> p r m", r=4),
              fin["c_causal"][:, :, :].rearrange("r p m -> p r m"),
              writes=[cmask])
        for h in range(2):
            for qt in range(NQT):
                nkt = qt // 4 + 1
                L = nkt * 512
                for j in range(nkt):
                    k.mm(cm.ps0[:, :],
                         qrT[:, h * T + qt * 128: h * T + (qt + 1) * 128],
                         krT[:, j * 512:(j + 1) * 512],
                         reads=[qrT, krT], writes=[cm.ps0],
                         start=True, stop=True)
                    if j == nkt - 1:
                        k.op("dve", lambda e, j=j, qt=qt: e.tensor_add(
                            Ssb[:, j * 512:(j + 1) * 512], cm.ps0[:, :],
                            cmask[:, (qt % 4) * 512:(qt % 4 + 1) * 512]),
                            reads=[cm.ps0, cmask], writes=[Ssb])
                    else:
                        k.op("dve", lambda e, j=j: e.tensor_copy(
                            Ssb[:, j * 512:(j + 1) * 512], cm.ps0[:, :]),
                            reads=[cm.ps0], writes=[Ssb])
                k.op("dve", lambda e, L=L: e.tensor_reduce(
                    nmax[:, :], Ssb[:, 0:L], AX.X, ALU.max, negate=True),
                    reads=[Ssb], writes=[nmax])
                k.op("act", lambda e, L=L: e.activation(
                    Psb[:, 0:L], Ssb[:, 0:L], AF.Exp, bias=nmax[:, 0:1],
                    accum_out=zrow[:, :]),
                    reads=[Ssb, nmax], writes=[Psb, zrow])
                k.op("dve", lambda e: e.reciprocal(rz[:, :], zrow[:, :]),
                     reads=[zrow], writes=[rz])
                k.op("dve", lambda e, L=L: e.tensor_scalar(
                    Psb[:, 0:L], Psb[:, 0:L], rz[:, 0:1], None, ALU.mult),
                    reads=[Psb, rz], writes=[Psb])
                for jj in range(nkt):
                    for q4 in range(4):
                        k.tr(cm.psT[:, q4 * 128:(q4 + 1) * 128],
                             Psb[:, jj * 512 + q4 * 128:
                                 jj * 512 + (q4 + 1) * 128],
                             cm.ident[:, :],
                             reads=[Psb, cm.ident], writes=[cm.psT])
                    k.op("dve", lambda e: e.tensor_copy(
                        PTs[:, :], cm.psT[:, 0:512]),
                        reads=[cm.psT], writes=[PTs])
                    for q4 in range(4):
                        kt = jj * 4 + q4
                        k.mm(cm.ps2[:, 0:128],
                             v_sb[:, kt * 128:(kt + 1) * 128],
                             PTs[:, q4 * 128:(q4 + 1) * 128],
                             reads=[v_sb, PTs], writes=[cm.ps2],
                             start=(kt == 0), stop=(kt == nkt * 4 - 1))
                k.op("dve", lambda e, h=h, qt=qt: e.tensor_copy(
                    aoT[:, h * T + qt * 128: h * T + (qt + 1) * 128],
                    cm.ps2[:, 0:128]),
                    reads=[cm.ps2], writes=[aoT])
        sstage = k.sb("l2_sst", [128, 512], F32, ctx=st)
        for mc in range(NCH):
            for tt in range(NTT):
                tsl = slice(tt * 512, (tt + 1) * 512)
                for j in range(2):
                    k.mm(cm.ps0[:, :],
                         woT[:, (j * NCH + mc) * 128:
                             (j * NCH + mc + 1) * 128],
                         aoT[:, j * T + tt * 512: j * T + (tt + 1) * 512],
                         reads=[woT, aoT], writes=[cm.ps0],
                         start=(j == 0), stop=(j == 1))
                k.op("dve", lambda e: e.tensor_copy(sstage[:, :],
                                                    cm.ps0[:, :]),
                     reads=[cm.ps0], writes=[sstage])
                k.dma("pool", s_dram.ap[mc * 128:(mc + 1) * 128, tsl],
                      sstage[:, :], reads=[sstage], writes=[s_dram])


def emit_l2_ffn(k, cm, xnT, g_dram, wbf, fin, gates):
    DD = 256
    with k.scope() as st:
        msk = k.sb("l2_msk", [128, NQT * 8], F32, ctx=st)
        m1 = k.sb("l2_m1", [128, 1], F32, ctx=st)
        for tt in range(NQT):
            gsl = slice(tt * 8, (tt + 1) * 8)
            k.op("dve", lambda e, gsl=gsl: e.tensor_reduce(
                m1[:, :], gates.ap[:, gsl], AX.X, ALU.max),
                reads=[gates], writes=[m1])
            k.op("dve", lambda e, gsl=gsl: e.tensor_scalar(
                msk[:, gsl], gates.ap[:, gsl], m1[:, 0:1], None,
                ALU.is_equal),
                reads=[gates, m1], writes=[msk])
        wg_sb = k.sb("l2_wg", [128, NCH * DD], BF16, ctx=st)
        wu_sb = k.sb("l2_wu", [128, NCH * DD], BF16, ctx=st)
        wd_sb = k.sb("l2_wd", [128, 2 * 1024], BF16, ctx=st)
        inter = k.sb("l2_int", [128, 2 * 512], BF16, ctx=st)
        sq = k.sb("l2_sq", [128, 512], F32, ctx=st)
        acc = k.sb("l2_acc", [128, 4 * D], F32, ctx=st)
        sil = k.sb("l2_sil", [128, 4 * D], BF16, ctx=st)
        gst = k.sb("l2_gst", [128, 512], BF16, ctx=st)
        for tt in range(NTT):
            for e_ in range(8):
                k.dma("sp", wg_sb[:, :].rearrange("p (c m) -> p c m", c=NCH),
                      wbf["L2_ffn_wg"].ap[e_].rearrange("(c p) m -> p c m",
                                                        p=128),
                      reads=[wbf["L2_ffn_wg"]], writes=[wg_sb])
                k.dma("sp", wu_sb[:, :].rearrange("p (c m) -> p c m", c=NCH),
                      wbf["L2_ffn_wu"].ap[e_].rearrange("(c p) m -> p c m",
                                                        p=128),
                      reads=[wbf["L2_ffn_wu"]], writes=[wu_sb])
                k.dma("sp", wd_sb[:, :].rearrange("p (c m) -> p c m", c=2),
                      wbf["L2_ffn_wd"].ap[e_].rearrange("(c p) m -> p c m",
                                                        p=128),
                      reads=[wbf["L2_ffn_wd"]], writes=[wd_sb])
                for ddc in range(2):
                    for c in range(NCH):
                        k.mm(cm.ps0[:, :],
                             wg_sb[:, c * DD + ddc * 128:
                                   c * DD + (ddc + 1) * 128],
                             xnT[:, c * T + tt * 512: c * T + (tt + 1) * 512],
                             reads=[wg_sb, xnT], writes=[cm.ps0],
                             start=(c == 0), stop=(c == NCH - 1))
                    for c in range(NCH):
                        k.mm(cm.ps1[:, :],
                             wu_sb[:, c * DD + ddc * 128:
                                   c * DD + (ddc + 1) * 128],
                             xnT[:, c * T + tt * 512: c * T + (tt + 1) * 512],
                             reads=[wu_sb, xnT], writes=[cm.ps1],
                             start=(c == 0), stop=(c == NCH - 1))
                    k.op("dve", lambda e: e.tensor_scalar_max(
                        sq[:, :], cm.ps0[:, :], 0.0),
                        reads=[cm.ps0], writes=[sq])
                    k.op("act", lambda e: e.activation(
                        sq[:, :], sq[:, :], AF.Square),
                        reads=[sq], writes=[sq])
                    k.op("dve", lambda e, ddc=ddc: e.tensor_mul(
                        inter[:, ddc * 512:(ddc + 1) * 512], sq[:, :],
                        cm.ps1[:, :]),
                        reads=[sq, cm.ps1], writes=[inter])
                for ts4 in range(4):
                    tb = tt * 4 + ts4
                    for mt in range(2):
                        for ddc in range(2):
                            k.mm(cm.ps2[:, :],
                                 inter[:, ddc * 512 + ts4 * 128:
                                       ddc * 512 + (ts4 + 1) * 128],
                                 wd_sb[:, ddc * 1024 + mt * 512:
                                       ddc * 1024 + (mt + 1) * 512],
                                 reads=[inter, wd_sb], writes=[cm.ps2],
                                 start=(ddc == 0), stop=(ddc == 1))
                        out_sl = slice(ts4 * D + mt * 512,
                                       ts4 * D + (mt + 1) * 512)
                        if e_ == 0:
                            k.op("dve", lambda e, out_sl=out_sl, tb=tb, e_=e_:
                                 e.tensor_scalar(
                                     acc[:, out_sl], cm.ps2[:, :],
                                     msk[:, tb * 8 + e_:tb * 8 + e_ + 1],
                                     None, ALU.mult),
                                 reads=[cm.ps2, msk], writes=[acc])
                        else:
                            k.op("dve", lambda e, out_sl=out_sl, tb=tb, e_=e_:
                                 e.scalar_tensor_tensor(
                                     acc[:, out_sl], cm.ps2[:, :],
                                     msk[:, tb * 8 + e_:tb * 8 + e_ + 1],
                                     acc[:, out_sl], ALU.mult, ALU.add),
                                 reads=[cm.ps2, msk, acc], writes=[acc])
            k.op("act", lambda e: e.activation(sil[:, :], acc[:, :], AF.Silu),
                 reads=[acc], writes=[sil])
            for c in range(NCH):
                for ts4 in range(4):
                    k.tr(cm.psT[:, 0:128],
                         sil[:, ts4 * D + c * 128: ts4 * D + (c + 1) * 128],
                         cm.ident[:, :],
                         reads=[sil, cm.ident], writes=[cm.psT])
                    k.op("dve", lambda e, ts4=ts4: e.tensor_copy(
                        gst[:, ts4 * 128:(ts4 + 1) * 128], cm.psT[:, 0:128]),
                        reads=[cm.psT], writes=[gst])
                k.dma("pool",
                      g_dram.ap[c * 128:(c + 1) * 128,
                                tt * 512:(tt + 1) * 512],
                      gst[:, :], reads=[gst], writes=[g_dram])





def emit_l3_attn(k, cm, xnT, s_dram, wbf, fin, scratch):
    from concourse.ap import AP as APc
    ao_dram = scratch["ao_dram"]
    with k.scope() as st:
        dall = k.sb("l3_dall", [128, 16 * 256], F32, ctx=st)
        wpT = k.sb("l3_wpT", [128, NCH * D], BF16, ctx=st)
        for c in range(NCH):
            k.dmat(wpT[:, c * D:(c + 1) * D],
                   wbf["L3_attn_W_proj"].ap[0:D, c * 128:(c + 1) * 128],
                   reads=[wbf["L3_attn_W_proj"]], writes=[wpT])
        barow = k.sb("l3_barow", [1, 3072], BF16, ctx=st)
        bprow = k.sb("l3_bprow", [1, D], BF16, ctx=st)
        load_row_bf16(k, barow, fin["L3_attn_b_attn"], 3072)
        load_row_bf16(k, bprow, fin["L3_attn_b_proj"], D)
        with k.scope() as stb:
            relmap = k.sb("l3_relmap", [32, 384], F32, ctx=stb)
            rele = k.sb("l3_rele", [32, 16], F32, ctx=stb)
            prev = k.sb("l3_prev", [128, 128], F32, ctx=stb)
            revp = k.sb("l3_revp", [128, 3 * 16], F32, ctx=stb)
            revt = k.sb("l3_revt", [16, 384], F32, ctx=stb)
            eps_t = k.sb("l3_eps", [128, 256], F32, ctx=stb)
            revt_d = k.dram("l3_revt_d", [16, 384], F32)
            k.dma("sp", relmap[0:32, :], fin["c_relmap"][:, :],
                  writes=[relmap])
            k.dma("sp", rele[0:32, :], fin["L3_attn_rel_emb"][:, :],
                  writes=[rele])
            k.dma("sp", prev[:, :], fin["c_prev"][:, :], writes=[prev])
            for i in range(3):
                k.mm(cm.psS[:, 0:16], relmap[0:32, i * 128:(i + 1) * 128],
                     rele[0:32, :], reads=[relmap, rele], writes=[cm.psS])
                k.op("dve", lambda e, i=i: e.tensor_copy(
                    revp[:, i * 16:(i + 1) * 16], cm.psS[:, 0:16]),
                    reads=[cm.psS], writes=[revp])
                k.tr(cm.psS[0:16, 128:256], revp[:, i * 16:(i + 1) * 16],
                     cm.identf[:, :], reads=[revp, cm.identf],
                     writes=[cm.psS])
                k.op("dve", lambda e, i=i: e.tensor_copy(
                    revt[0:16, i * 128:(i + 1) * 128], cm.psS[0:16, 128:256]),
                    reads=[cm.psS], writes=[revt])
            k.dma("pool", revt_d.ap[:, :], revt[0:16, :],
                  reads=[revt], writes=[revt_d])
            for h in range(16):
                win = APc(revt_d.ap.tensor, h * 384, [[1, 128], [1, 256]])
                k.dma("sp", eps_t[:, :], win, reads=[revt_d], writes=[eps_t])
                k.mm(cm.ps0[:, 0:256], prev[:, :], eps_t[:, :],
                     reads=[prev, eps_t], writes=[cm.ps0])
                k.op("dve", lambda e, h=h: e.tensor_copy(
                    dall[:, h * 256:(h + 1) * 256], cm.ps0[:, 0:256]),
                    reads=[cm.ps0], writes=[dall])

        cmask = k.sb("l3_cmask", [128, 4 * 512], F32, ctx=st)
        k.dma("sp", cmask[:, :].rearrange("p (r m) -> p r m", r=4),
              fin["c_causal"][:, :, :].rearrange("r p m -> p r m"),
              writes=[cmask])

        qT = k.sb("l3_qT", [128, T], BF16, ctx=st)
        kT = k.sb("l3_kT", [128, T], BF16, ctx=st)
        v_sb = k.sb("l3_v", [128, NQT * 128], BF16, ctx=st)
        wblk = k.sb("l3_wblk", [128, NCH * 128], BF16, ctx=st)
        Ssb = k.sb("l3_S", [128, T], F32, ctx=st)
        Psb = k.sb("l3_P", [128, T], BF16, ctx=st)
        PTs = k.sb("l3_PT", [128, 512], BF16, ctx=st)
        nmax = k.sb("l3_nmax", [128, 1], F32, ctx=st)
        zrow = k.sb("l3_zrow", [128, 1], F32, ctx=st)
        rz = k.sb("l3_rz", [128, 1], F32, ctx=st)
        aog = k.sb("l3_aog", [128, T], BF16, ctx=st)
        for g2 in range(8):
            for dst, base, scale in ((qT, 0, 0.125), (kT, D, 1.0)):
                k.dma("sp", wblk[:, :].rearrange("p (c m) -> p c m", c=NCH),
                      wbf["waT"].ap[:, base + g2 * 128: base + (g2 + 1) * 128]
                      .rearrange("(c p) m -> p c m", p=128),
                      reads=[wbf["waT"]], writes=[wblk])
                for tt in range(NTT):
                    for c in range(NCH):
                        k.mm(cm.ps0[:, :],
                             wblk[:, c * 128:(c + 1) * 128],
                             xnT[:, c * T + tt * 512: c * T + (tt + 1) * 512],
                             reads=[wblk, xnT], writes=[cm.ps0],
                             start=(c == 0), stop=False)
                    k.mm(cm.ps0[:, :],
                         barow[0:1, base + g2 * 128: base + (g2 + 1) * 128],
                         cm.ones1[0:1, 0:512],
                         reads=[barow, cm.ones1], writes=[cm.ps0],
                         start=False, stop=True)
                    k.op("act", lambda e, tt=tt, dst=dst, scale=scale:
                         e.activation(
                             dst.ap[:, tt * 512:(tt + 1) * 512],
                             cm.ps0[:, :], AF.Copy, scale=scale),
                         reads=[cm.ps0], writes=[dst])
            k.dma("sp", wblk[:, :].rearrange("p (c m) -> p c m", c=NCH),
                  wbf["waT"].ap[:, 2 * D + g2 * 128: 2 * D + (g2 + 1) * 128]
                  .rearrange("(c p) m -> p c m", p=128),
                  reads=[wbf["waT"]], writes=[wblk])
            for kt in range(NQT):
                for c in range(NCH):
                    k.mm(cm.ps1[:, 0:128],
                         xnT[:, c * T + kt * 128: c * T + (kt + 1) * 128],
                         wblk[:, c * 128:(c + 1) * 128],
                         reads=[xnT, wblk], writes=[cm.ps1],
                         start=(c == 0), stop=False)
                k.mm(cm.ps1[:, 0:128], cm.ones1[0:1, 0:128],
                     barow[0:1, 2 * D + g2 * 128: 2 * D + (g2 + 1) * 128],
                     reads=[cm.ones1, barow], writes=[cm.ps1],
                     start=False, stop=True)
                k.op("dve", lambda e, kt=kt: e.tensor_copy(
                    v_sb[:, kt * 128:(kt + 1) * 128], cm.ps1[:, 0:128]),
                    reads=[cm.ps1], writes=[v_sb])
            for hh in range(2):
                h = g2 * 2 + hh
                r0 = hh * 64
                for qt in range(NQT):
                    nkt = qt // 4 + 1
                    L = nkt * 512
                    for j in range(nkt):
                        k.mm(cm.ps0[:, :],
                             qT[r0:r0 + 64, qt * 128:(qt + 1) * 128],
                             kT[r0:r0 + 64, j * 512:(j + 1) * 512],
                             reads=[qT, kT], writes=[cm.ps0],
                             start=True, stop=True)
                        if j == nkt - 1:
                            k.op("dve", lambda e, j=j, qt=qt: e.tensor_add(
                                Ssb[:, j * 512:(j + 1) * 512], cm.ps0[:, :],
                                cmask[:, (qt % 4) * 512:
                                      (qt % 4 + 1) * 512]),
                                reads=[cm.ps0, cmask], writes=[Ssb])
                        else:
                            k.op("dve", lambda e, j=j: e.tensor_copy(
                                Ssb[:, j * 512:(j + 1) * 512], cm.ps0[:, :]),
                                reads=[cm.ps0], writes=[Ssb])
                    klo = qt * 128 - 128
                    c0 = -klo if klo < 0 else 0
                    klo = max(klo, 0)
                    wid = qt * 128 + 128 - klo
                    k.op("dve", lambda e, klo=klo, c0=c0, wid=wid, h=h:
                         e.tensor_add(
                             Ssb[:, klo:klo + wid], Ssb[:, klo:klo + wid],
                             dall[:, h * 256 + c0: h * 256 + c0 + wid]),
                         reads=[Ssb, dall], writes=[Ssb])
                    k.op("dve", lambda e, L=L: e.tensor_reduce(
                        nmax[:, :], Ssb[:, 0:L], AX.X, ALU.max, negate=True),
                        reads=[Ssb], writes=[nmax])
                    k.op("act", lambda e, L=L: e.activation(
                        Psb[:, 0:L], Ssb[:, 0:L], AF.Exp, bias=nmax[:, 0:1],
                        accum_out=zrow[:, :]),
                        reads=[Ssb, nmax], writes=[Psb, zrow])
                    k.op("dve", lambda e: e.reciprocal(rz[:, :], zrow[:, :]),
                         reads=[zrow], writes=[rz])
                    k.op("dve", lambda e, L=L: e.tensor_scalar(
                        Psb[:, 0:L], Psb[:, 0:L], rz[:, 0:1], None,
                        ALU.mult),
                        reads=[Psb, rz], writes=[Psb])
                    for jj in range(nkt):
                        for q4 in range(4):
                            k.tr(cm.psT[:, q4 * 128:(q4 + 1) * 128],
                                 Psb[:, jj * 512 + q4 * 128:
                                     jj * 512 + (q4 + 1) * 128],
                                 cm.ident[:, :],
                                 reads=[Psb, cm.ident], writes=[cm.psT])
                        k.op("dve", lambda e: e.tensor_copy(
                            PTs[:, :], cm.psT[:, 0:512]),
                            reads=[cm.psT], writes=[PTs])
                        for q4 in range(4):
                            kt = jj * 4 + q4
                            k.mm(cm.ps2[r0:r0 + 64, 0:128],
                                 v_sb[:, kt * 128 + r0: kt * 128 + r0 + 64],
                                 PTs[:, q4 * 128:(q4 + 1) * 128],
                                 reads=[v_sb, PTs], writes=[cm.ps2],
                                 start=(kt == 0), stop=(kt == nkt * 4 - 1))
                    k.op("dve", lambda e, r0=r0, qt=qt: e.tensor_copy(
                        aog[r0:r0 + 64, qt * 128:(qt + 1) * 128],
                        cm.ps2[r0:r0 + 64, 0:128]),
                        reads=[cm.ps2], writes=[aog])
            k.dma("pool", ao_dram.ap[g2 * 128:(g2 + 1) * 128, :],
                  aog[:, :], reads=[aog], writes=[ao_dram])
        aoblk = k.sb("l3_aoblk", [128, NCH * 512], BF16, ctx=st)
        sstage = k.sb("l3_sst", [128, 512], F32, ctx=st)
        for tt in range(NTT):
            tsl = slice(tt * 512, (tt + 1) * 512)
            k.dma("sp", aoblk[:, :].rearrange("p (c m) -> p c m", c=NCH),
                  ao_dram.ap[:, tsl].rearrange("(c p) m -> p c m", p=128),
                  reads=[ao_dram], writes=[aoblk])
            for mc in range(NCH):
                for c in range(NCH):
                    k.mm(cm.ps0[:, :],
                         wpT[:, c * D + mc * 128: c * D + (mc + 1) * 128],
                         aoblk[:, c * 512:(c + 1) * 512],
                         reads=[wpT, aoblk], writes=[cm.ps0],
                         start=(c == 0), stop=False)
                k.mm(cm.ps0[:, :], bprow[0:1, mc * 128:(mc + 1) * 128],
                     cm.ones1[0:1, 0:512],
                     reads=[bprow, cm.ones1], writes=[cm.ps0],
                     start=False, stop=True)
                k.op("dve", lambda e: e.tensor_copy(sstage[:, :],
                                                    cm.ps0[:, :]),
                     reads=[cm.ps0], writes=[sstage])
                k.dma("pool", s_dram.ap[mc * 128:(mc + 1) * 128, tsl],
                      sstage[:, :], reads=[sstage], writes=[s_dram])




def _rms_scale_row(k, cm, st, n, read_bf, srow, onesc):
    sqb = k.sb("rms_sqb", [128, 512], BF16, ctx=st)
    epsc = k.sb("rms_eps", [1, 1], F32, ctx=st)
    k.op("dve", lambda e: e.memset(epsc[:, :], 0.0), writes=[epsc])
    for tt in range(n // 512):
        for c in range(NCH):
            src_ap, src_tile = read_bf(c, tt)
            k.op("act", lambda e, a=src_ap: e.activation(
                sqb[:, :], a, AF.Square), reads=[src_tile], writes=[sqb])
            k.mm(cm.psR1[0:1, :], onesc[:, 0:1], sqb[:, :],
                 reads=[onesc, sqb], writes=[cm.psR1],
                 start=(c == 0), stop=(c == NCH - 1))
        k.op("dve", lambda e, tt=tt: e.tensor_copy(
            srow[0:1, tt * 512:(tt + 1) * 512], cm.psR1[0:1, :]),
            reads=[cm.psR1], writes=[srow])
    k.op("act", lambda e, n=n: e.activation(
        srow[0:1, 0:n], srow[0:1, 0:n], AF.Sqrt, bias=epsc[0:1, 0:1]),
        reads=[srow, epsc], writes=[srow])
    k.op("dve", lambda e, n=n: e.tensor_scalar_max(
        srow[0:1, 0:n], srow[0:1, 0:n], 1e-8), reads=[srow], writes=[srow])
    k.op("dve", lambda e, n=n: e.reciprocal(srow[0:1, 0:n], srow[0:1, 0:n]),
         reads=[srow], writes=[srow])
    k.op("dve", lambda e, n=n: e.tensor_scalar_mul(
        srow[0:1, 0:n], srow[0:1, 0:n], 32.0), reads=[srow], writes=[srow])


def emit_l3_ffn(k, cm, xnT, g_dram, wbf, fin, scratch):
    G, KK, M, E1, S1, E2, HID = 64, 64, 16, 8, 2, 4, 4096
    L2N = G * M
    xtok_dram = scratch["xtok_dram"]
    seq_dram = scratch["seq_dram"]
    with k.scope() as st:
        onesc = k.sb("l3f_ones", [128, 1], BF16, ctx=st)
        k.op("dve", lambda e: e.memset(onesc[:, :], 1.0), writes=[onesc])
        g1c = k.sb("l3f_g1c", [128, NCH], F32, ctx=st)
        gs1c = k.sb("l3f_gs1c", [128, NCH], F32, ctx=st)
        g2c = k.sb("l3f_g2c", [128, NCH], F32, ctx=st)
        gs2c = k.sb("l3f_gs2c", [128, NCH], F32, ctx=st)
        load_cols(k, g1c, fin["L3_ffn_gamma1"], NCH)
        load_cols(k, gs1c, fin["L3_ffn_gamma_slot1"], NCH)
        load_cols(k, g2c, fin["L3_ffn_gamma2"], NCH)
        load_cols(k, gs2c, fin["L3_ffn_gamma_slot2"], NCH)
        comb_all = k.sb("l3f_comb", [128, (G // 2) * 16], F32, ctx=st)
        gb128 = k.sb("l3f_gb128", [128, 128], F32, ctx=st)
        egcol = k.sb("l3f_egcol", [128, 8], F32, ctx=st)

        with k.scope() as stx:
            xts = k.sb("l3f_xts", [128, D], BF16, ctx=stx)
            for kt in range(NQT):
                for c in range(NCH):
                    k.tr(cm.psT[:, 0:128],
                         xnT[:, c * T + kt * 128: c * T + (kt + 1) * 128],
                         cm.ident[:, :],
                         reads=[xnT, cm.ident], writes=[cm.psT])
                    k.op("dve", lambda e, c=c: e.tensor_copy(
                        xts[:, c * 128:(c + 1) * 128], cm.psT[:, 0:128]),
                        reads=[cm.psT], writes=[xts])
                k.dma("pool", xtok_dram.ap[kt * 128:(kt + 1) * 128, :],
                      xts[:, :], reads=[xts], writes=[xtok_dram])

        st1 = contextlib.ExitStack()
        gmT = k.sb("l3f_gmT", [128, NCH * G], BF16, ctx=st1)
        gmTf = k.sb("l3f_gmTf", [128, G], F32, ctx=st1)
        gates_g = k.sb("l3f_gates", [128, 1], F32, ctx=st1)
        for c in range(NCH):
            k.op("dve", lambda e, c=c: e.tensor_reduce(
                gmTf[:, :],
                xnT.ap[:, c * T:(c + 1) * T].rearrange(
                    "p (g kk) -> p g kk", kk=KK),
                AX.X, ALU.add), reads=[xnT], writes=[gmTf])
            k.op("dve", lambda e, c=c: e.tensor_scalar(
                gmT[:, c * G:(c + 1) * G], gmTf[:, :], 1.0 / KK, None,
                ALU.mult), reads=[gmTf], writes=[gmT])
        wgcb = k.sb("l3f_wgcb", [128, NCH], BF16, ctx=st1)
        load_cols(k, wgcb, fin["L3_ffn_Wg"].rearrange("a b -> (a b)"), NCH)
        bgrow = k.sb("l3f_bgrow", [1, 1], BF16, ctx=st1)
        load_row_bf16(k, bgrow, fin["L3_ffn_bg"], 1)
        for c in range(NCH):
            k.mm(cm.psS[0:64, 0:1], gmT[:, c * G:(c + 1) * G],
                 wgcb[:, c:c + 1],
                 reads=[gmT, wgcb], writes=[cm.psS],
                 start=(c == 0), stop=False)
        k.mm(cm.psS[0:64, 0:1], cm.ones1[0:1, 0:64], bgrow[0:1, 0:1],
             reads=[cm.ones1, bgrow], writes=[cm.psS],
             start=False, stop=True)
        k.op("act", lambda e: e.activation(gates_g[0:64, 0:1],
                                           cm.psS[0:64, 0:1], AF.Sigmoid),
             reads=[cm.psS], writes=[gates_g])
        k.op("dve", lambda e: e.tensor_scalar(
            gates_g[0:64, 0:1], gates_g[0:64, 0:1], 0.01, 1.0,
            ALU.max, ALU.min),
            reads=[gates_g], writes=[gates_g])
        k.tr(cm.psS[0:1, 0:64], gates_g[0:64, 0:1], cm.identf[0:64, 0:64],
             reads=[gates_g, cm.identf], writes=[cm.psS])
        grow = k.sb("l3f_grow", [1, 128], F32, ctx=st1)
        k.op("dve", lambda e: e.tensor_copy(grow[0:1, 0:64],
                                            cm.psS[0:1, 0:64]),
             reads=[cm.psS], writes=[grow])
        k.op("dve", lambda e: e.tensor_copy(grow[0:1, 64:128],
                                            grow[0:1, 0:64]),
             reads=[grow], writes=[grow])
        k.mm(cm.ps0[:, 0:128], cm.ones1f[0:1, 0:128], grow[0:1, :],
             reads=[cm.ones1f, grow], writes=[cm.ps0])
        k.op("dve", lambda e: e.tensor_copy(gb128[:, :], cm.ps0[:, 0:128]),
             reads=[cm.ps0], writes=[gb128])
        egrow = k.sb("l3f_egrow", [1, 8], F32, ctx=st1)
        k.dma("sp", egrow[0:1, :], fin["L3_ffn_expert_gates"].unsqueeze(0),
              writes=[egrow])
        k.mm(cm.ps0[:, 0:8], cm.ones1f[0:1, 0:128], egrow[0:1, :],
             reads=[cm.ones1f, egrow], writes=[cm.ps0])
        k.op("dve", lambda e: e.tensor_copy(egcol[:, :], cm.ps0[:, 0:8]),
             reads=[cm.ps0], writes=[egcol])

        seT = k.sb("l3f_seT", [128, NCH * 1024], BF16, ctx=st1)
        w1blk = k.sb("l3f_w1blk", [128, NCH * 128], BF16, ctx=st1)
        serow = k.sb("l3f_serow", [1, 1024], F32, ctx=st1)
        seb = k.sb("l3f_seb", [128, 1024], F32, ctx=st1)
        for m in range(M):
            for ddc in range(NCH):
                k.dma("sp",
                      w1blk[:, :].rearrange("p (c m2) -> p c m2", c=NCH),
                      wbf["L3_ffn_W_slot1"].ap[:, m * D + ddc * 128:
                                               m * D + (ddc + 1) * 128]
                      .rearrange("(c p) m2 -> p c m2", p=128),
                      reads=[wbf["L3_ffn_W_slot1"]], writes=[w1blk])
                for c in range(NCH):
                    k.mm(cm.ps0[:, 0:G],
                         w1blk[:, c * 128:(c + 1) * 128],
                         gmT[:, c * G:(c + 1) * G],
                         reads=[w1blk, gmT], writes=[cm.ps0],
                         start=(c == 0), stop=(c == NCH - 1))
                k.op("dve", lambda e, m=m, ddc=ddc: e.tensor_copy(
                    seT.ap[:, ddc * 1024 + m: ddc * 1024 + m + 1009: 16],
                    cm.ps0[:, 0:G]),
                    reads=[cm.ps0], writes=[seT])
        _rms_scale_row(k, cm, st1, 1024,
                       lambda c, tt: (seT.ap[:, c * 1024 + tt * 512:
                                             c * 1024 + (tt + 1) * 512],
                                      seT),
                       serow, onesc)
        for tt2 in range(2):
            k.mm(cm.ps0[:, :], cm.ones1f[0:1, 0:128],
                 serow[0:1, tt2 * 512:(tt2 + 1) * 512],
                 reads=[cm.ones1f, serow], writes=[cm.ps0])
            k.op("dve", lambda e, tt2=tt2: e.tensor_copy(
                seb[:, tt2 * 512:(tt2 + 1) * 512], cm.ps0[:, :]),
                reads=[cm.ps0], writes=[seb])
        for c in range(NCH):
            k.op("dve", lambda e, c=c: e.scalar_tensor_tensor(
                seT[:, c * 1024:(c + 1) * 1024],
                seT[:, c * 1024:(c + 1) * 1024],
                gs1c[:, c:c + 1], seb[:, :], ALU.mult, ALU.mult),
                reads=[seT, gs1c, seb], writes=[seT])

        xnrow = k.sb("l3f_xnrow", [1, T], F32, ctx=st1)
        _rms_scale_row(k, cm, st1, T,
                       lambda c, tt: (xnT.ap[:, c * T + tt * 512:
                                             c * T + (tt + 1) * 512], xnT),
                       xnrow, onesc)
        slotsT = k.sb("l3f_slotsT", [128, NCH * 1024], BF16, ctx=st1)
        xn1p = k.sb("l3f_xn1p", [128, NCH * 128], BF16, ctx=st1)
        nsb = k.sb("l3f_nsb", [128, 128], F32, ctx=st1)
        xtokp = k.sb("l3f_xtokp", [128, D], BF16, ctx=st1)
        comb_bf = k.sb("l3f_combbf", [128, 16], BF16, ctx=st1)
        lgm = k.sb("l3f_lgm", [128, 1], F32, ctx=st1)
        lgz = k.sb("l3f_lgz", [128, 1], F32, ctx=st1)
        lgr = k.sb("l3f_lgr", [128, 1], F32, ctx=st1)
        for gp in range(G // 2):
            t0 = gp * 128
            k.mm(cm.ps1[:, 0:128], cm.ones1f[0:1, 0:128],
                 xnrow[0:1, t0:t0 + 128],
                 reads=[cm.ones1f, xnrow], writes=[cm.ps1])
            k.op("dve", lambda e: e.tensor_copy(nsb[:, :],
                                                cm.ps1[:, 0:128]),
                 reads=[cm.ps1], writes=[nsb])
            for c in range(NCH):
                k.op("dve", lambda e, c=c, t0=t0: e.scalar_tensor_tensor(
                    xn1p[:, c * 128:(c + 1) * 128],
                    xnT[:, c * T + t0: c * T + t0 + 128],
                    g1c[:, c:c + 1], nsb[:, :], ALU.mult, ALU.mult),
                    reads=[xnT, g1c, nsb], writes=[xn1p])
            k.dma("sp", xtokp[:, :], xtok_dram.ap[t0:t0 + 128, :],
                  reads=[xtok_dram], writes=[xtokp])
            for gg in range(2):
                g = gp * 2 + gg
                r0 = gg * 64
                csl = slice(gp * 16, (gp + 1) * 16)
                for c in range(NCH):
                    k.mm(cm.psS[r0:r0 + 64, 0:16],
                         xn1p[:, c * 128 + r0: c * 128 + r0 + 64],
                         seT[:, c * 1024 + g * 16: c * 1024 + (g + 1) * 16],
                         reads=[xn1p, seT], writes=[cm.psS],
                         start=(c == 0), stop=(c == NCH - 1))
                k.op("dve", lambda e, r0=r0, csl=csl: e.tensor_copy(
                    comb_all.ap[r0:r0 + 64, csl], cm.psS[r0:r0 + 64, 0:16]),
                    reads=[cm.psS], writes=[comb_all])
                k.op("dve", lambda e, r0=r0, csl=csl: e.tensor_reduce(
                    lgm[r0:r0 + 64, :], comb_all.ap[r0:r0 + 64, csl],
                    AX.X, ALU.max, negate=True),
                    reads=[comb_all], writes=[lgm])
                k.op("act", lambda e, r0=r0, csl=csl: e.activation(
                    comb_all.ap[r0:r0 + 64, csl],
                    comb_all.ap[r0:r0 + 64, csl], AF.Exp,
                    bias=lgm[r0:r0 + 64, 0:1],
                    accum_out=lgz[r0:r0 + 64, :]),
                    reads=[comb_all, lgm], writes=[comb_all, lgz])
                k.op("dve", lambda e, r0=r0: e.reciprocal(
                    lgr[r0:r0 + 64, :], lgz[r0:r0 + 64, :]),
                    reads=[lgz], writes=[lgr])
                k.op("dve", lambda e, r0=r0, csl=csl: e.tensor_scalar(
                    comb_all.ap[r0:r0 + 64, csl],
                    comb_all.ap[r0:r0 + 64, csl],
                    lgr[r0:r0 + 64, 0:1], None, ALU.mult),
                    reads=[comb_all, lgr], writes=[comb_all])
                k.op("dve", lambda e, r0=r0, csl=csl: e.tensor_copy(
                    comb_bf.ap[r0:r0 + 64, 0:16],
                    comb_all.ap[r0:r0 + 64, csl]),
                    reads=[comb_all], writes=[comb_bf])
                for ddc in range(NCH):
                    k.mm(cm.ps2[:, 0:16],
                         xtokp.ap[r0:r0 + 64, ddc * 128:(ddc + 1) * 128],
                         comb_bf.ap[r0:r0 + 64, 0:16],
                         reads=[xtokp, comb_bf], writes=[cm.ps2],
                         start=True, stop=True)
                    k.op("dve", lambda e, ddc=ddc, g=g: e.tensor_copy(
                        slotsT.ap[:, ddc * 1024 + g:
                                  ddc * 1024 + g + 961: 64],
                        cm.ps2[:, 0:16]),
                        reads=[cm.ps2], writes=[slotsT])

        b1f = k.sb("l3f_b1f", [1, HID], BF16, ctx=st1)
        b2f = k.sb("l3f_b2f", [1, D], BF16, ctx=st1)
        h1T = k.sb("l3f_h1T", [128, 32 * 128], BF16, ctx=st1)
        w2blk = k.sb("l3f_w2blk", [128, 32 * 128], BF16, ctx=st1)
        o1tmp = k.sb("l3f_o1tmp", [128, 128], BF16, ctx=st1)
        for e_ in range(E1):
            k.dma("pool", b1f[0:1, :],
                  fin["L3_ffn_b1_first"][e_].unsqueeze(0), writes=[b1f])
            k.dma("pool", b2f[0:1, :],
                  fin["L3_ffn_b2_first"][e_].unsqueeze(0), writes=[b2f])
            for hc in range(32):
                k.dma("sp",
                      w1blk[:, :].rearrange("p (c m2) -> p c m2", c=NCH),
                      wbf["L3_ffn_W1_first"].ap[e_][:, hc * 128:
                                                    (hc + 1) * 128]
                      .rearrange("(c p) m2 -> p c m2", p=128),
                      reads=[wbf["L3_ffn_W1_first"]], writes=[w1blk])
                for c in range(NCH):
                    k.mm(cm.ps0[:, 0:128],
                         w1blk[:, c * 128:(c + 1) * 128],
                         slotsT[:, c * 1024 + e_ * 128:
                                c * 1024 + (e_ + 1) * 128],
                         reads=[w1blk, slotsT], writes=[cm.ps0],
                         start=(c == 0), stop=False)
                k.mm(cm.ps0[:, 0:128],
                     b1f[0:1, hc * 128:(hc + 1) * 128],
                     cm.ones1[0:1, 0:128],
                     reads=[b1f, cm.ones1], writes=[cm.ps0],
                     start=False, stop=True)
                k.op("act", lambda e, hc=hc: e.activation(
                    h1T[:, hc * 128:(hc + 1) * 128], cm.ps0[:, 0:128],
                    AF.Gelu), reads=[cm.ps0], writes=[h1T])
            for ddc in range(NCH):
                k.dma("sp",
                      w2blk[:, :].rearrange("p (hc m2) -> p hc m2", hc=32),
                      wbf["L3_ffn_W2_first"].ap[e_][:, ddc * 128:
                                                    (ddc + 1) * 128]
                      .rearrange("(hc p) m2 -> p hc m2", p=128),
                      reads=[wbf["L3_ffn_W2_first"]], writes=[w2blk])
                for hc in range(32):
                    k.mm(cm.ps1[:, 0:128],
                         w2blk[:, hc * 128:(hc + 1) * 128],
                         h1T[:, hc * 128:(hc + 1) * 128],
                         reads=[w2blk, h1T], writes=[cm.ps1],
                         start=(hc == 0), stop=False)
                k.mm(cm.ps1[:, 0:128],
                     b2f[0:1, ddc * 128:(ddc + 1) * 128],
                     cm.ones1[0:1, 0:128],
                     reads=[b2f, cm.ones1], writes=[cm.ps1],
                     start=False, stop=True)
                k.op("dve", lambda e, e_=e_: e.scalar_tensor_tensor(
                    o1tmp[:, :], cm.ps1[:, 0:128], egcol[:, e_:e_ + 1],
                    gb128[:, :], ALU.mult, ALU.mult),
                    reads=[cm.ps1, egcol, gb128], writes=[o1tmp])
                k.dma("pool",
                      seq_dram.ap[ddc * 128:(ddc + 1) * 128,
                                  e_ * 128:(e_ + 1) * 128],
                      o1tmp[:, :], reads=[o1tmp], writes=[seq_dram])
        k.dump("l3f_comb", comb_all.ap[:, :], [128, (G // 2) * 16],
               [comb_all])
        k.dump("l3f_slotsT", slotsT.ap[:, :], [128, NCH * 1024], [slotsT],
               dt=BF16)
        k.dump("l3f_gb", gb128.ap[:, :], [128, 128], [gb128])
        k.dump("l3f_seq", seq_dram.ap[:, :], [D, L2N], [seq_dram], dt=BF16)
        k.barrier()
        st1.close()

        xsT = k.sb("l3f_xsT", [128, NCH * L2N], BF16, ctx=st)
        comb2R = k.sb("l3f_c2R", [128, 8 * L2N], BF16, ctx=st)
        dispT = k.sb("l3f_dispT", [128, 8 * L2N], BF16, ctx=st)
        lgm2 = k.sb("l3f_lgm2", [128, 1], F32, ctx=st)
        lgz2 = k.sb("l3f_lgz2", [128, 1], F32, ctx=st)
        lgr2 = k.sb("l3f_lgr2", [128, 1], F32, ctx=st)
        stD = contextlib.ExitStack()
        se2T = k.sb("l3f_se2T", [128, NCH * L2N], BF16, ctx=stD)
        cmeanT = k.sb("l3f_cmean", [128, NCH * 256], BF16, ctx=stD)
        st2 = contextlib.ExitStack()
        xsrow = k.sb("l3f_xsrow", [1, L2N], F32, ctx=st2)
        xsb = k.sb("l3f_xsb", [128, L2N], F32, ctx=st2)
        seqb = k.sb("l3f_seqb", [128, 512], BF16, ctx=st2)

        def seq_read(c, tt):
            k.dma("sp", seqb[:, :],
                  seq_dram.ap[c * 128:(c + 1) * 128,
                              tt * 512:(tt + 1) * 512],
                  reads=[seq_dram], writes=[seqb])
            return seqb.ap[:, :], seqb
        _rms_scale_row(k, cm, st2, L2N, seq_read, xsrow, onesc)
        for tt2 in range(2):
            k.mm(cm.ps0[:, :], cm.ones1f[0:1, 0:128],
                 xsrow[0:1, tt2 * 512:(tt2 + 1) * 512],
                 reads=[cm.ones1f, xsrow], writes=[cm.ps0])
            k.op("dve", lambda e, tt2=tt2: e.tensor_copy(
                xsb[:, tt2 * 512:(tt2 + 1) * 512], cm.ps0[:, :]),
                reads=[cm.ps0], writes=[xsb])
        for c in range(NCH):
            k.dma("sp", seqb[:, :], seq_dram.ap[c * 128:(c + 1) * 128, 0:512],
                  reads=[seq_dram], writes=[seqb])
            k.op("dve", lambda e, c=c: e.scalar_tensor_tensor(
                xsT[:, c * L2N: c * L2N + 512], seqb[:, :],
                g2c[:, c:c + 1], xsb[:, 0:512], ALU.mult, ALU.mult),
                reads=[seqb, g2c, xsb], writes=[xsT])
            k.dma("sp", seqb[:, :],
                  seq_dram.ap[c * 128:(c + 1) * 128, 512:1024],
                  reads=[seq_dram], writes=[seqb])
            k.op("dve", lambda e, c=c: e.scalar_tensor_tensor(
                xsT[:, c * L2N + 512: c * L2N + 1024], seqb[:, :],
                g2c[:, c:c + 1], xsb[:, 512:1024], ALU.mult, ALU.mult),
                reads=[seqb, g2c, xsb], writes=[xsT])
        cmf = k.sb("l3f_cmf", [128, 256], F32, ctx=st2)
        for c in range(NCH):
            k.op("dve", lambda e, c=c: e.tensor_reduce(
                cmf[:, :],
                APc3(xsT, c * L2N),
                AX.X, ALU.add), reads=[xsT], writes=[cmf])
            k.op("dve", lambda e, c=c: e.tensor_scalar(
                cmeanT[:, c * 256:(c + 1) * 256], cmf[:, :], 0.25, None,
                ALU.mult), reads=[cmf], writes=[cmeanT])
        w1blk2 = k.sb("l3f_w1blk2", [128, NCH * 128], BF16, ctx=st2)
        se2row = k.sb("l3f_se2row", [1, L2N], F32, ctx=st2)
        se2b = k.sb("l3f_se2b", [128, L2N], F32, ctx=st2)
        for e2 in range(E2):
            for ddc in range(NCH):
                k.dma("sp",
                      w1blk2[:, :].rearrange("p (c m2) -> p c m2", c=NCH),
                      wbf["L3_ffn_W_slot2"].ap[:, e2 * D + ddc * 128:
                                               e2 * D + (ddc + 1) * 128]
                      .rearrange("(c p) m2 -> p c m2", p=128),
                      reads=[wbf["L3_ffn_W_slot2"]], writes=[w1blk2])
                for c in range(NCH):
                    k.mm(cm.ps0[:, 0:256],
                         w1blk2[:, c * 128:(c + 1) * 128],
                         cmeanT[:, c * 256:(c + 1) * 256],
                         reads=[w1blk2, cmeanT], writes=[cm.ps0],
                         start=(c == 0), stop=(c == NCH - 1))
                k.op("dve", lambda e, e2=e2, ddc=ddc: e.tensor_copy(
                    se2T[:, ddc * L2N + e2 * 256: ddc * L2N + (e2 + 1) * 256],
                    cm.ps0[:, 0:256]),
                    reads=[cm.ps0], writes=[se2T])
        _rms_scale_row(k, cm, st2, L2N,
                       lambda c, tt: (se2T.ap[:, c * L2N + tt * 512:
                                              c * L2N + (tt + 1) * 512],
                                      se2T),
                       se2row, onesc)
        for tt2 in range(2):
            k.mm(cm.ps0[:, :], cm.ones1f[0:1, 0:128],
                 se2row[0:1, tt2 * 512:(tt2 + 1) * 512],
                 reads=[cm.ones1f, se2row], writes=[cm.ps0])
            k.op("dve", lambda e, tt2=tt2: e.tensor_copy(
                se2b[:, tt2 * 512:(tt2 + 1) * 512], cm.ps0[:, :]),
                reads=[cm.ps0], writes=[se2b])
        for c in range(NCH):
            k.op("dve", lambda e, c=c: e.scalar_tensor_tensor(
                se2T[:, c * L2N:(c + 1) * L2N],
                se2T[:, c * L2N:(c + 1) * L2N],
                gs2c[:, c:c + 1], se2b[:, :], ALU.mult, ALU.mult),
                reads=[se2T, gs2c, se2b], writes=[se2T])
        k.barrier()
        st2.close()

        st3 = contextlib.ExitStack()
        l2x = k.sb("l3f_l2x", [128, 8 * L2N], F32, ctx=st3)
        pbf = k.sb("l3f_pbf", [128, L2N], BF16, ctx=st3)
        for lt in range(8):
            for est2 in range(2):
                for c in range(NCH):
                    k.mm(cm.ps0[:, :],
                         xsT[:, c * L2N + lt * 128: c * L2N + (lt + 1) * 128],
                         se2T[:, c * L2N + est2 * 512:
                              c * L2N + (est2 + 1) * 512],
                         reads=[xsT, se2T], writes=[cm.ps0],
                         start=(c == 0), stop=(c == NCH - 1))
                k.op("dve", lambda e, lt=lt, est2=est2: e.tensor_copy(
                    l2x[:, lt * L2N + est2 * 512:
                        lt * L2N + (est2 + 1) * 512], cm.ps0[:, :]),
                    reads=[cm.ps0], writes=[l2x])
            lsl = slice(lt * L2N, (lt + 1) * L2N)
            k.op("dve", lambda e, lsl=lsl: e.tensor_reduce(
                lgm2[:, :], l2x.ap[:, lsl], AX.X, ALU.max, negate=True),
                reads=[l2x], writes=[lgm2])
            k.op("act", lambda e, lsl=lsl: e.activation(
                pbf[:, :], l2x.ap[:, lsl], AF.Exp, bias=lgm2[:, 0:1],
                accum_out=lgz2[:, :]),
                reads=[l2x, lgm2], writes=[pbf, lgz2])
            k.op("dve", lambda e: e.reciprocal(lgr2[:, :], lgz2[:, :]),
                 reads=[lgz2], writes=[lgr2])
            k.op("dve", lambda e: e.tensor_scalar(
                pbf[:, :], pbf[:, :], lgr2[:, 0:1], None, ALU.mult),
                reads=[pbf, lgr2], writes=[pbf])
            for e128 in range(8):
                k.tr(cm.psT[:, 0:128],
                     pbf[:, e128 * 128:(e128 + 1) * 128], cm.ident[:, :],
                     reads=[pbf, cm.ident], writes=[cm.psT])
                k.op("dve", lambda e, e128=e128, lt=lt: e.tensor_copy(
                    comb2R[:, e128 * L2N + lt * 128:
                           e128 * L2N + (lt + 1) * 128], cm.psT[:, 0:128]),
                    reads=[cm.psT], writes=[comb2R])
        for est in range(8):
            for lt2 in range(2):
                for c in range(NCH):
                    k.mm(cm.ps0[:, :],
                         se2T[:, c * L2N + est * 128:
                              c * L2N + (est + 1) * 128],
                         xsT[:, c * L2N + lt2 * 512:
                             c * L2N + (lt2 + 1) * 512],
                         reads=[se2T, xsT], writes=[cm.ps0],
                         start=(c == 0), stop=(c == NCH - 1))
                k.op("dve", lambda e, est=est, lt2=lt2: e.tensor_copy(
                    l2x[:, est * L2N + lt2 * 512:
                        est * L2N + (lt2 + 1) * 512], cm.ps0[:, :]),
                    reads=[cm.ps0], writes=[l2x])
            esl = slice(est * L2N, (est + 1) * L2N)
            k.op("dve", lambda e, esl=esl: e.tensor_reduce(
                lgm2[:, :], l2x.ap[:, esl], AX.X, ALU.max, negate=True),
                reads=[l2x], writes=[lgm2])
            k.op("act", lambda e, esl=esl: e.activation(
                pbf[:, :], l2x.ap[:, esl], AF.Exp, bias=lgm2[:, 0:1],
                accum_out=lgz2[:, :]),
                reads=[l2x, lgm2], writes=[pbf, lgz2])
            k.op("dve", lambda e: e.reciprocal(lgr2[:, :], lgz2[:, :]),
                 reads=[lgz2], writes=[lgr2])
            k.op("dve", lambda e: e.tensor_scalar(
                pbf[:, :], pbf[:, :], lgr2[:, 0:1], None, ALU.mult),
                reads=[pbf, lgr2], writes=[pbf])
            for lt in range(8):
                k.tr(cm.psT[:, 0:128],
                     pbf[:, lt * 128:(lt + 1) * 128], cm.ident[:, :],
                     reads=[pbf, cm.ident], writes=[cm.psT])
                k.op("dve", lambda e, lt=lt, est=est: e.tensor_copy(
                    dispT[:, lt * L2N + est * 128:
                          lt * L2N + (est + 1) * 128], cm.psT[:, 0:128]),
                    reads=[cm.psT], writes=[dispT])
        k.dump("l3f_xsT", xsT.ap[:, :], [128, NCH * L2N], [xsT], dt=BF16)
        k.dump("l3f_cmean", cmeanT.ap[:, :], [128, NCH * 256], [cmeanT],
               dt=BF16)
        k.dump("l3f_se2T", se2T.ap[:, :], [128, NCH * L2N], [se2T], dt=BF16)
        k.dump("l3f_c2R", comb2R.ap[:, :], [128, 8 * L2N], [comb2R], dt=BF16)
        k.dump("l3f_disp", dispT.ap[:, :], [128, 8 * L2N], [dispT], dt=BF16)
        k.barrier()
        st3.close()
        stD.close()

        st4 = contextlib.ExitStack()
        xst_a = k.sb("l3f_xsta", [128, D], BF16, ctx=st4)
        xst_b = k.sb("l3f_xstb", [128, D], BF16, ctx=st4)
        slots2T = k.sb("l3f_s2T", [128, NCH * L2N], BF16, ctx=st4)
        for lt in range(8):
            xb = xst_a if lt % 2 == 0 else xst_b
            for c in range(NCH):
                k.tr(cm.psT[:, 0:128],
                     xsT[:, c * L2N + lt * 128: c * L2N + (lt + 1) * 128],
                     cm.ident[:, :], reads=[xsT, cm.ident], writes=[cm.psT])
                k.op("dve", lambda e, c=c, xb=xb: e.tensor_copy(
                    xb.ap[:, c * 128:(c + 1) * 128], cm.psT[:, 0:128]),
                    reads=[cm.psT], writes=[xb])
            k.dma("pool", xtok_dram.ap[lt * 128:(lt + 1) * 128, :],
                  xb[:, :], reads=[xb], writes=[xtok_dram])
        for ddc in range(NCH):
            for est2 in range(2):
                for lt in range(8):
                    xb = xst_a if lt % 2 == 0 else xst_b
                    k.dma("sp", xb[:, :],
                          xtok_dram.ap[lt * 128:(lt + 1) * 128, :],
                          reads=[xtok_dram], writes=[xb])
                    k.mm(cm.ps0[:, :],
                         xb.ap[:, ddc * 128:(ddc + 1) * 128],
                         dispT[:, lt * L2N + est2 * 512:
                               lt * L2N + (est2 + 1) * 512],
                         reads=[xb, dispT], writes=[cm.ps0],
                         start=(lt == 0), stop=(lt == 7))
                k.op("dve", lambda e, ddc=ddc, est2=est2: e.tensor_copy(
                    slots2T[:, ddc * L2N + est2 * 512:
                            ddc * L2N + (est2 + 1) * 512], cm.ps0[:, :]),
                    reads=[cm.ps0], writes=[slots2T])
        b1s = k.sb("l3f_b1s", [1, 128], BF16, ctx=st4)
        b2s = k.sb("l3f_b2s", [1, D], BF16, ctx=st4)
        h2T = k.sb("l3f_h2T", [128, 32 * 256], BF16, ctx=st4)
        w1b2 = k.sb("l3f_w1b2", [128, NCH * 128], BF16, ctx=st4)
        w2b2 = k.sb("l3f_w2b2", [128, 32 * 128], BF16, ctx=st4)
        o2_tok = k.sb("l3f_o2tok", [128, 8 * D], BF16, ctx=st4)
        o2tmp = k.sb("l3f_o2tmp", [128, 256], BF16, ctx=st4)
        for e2 in range(E2):
            k.dma("pool", b2s[0:1, :],
                  fin["L3_ffn_b2_second"][e2].unsqueeze(0), writes=[b2s])
            for hc in range(32):
                k.dma("pool", b1s[0:1, :],
                      fin["L3_ffn_b1_second"][e2][hc * 128:(hc + 1) * 128]
                      .unsqueeze(0), writes=[b1s])
                k.dma("sp",
                      w1b2[:, :].rearrange("p (c m2) -> p c m2", c=NCH),
                      wbf["L3_ffn_W1_second"].ap[e2][:, hc * 128:
                                                     (hc + 1) * 128]
                      .rearrange("(c p) m2 -> p c m2", p=128),
                      reads=[wbf["L3_ffn_W1_second"]], writes=[w1b2])
                for c in range(NCH):
                    k.mm(cm.ps0[:, 0:256],
                         w1b2[:, c * 128:(c + 1) * 128],
                         slots2T[:, c * L2N + e2 * 256:
                                 c * L2N + (e2 + 1) * 256],
                         reads=[w1b2, slots2T], writes=[cm.ps0],
                         start=(c == 0), stop=False)
                k.mm(cm.ps0[:, 0:256],
                     b1s[0:1, 0:128],
                     cm.ones1[0:1, 0:256],
                     reads=[b1s, cm.ones1], writes=[cm.ps0],
                     start=False, stop=True)
                k.op("act", lambda e, hc=hc: e.activation(
                    h2T[:, hc * 256:(hc + 1) * 256], cm.ps0[:, 0:256],
                    AF.Gelu), reads=[cm.ps0], writes=[h2T])
            for ddc in range(NCH):
                k.dma("sp",
                      w2b2[:, :].rearrange("p (hc m2) -> p hc m2", hc=32),
                      wbf["L3_ffn_W2_second"].ap[e2][:, ddc * 128:
                                                     (ddc + 1) * 128]
                      .rearrange("(hc p) m2 -> p hc m2", p=128),
                      reads=[wbf["L3_ffn_W2_second"]], writes=[w2b2])
                for hc in range(32):
                    k.mm(cm.ps1[:, 0:256],
                         w2b2[:, hc * 128:(hc + 1) * 128],
                         h2T[:, hc * 256:(hc + 1) * 256],
                         reads=[w2b2, h2T], writes=[cm.ps1],
                         start=(hc == 0), stop=False)
                k.mm(cm.ps1[:, 0:256],
                     b2s[0:1, ddc * 128:(ddc + 1) * 128],
                     cm.ones1[0:1, 0:256],
                     reads=[b2s, cm.ones1], writes=[cm.ps1],
                     start=False, stop=True)
                k.op("dve", lambda e: e.tensor_copy(o2tmp[:, :],
                                                    cm.ps1[:, 0:256]),
                     reads=[cm.ps1], writes=[o2tmp])
                for ns in range(2):
                    k.tr(cm.psT[:, 0:128],
                         o2tmp[:, ns * 128:(ns + 1) * 128], cm.ident[:, :],
                         reads=[o2tmp, cm.ident], writes=[cm.psT])
                    esc = e2 * 2 + ns
                    k.op("dve", lambda e, esc=esc, ddc=ddc: e.tensor_copy(
                        o2_tok[:, esc * D + ddc * 128:
                               esc * D + (ddc + 1) * 128],
                        cm.psT[:, 0:128]),
                        reads=[cm.psT], writes=[o2_tok])
        out2g = k.sb("l3f_out2g", [128, NCH * L2N], BF16, ctx=st4)
        for ddc in range(NCH):
            for lt2 in range(2):
                for esc in range(8):
                    k.mm(cm.ps0[:, :],
                         o2_tok[:, esc * D + ddc * 128:
                                esc * D + (ddc + 1) * 128],
                         comb2R[:, esc * L2N + lt2 * 512:
                                esc * L2N + (lt2 + 1) * 512],
                         reads=[o2_tok, comb2R], writes=[cm.ps0],
                         start=(esc == 0), stop=(esc == 7)),
                k.op("dve", lambda e, ddc=ddc, lt2=lt2: e.tensor_copy(
                    out2g.ap[:, ddc * L2N:(ddc + 1) * L2N]
                    .rearrange("p (g m) -> p g m", m=16)
                    [:, :, lt2 * 8:(lt2 + 1) * 8],
                    cm.ps0.ap[:, :].rearrange("p (m g) -> p g m", g=64)),
                    reads=[cm.ps0], writes=[out2g])
        k.dump("l3f_out2g", out2g.ap[:, :], [128, NCH * L2N], [out2g],
               dt=BF16)
        combT = k.sb("l3f_combT", [16, 64], BF16, ctx=st4)
        o2s = k.sb("l3f_o2s", [16, 128], BF16, ctx=st4)
        gsg = k.sb("l3f_gsg", [128, NCH * 64], BF16, ctx=st4)
        for g in range(G):
            gp, gg = g // 2, g % 2
            r0 = gg * 64
            k.tr(cm.psS[0:16, 0:64],
                 comb_all.ap[r0:r0 + 64, gp * 16:(gp + 1) * 16],
                 cm.identf[r0:r0 + 64, r0:r0 + 64],
                 reads=[comb_all, cm.identf], writes=[cm.psS])
            k.op("dve", lambda e: e.tensor_copy(combT[0:16, :],
                                                cm.psS[0:16, 0:64]),
                 reads=[cm.psS], writes=[combT])
            for ddc in range(NCH):
                k.tr(cm.psT[0:16, 0:128],
                     out2g[:, ddc * L2N + g * 16: ddc * L2N + (g + 1) * 16],
                     cm.ident[:, :],
                     reads=[out2g, cm.ident], writes=[cm.psT])
                k.op("dve", lambda e: e.tensor_copy(o2s[0:16, :],
                                                    cm.psT[0:16, 0:128]),
                     reads=[cm.psT], writes=[o2s])
                k.mm(cm.ps1[:, 0:64], o2s[0:16, :], combT[0:16, :],
                     reads=[o2s, combT], writes=[cm.ps1],
                     start=True, stop=True)
                k.op("act", lambda e, ddc=ddc: e.activation(
                    gsg[:, ddc * 64:(ddc + 1) * 64], cm.ps1[:, 0:64],
                    AF.Silu), reads=[cm.ps1], writes=[gsg])
            k.dma("pool",
                  g_dram.ap[:, g * 64:(g + 1) * 64]
                  .rearrange("(c p) m2 -> p c m2", p=128),
                  gsg[:, :].rearrange("p (c m2) -> p c m2", c=NCH),
                  reads=[gsg], writes=[g_dram])
        k.barrier()
        st4.close()


def APc3(xsT, off):
    from concourse.ap import AP as _AP
    a = xsT.ap[:, :]
    return _AP(a.tensor, a.offset + off,
               [list(a.ap[0]), [1, 64], [256, 4], [64, 4]])



def emit_combine(k, cm, xin_dram, s_dram, sG, g_dram, xout_dram):
    with k.scope() as st:
        xc = k.sb("cb_x", [128, T], F32, ctx=st)
        sc = k.sb("cb_s", [128, T], F32, ctx=st)
        gc = k.sb("cb_g", [128, T], BF16, ctx=st)
        for c in range(NCH):
            k.dma("sp", xc[:, :], xin_dram[c * 128:(c + 1) * 128, :],
                  writes=[xc])
            k.dma("sp", sc[:, :], s_dram.ap[c * 128:(c + 1) * 128, :],
                  reads=[s_dram], writes=[sc])
            k.dma("sp", gc[:, :], g_dram.ap[c * 128:(c + 1) * 128, :],
                  reads=[g_dram], writes=[gc])
            if sG is not None:
                k.op("dve", lambda e, c=c: e.tensor_copy(
                    sc[:, GPOS[0]::GSTEP][:, 0:8],
                    sG[:, c * 8:(c + 1) * 8]),
                    reads=[sG], writes=[sc])
            k.op("dve", lambda e: e.tensor_mul(sc[:, :], sc[:, :], gc[:, :]),
                 reads=[sc, gc], writes=[sc])
            k.op("dve", lambda e: e.tensor_add(sc[:, :], sc[:, :], xc[:, :]),
                 reads=[sc, xc], writes=[sc])
            k.dma("pool", xout_dram[c * 128:(c + 1) * 128, :], sc[:, :],
                  reads=[sc], writes=[])



L0_CAST = ["L0_attn_Wq", "L0_attn_Wk", "L0_attn_Wv", "L0_attn_Wo",
           "L0_ffn_router_w", "L0_ffn_w1", "L0_ffn_w2", "L0_ffn_sw1",
           "L0_ffn_sw2"]

L1_CAST = ["L1_attn_Wk", "L1_attn_Wv", "L1_attn_Wr", "L1_attn_Wo",
           "L1_attn_Fk", "L1_attn_Fv", "L1_attn_Fr", "L1_ffn_wi",
           "L1_ffn_wo", "L1_ffn_lora_A", "L1_ffn_lora_B"]

L2_CAST = ["L2_attn_Wq", "L2_attn_Wk", "L2_attn_Wv", "L2_attn_Wo",
           "L2_ffn_wg", "L2_ffn_wu", "L2_ffn_wd"]

L3_CAST = ["L3_attn_W_attn", "L3_attn_W_proj", "L3_ffn_W_slot1",
           "L3_ffn_W1_first", "L3_ffn_W2_first", "L3_ffn_W_slot2",
           "L3_ffn_W1_second", "L3_ffn_W2_second"]

LAYER_CAST = {0: L0_CAST, 1: L1_CAST, 2: L2_CAST, 3: L3_CAST}

INPUT_SPECS = None

WSHAPES = {
    "L0_attn_Wq": [D, D], "L0_attn_bq": [D], "L0_attn_Wk": [D, D],
    "L0_attn_bk": [D], "L0_attn_Wv": [D, D], "L0_attn_bv": [D],
    "L0_attn_Wo": [D, D], "L0_attn_bo": [D],
    "L0_ffn_router_w": [D, 8], "L0_ffn_router_b": [8],
    "L0_ffn_w1": [8, D, 2048], "L0_ffn_b1": [8, 2048],
    "L0_ffn_w2": [8, 2048, D], "L0_ffn_b2": [8, D],
    "L0_ffn_sw1": [2, D, 2048], "L0_ffn_sb1": [2, 2048],
    "L0_ffn_sw2": [2, 2048, D], "L0_ffn_sb2": [2, D],
    "L1_attn_ln1_g": [D], "L1_attn_ln1_b": [D],
    "L1_attn_ln2_g": [D], "L1_attn_ln2_b": [D],
    "L1_attn_att_time_mix": [1, 1, D], "L1_attn_att_combined_mix": [1, 1, D],
    "L1_attn_time_decay": [D], "L1_attn_time_first": [D],
    "L1_attn_Wk": [D, D], "L1_attn_Wv": [D, D], "L1_attn_Wr": [D, D],
    "L1_attn_Wo": [D, D],
    "L1_attn_ffn_time_mix": [1, 1, D], "L1_attn_ffn_combined_mix": [1, 1, D],
    "L1_attn_Fk": [4096, D], "L1_attn_Fv": [D, 4096], "L1_attn_Fr": [D, D],
    "L1_ffn_wi": [4096, D], "L1_ffn_bi": [4096], "L1_ffn_wo": [D, 4096],
    "L1_ffn_bo": [D], "L1_ffn_lora_A": [8, 16, D],
    "L1_ffn_lora_B": [8, 4096, 16], "L1_ffn_gate_w": [8, D],
    "L1_ffn_gate_b": [8],
    "L2_attn_Wq": [2, 128, D], "L2_attn_Wk": [128, D], "L2_attn_Wv": [128, D],
    "L2_attn_Wo": [D, 256],
    "L2_ffn_gate_w": [8, D], "L2_ffn_wg": [8, D, 256],
    "L2_ffn_wu": [8, D, 256], "L2_ffn_wd": [8, 256, D],
    "L3_attn_W_attn": [3072, D], "L3_attn_b_attn": [3072],
    "L3_attn_W_proj": [D, D], "L3_attn_b_proj": [D],
    "L3_attn_rel_emb": [32, 16],
    "L3_ffn_gamma1": [D], "L3_ffn_W_slot1": [D, 16384],
    "L3_ffn_gamma_slot1": [D],
    "L3_ffn_W1_first": [8, D, 4096], "L3_ffn_b1_first": [8, 4096],
    "L3_ffn_W2_first": [8, 4096, D], "L3_ffn_b2_first": [8, D],
    "L3_ffn_expert_gates": [8], "L3_ffn_Wg": [D, 1], "L3_ffn_bg": [1],
    "L3_ffn_gamma2": [D], "L3_ffn_W_slot2": [D, 4096],
    "L3_ffn_gamma_slot2": [D],
    "L3_ffn_W1_second": [4, D, 4096], "L3_ffn_b1_second": [4, 4096],
    "L3_ffn_W2_second": [4, 4096, D], "L3_ffn_b2_second": [4, D],
}

LAYER_INPUTS = {
    i: [n for n in WSHAPES if n.startswith("L%d_" % i)] for i in range(4)
}


def emit_layer(k, cm, lid, xin_dram, xout_dram, fin, wbf, scratch):
    s_dram = scratch["s_dram"]
    g_dram = scratch["g_dram"]
    v_scratch = scratch["v_scratch"]
    with k.scope() as lst:
        xnT = k.sb("xnT", [128, NCH * T], BF16, ctx=lst)
        routers = []
        if lid == 0:
            rw32 = k.sb("l0_rw32", [128, NCH * 8], F32, ctx=lst)
            k.dma("sp", rw32[:, :].rearrange("p (c m) -> p c m", c=NCH),
                  fin["L0_ffn_router_w"].rearrange("(c p) m -> p c m", p=128),
                  writes=[rw32])
            rb32 = k.sb("l0_rb32", [1, 8], F32, ctx=lst)
            k.dma("sp", rb32[0:1, :], fin["L0_ffn_router_b"].unsqueeze(0),
                  writes=[rb32])
            gates = k.sb("l0_gates", [128, NQT * 8], F32, ctx=lst)
            routers.append(dict(rw=rw32, rb=rb32, ne=8, out=gates))
        if lid in (1, 2):
            gw_name = "L%d_ffn_gate_w" % lid
            rw32 = k.sb("l1_rw32", [128, NCH * 8], F32, ctx=lst)
            gwsb = k.sb("l1_gwsb", [8, D], F32, ctx=lst)
            k.dma("sp", gwsb[0:8, :], fin[gw_name][:, :], writes=[gwsb])
            for c in range(NCH):
                k.tr(cm.psS[0:128, 0:8],
                     gwsb[0:8, c * 128:(c + 1) * 128], cm.identf[0:8, 0:8],
                     reads=[gwsb, cm.identf], writes=[cm.psS])
                k.op("dve", lambda e, c=c: e.tensor_copy(
                    rw32[:, c * 8:(c + 1) * 8], cm.psS[0:128, 0:8]),
                    reads=[cm.psS], writes=[rw32])
            rb32 = None
            if lid == 1:
                rb32 = k.sb("l1_rb32", [1, 8], F32, ctx=lst)
                k.dma("sp", rb32[0:1, :], fin["L1_ffn_gate_b"].unsqueeze(0),
                      writes=[rb32])
            gates = k.sb("l1_gates", [128, NQT * 8], F32, ctx=lst)
            routers.append(dict(rw=rw32, rb=rb32, ne=8, out=gates))
        emit_ln_ff(k, cm, xnT, xin_dram=xin_dram, routers=routers)
        if lid == 0:
            sG = k.sb("l0_sG", [128, NCH * 8], F32, ctx=lst)
            emit_l0_attn(k, cm, xnT, s_dram, sG, wbf, fin, v_scratch,
                         scratch["sg_dram"], scratch["png_dram"])
            emit_l0_ffn(k, cm, xnT, g_dram, wbf, fin, gates)
        elif lid == 1:
            with k.scope() as stt_:
                for nm, srcn, R, C in (
                        ("fkT", "L1_attn_Fk", 4096, D),
                        ("fvT", "L1_attn_Fv", D, 4096),
                        ("frT", "L1_attn_Fr", D, D),
                        ("wiT", "L1_ffn_wi", 4096, D),
                        ("woT", "L1_ffn_wo", D, 4096)):
                    if nm not in wbf:
                        wbf[nm] = k.dram("t_" + nm, [C, R], BF16)
                    build_transposed_dram(k, stt_, wbf[nm], wbf[srcn], R, C)
            emit_l1_attn(k, cm, xnT, s_dram, wbf, fin, scratch)
            emit_l1_ffn(k, cm, xnT, g_dram, wbf, fin, gates, scratch)
        elif lid == 2:
            emit_l2_attn(k, cm, xnT, s_dram, wbf, fin)
            emit_l2_ffn(k, cm, xnT, g_dram, wbf, fin, gates)
        elif lid == 3:
            with k.scope() as stt_:
                if "waT" not in wbf:
                    wbf["waT"] = k.dram("t_waT", [D, 3072], BF16)
                build_transposed_dram(k, stt_, wbf["waT"],
                                      wbf["L3_attn_W_attn"], 3072, D)
            emit_l3_attn(k, cm, xnT, s_dram, wbf, fin, scratch)
            emit_l3_ffn(k, cm, xnT, g_dram, wbf, fin, scratch)
        else:
            raise NotImplementedError("layer %d" % lid)
        emit_combine(k, cm, xin_dram, s_dram, sG if lid == 0 else None,
                     g_dram, xout_dram)


def build_program(layers, dump_names=()):
    nc = bass.Bass("TRN2", target_bir_lowering=False)
    ctx = contextlib.ExitStack()
    k = K(nc, ctx)
    k.dump_names = set(dump_names)

    fin = {}
    for name in CONST_SHAPES:
        shp, dt = CONST_SHAPES[name]
        fin[name] = nc.dram_tensor(name, shp, dt, kind="ExternalInput").ap()
    for lid in layers:
        for name in LAYER_INPUTS[lid]:
            fin[name] = nc.dram_tensor(name, WSHAPES[name], F32,
                                       kind="ExternalInput").ap()
    xin = nc.dram_tensor("xin", [D, T], F32, kind="ExternalInput").ap()
    xout = nc.dram_tensor("xout", [D, T], F32, kind="ExternalOutput").ap()

    cm = Common(k, fin)

    wbf = {}
    for lid in layers:
        for name in LAYER_CAST[lid]:
            wbf[name] = k.dram("bf_" + name, WSHAPES[name], BF16)
            emit_cast(k, wbf[name], fin[name])

    scratch = dict(
        s_dram=k.dram("s_scr", [D, T], F32),
        g_dram=k.dram("g_scr", [D, T], BF16),
        v_scratch=k.dram("v_scr", [T, D], BF16),
        sg_dram=k.dram("sg_scr", [16, 8, T], F32),
        png_dram=k.dram("png_scr", [128, T], BF16),
        x0_dram=k.dram("x0_scr", [D, T], BF16),
        xa_dram=k.dram("xa_scr", [D, T], BF16),
        xb_dram=k.dram("xb_scr", [D, T], BF16),
        row_dram=k.dram("row_scr", [NQT, 128], F32),
        beT_dram=k.dram("beT_scr", [8, 16, 4096], BF16),
        ao_dram=k.dram("ao_scr", [D, T], BF16),
        xtok_dram=k.dram("xtok_scr", [T, D], BF16),
        seq_dram=k.dram("seq_scr", [D, 1024], BF16),
    )

    bufs = [xin]
    for i in range(len(layers) - 1):
        bufs.append(k.dram("xmid%d" % i, [D, T], F32).ap)
    bufs.append(xout)
    for j, lid in enumerate(layers):
        emit_layer(k, cm, lid, bufs[j], bufs[j + 1], fin, wbf, scratch)
    k.barrier()
    k.finalize()
    ctx.close()
    return nc, k


LAUNCH_PLAN = [[0, 1, 2, 3]]


def kernel(**inputs):
    consts = host_constants()
    x = np.ascontiguousarray(inputs["L0_attn_x"])
    xT = [np.ascontiguousarray(x[b].T) for b in range(8)]

    for group in LAUNCH_PLAN:
        nc, k = build_program(group)
        in_maps = []
        for b in range(8):
            m = dict(consts)
            m["xin"] = xT[b]
            for lid in group:
                for name in LAYER_INPUTS[lid]:
                    m[name] = inputs[name]
            in_maps.append(m)
        res = run_bass_kernel_spmd(nc, in_maps, core_ids=list(range(8)))
        xT = [np.asarray(res.results[b]["xout"]) for b in range(8)]

    out = np.stack([xT[b].T for b in range(8)], axis=0)
    return np.ascontiguousarray(out.astype(np.float32))


# version 04
# speedup vs baseline: 1.2003x; 1.2003x over previous
import contextlib
import math
import numpy as np
import ml_dtypes
import concourse.bass as bass
import concourse.mybir as mybir
from concourse.bass_utils import run_bass_kernel_spmd

F32 = mybir.dt.float32
BF16 = mybir.dt.bfloat16
AF = mybir.ActivationFunctionType
ALU = mybir.AluOpType
AX = mybir.AxisListType

T = 4096
D = 1024
NCH = D // 128
NTT = T // 512
NQT = T // 128
NEG = -1.0e30
LN_EPS = 1e-5

ENGINES = ("pe", "dve", "act", "pool", "sp")

GPOS = [int(v) for v in np.linspace(0, T - 1, 8).astype(np.int64)]
GSTEP = GPOS[1] - GPOS[0]


class Tile:
    __slots__ = ("ap", "name", "lw", "rd")

    def __init__(self, ap, name):
        self.ap = ap
        self.name = name
        self.lw = None
        self.rd = []

    def __getitem__(self, key):
        return self.ap[key]


class K:

    NDQ = 12

    def __init__(self, nc, ctx):
        self.nc = nc
        self.ctx = ctx
        self.prog = {e: [] for e in ENGINES}
        self.chans = list(ENGINES)
        self.sem = {e: ctx.enter_context(nc.semaphore("sem_" + e))
                    for e in ENGINES}
        self.dma_rr = {}
        for e in ("sp", "pool", "act"):
            for j in range(self.NDQ):
                ch = "dq_%s_%d" % (e, j)
                self.chans.append(ch)
                self.sem[ch] = ctx.enter_context(nc.semaphore(ch))
            self.dma_rr[e] = 0
        self.cnt = {c: 0 for c in self.chans}
        self.waited = {e1: {c: 0 for c in self.chans} for e1 in ENGINES}
        self.dump_names = set()
        self.dbg = {}

    def _uniq(self, name):
        self._serial = getattr(self, "_serial", 0) + 1
        return "%s__%d" % (name, self._serial)

    def sb(self, name, shape, dt=F32, ctx=None):
        h = (ctx or self.ctx).enter_context(
            self.nc.sbuf_tensor(self._uniq(name), list(shape), dt))
        return Tile(h, name)

    def psum(self, name, shape, dt=F32, ctx=None):
        h = (ctx or self.ctx).enter_context(
            self.nc.psum_tensor(self._uniq(name), list(shape), dt))
        return Tile(h, name)

    def dram(self, name, shape, dt=F32, kind="Internal"):
        if kind == "Internal":
            name = self._uniq(name)
        h = self.nc.dram_tensor(name, list(shape), dt, kind=kind)
        return Tile(h.ap(), name)

    @contextlib.contextmanager
    def scope(self):
        with contextlib.ExitStack() as st:
            yield st
            self.barrier()

    def _need(self, eng, dep):
        if dep is None:
            return
        e2, c = dep
        if c <= 0:
            return
        if e2 == eng and eng == "pe":
            return
        if self.waited[eng][e2] >= c:
            return
        self.waited[eng][e2] = c
        sem = self.sem[e2]
        self.prog[eng].append(lambda e, s=sem, v=c: e.wait_ge(s, v))

    def _dma_issue(self, eng, fn, reads, writes):
        slot = self.dma_rr[eng] % self.NDQ
        self.dma_rr[eng] += 1
        ch = "dq_%s_%d" % (eng, slot)
        for t in reads:
            self._need(eng, t.lw)
        for t in writes:
            self._need(eng, t.lw)
            for r in t.rd:
                self._need(eng, r)
        self._need(eng, (ch, self.cnt[ch]))
        self.cnt[ch] += 16
        sem = self.sem[ch]
        th = lambda e, f=fn, s=sem: f(e).then_inc(s, 16)
        self.prog[eng].append(th)
        pos = (ch, self.cnt[ch])
        for t in reads:
            t.rd.append(pos)
        for t in writes:
            t.lw = pos
            t.rd = []
        return pos

    def op(self, eng, fn, reads=(), writes=(), inc=1):
        for t in reads:
            self._need(eng, t.lw)
        for t in writes:
            self._need(eng, t.lw)
            for r in t.rd:
                self._need(eng, r)
        if inc:
            self.cnt[eng] += inc
            sem = self.sem[eng]
            th = lambda e, f=fn, s=sem, v=inc: f(e).then_inc(s, v)
            import traceback
            frs = traceback.extract_stack(limit=5)[:-1]
            th._src = ",".join("%d" % f.lineno for f in frs)
            self.prog[eng].append(th)
            pos = (eng, self.cnt[eng])
        else:
            self.prog[eng].append(lambda e, f=fn: f(e))
            pos = (eng, self.cnt[eng] + 1)
        for t in reads:
            t.rd.append(pos)
        for t in writes:
            t.lw = pos
            t.rd = []
        return pos

    def dma(self, eng, out, in_, reads=(), writes=(), **kw):
        return self._dma_issue(
            eng, lambda e: e.dma_start(out=out, in_=in_, **kw),
            list(reads), list(writes))

    def dmat(self, out, in_, reads=(), writes=()):
        return self._dma_issue(
            "sp", lambda e: e.dma_start_transpose(out, in_),
            list(reads), list(writes))

    def mm(self, ps_ap, lhsT_ap, rhs_ap, reads=(), writes=(), start=True,
           stop=True, inc=None, skip=False):
        if inc is None:
            inc = 1
        return self.op(
            "pe",
            lambda e: e.matmul(ps_ap, lhsT_ap, rhs_ap, start=start, stop=stop,
                               skip_group_check=skip),
            reads=reads, writes=writes, inc=inc)

    def tr(self, ps_ap, in_ap, ident_ap, reads=(), writes=()):
        return self.op(
            "pe", lambda e: e.transpose(ps_ap, in_ap, ident_ap),
            reads=reads, writes=writes, inc=1)

    def barrier(self):
        snap = dict(self.cnt)
        for e1 in ENGINES:
            for c in self.chans:
                if c != e1:
                    self._need(e1, (c, snap[c]))

    def dump(self, name, src_ap, shape, reads, dt=F32):
        if name not in self.dump_names:
            return
        d = self.dram("dbg_" + name, list(shape), dt, kind="ExternalOutput")
        self.dbg[name] = d
        self.dma("pool", d.ap[:, :], src_ap, reads=reads, writes=[d])

    def finalize(self):
        for e1 in ENGINES:
            for c in self.chans:
                assert self.waited[e1][c] <= self.cnt[c], (
                    "dangling wait on %s@%d > %d (inc=0 op at program end?)"
                    % (c, self.waited[e1][c], self.cnt[c]))
        block = self.ctx.enter_context(self.nc.Block())
        section = dict(pe=block.tensor, dve=block.vector, act=block.scalar,
                       pool=block.gpsimd, sp=block.sync)
        for eng in ENGINES:
            thunks = self.prog[eng]

            def body(e, thunks=thunks):
                for t in thunks:
                    try:
                        t(e)
                    except Exception as ex:
                        loc = getattr(t, "_src", None)
                        raise RuntimeError(
                            "emission failed at %s: %s" % (loc, ex)) from ex
            section[eng](body)



def _rel_bucket_np(dist):
    num_buckets, max_distance = 32, 128
    rp = np.asarray(dist, dtype=np.int64)
    max_exact = num_buckets // 2
    is_small = rp < max_exact
    rp_f = rp.astype(np.float64)
    with np.errstate(divide="ignore"):
        rp_large = max_exact + (
            np.log(rp_f / max_exact + 1e-10)
            / math.log(max_distance / max_exact)
            * (num_buckets - max_exact)).astype(np.int32)
    rp_large = np.minimum(rp_large, num_buckets - 1)
    return np.where(is_small, rp, rp_large)


def host_constants():
    c = {}
    c["c_ident"] = np.eye(128, dtype=ml_dtypes.bfloat16)
    c["c_identf"] = np.eye(128, dtype=np.float32)

    masks = np.zeros((32, 128, 128), dtype=np.float32)
    base = np.full((128, 128), NEG, dtype=np.float32)
    base[0:64, 0:64] = 0.0
    base[64:128, 64:128] = 0.0
    for i in range(32):
        m = base.copy()
        for g in GPOS:
            if i * 128 <= g < (i + 1) * 128:
                m[:, g - i * 128] = NEG
        masks[i] = m
    c["c_l0_mask"] = masks

    cm = np.zeros((4, 128, 512), dtype=np.float32)
    for r in range(4):
        q = r * 128 + np.arange(128)[:, None]
        kcol = np.arange(512)[None, :]
        cm[r] = np.where(kcol <= q, 0.0, NEG)
    c["c_causal"] = cm

    hs = 128
    pos = np.arange(0, hs, 2, dtype=np.float64)
    theta = 10000.0 ** (-2.0 * pos / hs)
    ang = np.arange(T, dtype=np.float64)[:, None] * theta[None, :]
    cos = np.cos(ang).T
    sin = np.sin(ang).T
    cfull = np.empty((hs, T), dtype=np.float32)
    sfull = np.empty((hs, T), dtype=np.float32)
    cfull[0::2] = cos
    cfull[1::2] = cos
    sfull[0::2] = -sin
    sfull[1::2] = sin
    c["c_rope_cos"] = cfull
    c["c_rope_sin"] = sfull

    P = np.zeros((128, 128), dtype=np.float32)
    for i in range(64):
        P[2 * i + 1, 2 * i] = 1
        P[2 * i, 2 * i + 1] = 1
    c["c_rope_swap"] = P

    M = np.zeros((384, 32), dtype=np.float32)
    for i in range(384):
        dist = i - 128
        if dist >= 0:
            b = int(_rel_bucket_np(dist))
            M[i, b] += 1.0
            M[i, 31] -= 1.0
    c["c_relmap"] = np.ascontiguousarray(M[::-1, :].T)
    R = np.zeros((128, 128), dtype=np.float32)
    for i in range(128):
        R[127 - i, i] = 1.0
    c["c_prev"] = R
    return c


CONST_SHAPES = {
    "c_ident": ([128, 128], BF16),
    "c_identf": ([128, 128], F32),
    "c_l0_mask": ([32, 128, 128], F32),
    "c_causal": ([4, 128, 512], F32),
    "c_rope_cos": ([128, T], F32),
    "c_rope_sin": ([128, T], F32),
    "c_rope_swap": ([128, 128], F32),
    "c_relmap": ([32, 384], F32),
    "c_prev": ([128, 128], F32),
}



class Common:
    def __init__(self, k, fin):
        self.ps0 = k.psum("ps0", [128, 512], F32)
        self.ps1 = k.psum("ps1", [128, 512], F32)
        self.ps2 = k.psum("ps2", [128, 512], F32)
        self.psS = k.psum("psS", [128, 512], F32)
        self.psT = k.psum("psT", [128, 1024], BF16)
        self.psR1 = k.psum("psR1", [1, 512], F32)
        self.psR2 = k.psum("psR2", [1, 512], F32)
        self.ident = k.sb("ident", [128, 128], BF16)
        self.identf = k.sb("identf", [128, 128], F32)
        self.ones1 = k.sb("ones1", [1, 512], BF16)
        self.ones1f = k.sb("ones1f", [1, 512], F32)
        k.dma("sp", self.ident[:, :], fin["c_ident"][:, :],
              writes=[self.ident])
        k.dma("sp", self.identf[:, :], fin["c_identf"][:, :],
              writes=[self.identf])
        k.op("dve", lambda e: e.memset(self.ones1[:, :], 1.0),
             writes=[self.ones1])
        k.op("dve", lambda e: e.memset(self.ones1f[:, :], 1.0),
             writes=[self.ones1f])


def load_row_bf16(k, row_tile, src_1d_f32, n):
    k.dma("pool", row_tile[0:1, 0:n], src_1d_f32[0:n].unsqueeze(0),
          writes=[row_tile])


def emit_cast(k, dst_tile, src_ap, max_bytes=8 << 20):
    shape = src_ap.shape
    rows = int(shape[0])
    row_bytes = int(np.prod(shape[1:])) * 4 if len(shape) > 1 else 4
    step = max(1, max_bytes // max(row_bytes, 1))
    r = 0
    while r < rows:
        r1 = min(rows, r + step)
        k.dma("pool", dst_tile.ap[r:r1], src_ap[r:r1], writes=[dst_tile])
        r = r1


def emit_ln_ff(k, cm, xnT, xin_dram=None, x_sb=None, gamma_col=None,
               beta_col=None, nt=T, routers=(), in_bf16=False,
               dst0=None, on_chunk=None):
    if dst0 is None:
        dst0 = lambda c: c * nt
    """Feature-major layernorm over d: input either DRAM [1024, nt] f32 or
    SBUF tile [128, NCH*nt] f32; output xnT [128, NCH*nt] bf16."""
    ntt = nt // 512
    with k.scope() as st:
        xbf = k.sb("ln_xbf", [128, 512], BF16, ctx=st)
        sq = k.sb("ln_sq", [128, 512], BF16, ctx=st)
        srow = k.sb("ln_srow", [1, nt], F32, ctx=st)
        s2row = k.sb("ln_s2row", [1, nt], F32, ctx=st)
        mean_r, rstd_r = srow, s2row
        mb = k.sb("ln_mb", [128, nt], F32, ctx=st)
        rb = k.sb("ln_rb", [128, nt], F32, ctx=st)
        xc = k.sb("ln_xc", [128, nt], F32, ctx=st) if x_sb is not None \
            else mb
        onesc = k.sb("ln_ones", [128, 1], BF16, ctx=st)
        k.op("dve", lambda e: e.memset(onesc[:, :], 1.0), writes=[onesc])
        epsc = k.sb("ln_eps", [1, 1], F32, ctx=st)
        k.op("dve", lambda e: e.memset(epsc[:, :], float(LN_EPS)),
             writes=[epsc])

        for tt in range(ntt):
            for c in range(NCH):
                if x_sb is None:
                    k.dma("pool" if not in_bf16 else "sp", xbf[:, :],
                          xin_dram[c * 128:(c + 1) * 128,
                                   tt * 512:(tt + 1) * 512],
                          writes=[xbf])
                else:
                    k.op("dve", lambda e, c=c, tt=tt: e.tensor_copy(
                        xbf[:, :],
                        x_sb[:, c * nt + tt * 512: c * nt + (tt + 1) * 512]),
                        reads=[x_sb], writes=[xbf])
                k.op("act", lambda e: e.activation(sq[:, :], xbf[:, :],
                                                   AF.Square),
                     reads=[xbf], writes=[sq])
                k.mm(cm.psR1[0:1, :], onesc[:, 0:1], xbf[:, :],
                     reads=[onesc, xbf], writes=[cm.psR1],
                     start=(c == 0), stop=(c == NCH - 1))
                k.mm(cm.psR2[0:1, :], onesc[:, 0:1], sq[:, :],
                     reads=[onesc, sq], writes=[cm.psR2],
                     start=(c == 0), stop=(c == NCH - 1))
            k.op("dve", lambda e, tt=tt: e.tensor_copy(
                srow[0:1, tt * 512:(tt + 1) * 512], cm.psR1[0:1, :]),
                reads=[cm.psR1], writes=[srow])
            k.op("dve", lambda e, tt=tt: e.tensor_copy(
                s2row[0:1, tt * 512:(tt + 1) * 512], cm.psR2[0:1, :]),
                reads=[cm.psR2], writes=[s2row])

        k.op("dve", lambda e: e.tensor_scalar_mul(
            srow[0:1, :], srow[0:1, :], 1.0 / D),
            reads=[srow], writes=[srow])
        k.op("dve", lambda e: e.tensor_scalar_mul(
            s2row[0:1, :], s2row[0:1, :], 1.0 / D),
            reads=[s2row], writes=[s2row])
        k.op("dve", lambda e: e.tensor_mul(
            xc[0:1, :], srow[0:1, :], srow[0:1, :]),
            reads=[srow], writes=[xc])
        k.op("dve", lambda e: e.tensor_sub(
            s2row[0:1, :], s2row[0:1, :], xc[0:1, :]),
            reads=[xc, s2row], writes=[s2row])
        k.op("act", lambda e: e.activation(
            s2row[0:1, :], s2row[0:1, :], AF.Sqrt, bias=epsc[0:1, 0:1]),
            reads=[s2row, epsc], writes=[s2row])
        k.op("dve", lambda e: e.reciprocal(s2row[0:1, :], s2row[0:1, :]),
             reads=[s2row], writes=[s2row])

        for tt in range(ntt):
            sl = slice(tt * 512, (tt + 1) * 512)
            k.mm(cm.ps0[:, 0:512], cm.ones1f[0:1, 0:128], mean_r[0:1, sl],
                 reads=[cm.ones1f, mean_r], writes=[cm.ps0])
            k.op("dve", lambda e, sl=sl: e.tensor_copy(
                mb[:, sl], cm.ps0[:, 0:512]), reads=[cm.ps0], writes=[mb])
            k.mm(cm.ps1[:, 0:512], cm.ones1f[0:1, 0:128], rstd_r[0:1, sl],
                 reads=[cm.ones1f, rstd_r], writes=[cm.ps1])
            k.op("dve", lambda e, sl=sl: e.tensor_copy(
                rb[:, sl], cm.ps1[:, 0:512]), reads=[cm.ps1], writes=[rb])

        if x_sb is None:
            xnf = k.sb("ln_xnf", [128, 512], F32, ctx=st)
            for tt in range(ntt):
                tsl = slice(tt * 512, (tt + 1) * 512)
                for c in range(NCH):
                    k.dma("pool" if in_bf16 else "sp", xnf[:, :],
                          xin_dram[c * 128:(c + 1) * 128, tsl], writes=[xnf])
                    k.op("dve", lambda e, tsl=tsl: e.tensor_sub(
                        xnf[:, :], xnf[:, :], mb[:, tsl]),
                        reads=[xnf, mb], writes=[xnf])
                    k.op("dve", lambda e, tsl=tsl: e.tensor_mul(
                        xnf[:, :], xnf[:, :], rb[:, tsl]),
                        reads=[xnf, rb], writes=[xnf])
                    if gamma_col is None:
                        k.op("dve", lambda e, c=c, tt=tt: e.tensor_copy(
                            xnT[:, dst0(c) + tt * 512:
                                dst0(c) + (tt + 1) * 512],
                            xnf[:, :]),
                            reads=[xnf], writes=[xnT])
                    else:
                        k.op("dve", lambda e, c=c, tt=tt: e.tensor_scalar(
                            xnT[:, dst0(c) + tt * 512:
                                dst0(c) + (tt + 1) * 512],
                            xnf[:, :], gamma_col[:, c:c + 1],
                            beta_col[:, c:c + 1], ALU.mult, ALU.add),
                            reads=[xnf, gamma_col, beta_col], writes=[xnT])
                    for ri, r in enumerate(routers):
                        ne = r["ne"]
                        for sub in range(4):
                            k.mm(cm.psS[:, ri * 32 + sub * ne:
                                        ri * 32 + (sub + 1) * ne],
                                 xnf[:, sub * 128:(sub + 1) * 128],
                                 r["rw"].ap[:, c * ne:(c + 1) * ne],
                                 reads=[xnf, r["rw"]], writes=[cm.psS],
                                 start=(c == 0 and ri == 0 and sub == 0),
                                 stop=False, skip=True)
                for ri, r in enumerate(routers):
                    ne = r["ne"]
                    for sub in range(4):
                        psl = slice(ri * 32 + sub * ne,
                                    ri * 32 + (sub + 1) * ne)
                        if r.get("rb") is not None:
                            k.mm(cm.psS[:, psl], cm.ones1f[0:1, 0:128],
                                 r["rb"].ap[0:1, 0:ne],
                                 reads=[cm.ones1f, r["rb"]],
                                 writes=[cm.psS],
                                 start=False, stop=True, skip=True)
                        tsg = tt * 4 + sub
                        k.op("dve", lambda e, psl=psl, tsg=tsg, r=r, ne=ne:
                             e.tensor_copy(
                                 r["out"].ap[:, tsg * ne:(tsg + 1) * ne],
                                 cm.psS[:, psl]),
                             reads=[cm.psS], writes=[r["out"]])
        else:
            for c in range(NCH):
                k.op("dve", lambda e, c=c: e.tensor_sub(
                    xc[:, :], x_sb[:, c * nt:(c + 1) * nt], mb[:, :]),
                    reads=[x_sb, mb], writes=[xc])
                if gamma_col is None:
                    k.op("dve", lambda e, c=c: e.tensor_mul(
                        xnT[:, dst0(c):dst0(c) + nt], xc[:, :], rb[:, :]),
                        reads=[xc, rb], writes=[xnT])
                else:
                    k.op("dve", lambda e: e.tensor_mul(xc[:, :], xc[:, :],
                                                       rb[:, :]),
                         reads=[xc, rb], writes=[xc])
                    k.op("dve", lambda e, c=c: e.tensor_scalar(
                        xnT[:, dst0(c):dst0(c) + nt], xc[:, :],
                        gamma_col[:, c:c + 1], beta_col[:, c:c + 1],
                        ALU.mult, ALU.add),
                        reads=[xc, gamma_col, beta_col], writes=[xnT])
                if on_chunk is not None:
                    on_chunk(c)




def load_cols(k, out_cols, src_1d, nchunk):
    for c in range(nchunk):
        k.dma("pool", out_cols.ap[:, c:c + 1],
              src_1d[c * 128:(c + 1) * 128].unsqueeze(1), writes=[out_cols])


def build_transposed_dram(k, st, dst_dram, src_bf, R, C):
    tmp = k.sb("btd_tmp_%s" % dst_dram.name, [128, R], BF16, ctx=st)
    for cc in range(C // 128):
        k.dmat(tmp[:, :], src_bf.ap[0:R, cc * 128:(cc + 1) * 128],
               reads=[src_bf], writes=[tmp])
        k.dma("pool", dst_dram.ap[cc * 128:(cc + 1) * 128, 0:R], tmp[:, :],
              reads=[tmp], writes=[dst_dram])



def emit_l0_attn(k, cm, xnT, s_dram, sG, wbf, fin, v_scratch, sg_dram,
                 png_dram):
    with k.scope() as st:
        wq = k.sb("l0_wq", [128, NCH * D], BF16, ctx=st)
        wk = k.sb("l0_wk", [128, NCH * D], BF16, ctx=st)
        wv = k.sb("l0_wv", [128, NCH * D], BF16, ctx=st)
        wo = k.sb("l0_wo", [128, NCH * D], BF16, ctx=st)
        for (w, nm) in ((wq, "L0_attn_Wq"), (wk, "L0_attn_Wk"),
                        (wv, "L0_attn_Wv"), (wo, "L0_attn_Wo")):
            k.dma("sp", w[:, :].rearrange("p (c m) -> p c m", c=NCH),
                  wbf[nm].ap.rearrange("(c p) m -> p c m", p=128),
                  reads=[wbf[nm]], writes=[w])
        bq = k.sb("l0_bq", [1, D], BF16, ctx=st)
        bk = k.sb("l0_bk", [1, D], BF16, ctx=st)
        bv = k.sb("l0_bv", [1, D], BF16, ctx=st)
        bo = k.sb("l0_bo", [1, D], BF16, ctx=st)
        load_row_bf16(k, bq, fin["L0_attn_bq"], D)
        load_row_bf16(k, bk, fin["L0_attn_bk"], D)
        load_row_bf16(k, bv, fin["L0_attn_bv"], D)
        load_row_bf16(k, bo, fin["L0_attn_bo"], D)
        onesw = k.sb("l0_onesw", [1, 512], BF16, ctx=st)
        k.op("dve", lambda e: e.memset(onesw[0:1, :], 1.0), writes=[onesw])

        def proj_ff(dst, dst_sl, w, b, src, src_cols, n, scale=None,
                    mc_range=range(NCH)):
            for mc in mc_range:
                for c in range(NCH):
                    k.mm(cm.ps0[:, 0:n],
                         w[:, c * D + mc * 128: c * D + (mc + 1) * 128],
                         src.ap[:, src_cols(c)],
                         reads=[w, src], writes=[cm.ps0],
                         start=(c == 0), stop=False)
                k.mm(cm.ps0[:, 0:n], b[0:1, mc * 128:(mc + 1) * 128],
                     onesw[0:1, 0:n],
                     reads=[b, onesw], writes=[cm.ps0],
                     start=False, stop=True)
                k.op("act", lambda e, mc=mc: e.activation(
                    dst.ap[:, dst_sl(mc)], cm.ps0[:, 0:n], AF.Copy,
                    scale=(scale or 1.0)),
                    reads=[cm.ps0], writes=[dst])

        xnG = k.sb("l0_xnG", [128, NCH * 8], BF16, ctx=st)
        qG = k.sb("l0_qG", [128, NCH * 8], BF16, ctx=st)
        kG = k.sb("l0_kG", [128, NCH * 8], BF16, ctx=st)
        vGf = k.sb("l0_vGf", [128, NCH * 8], BF16, ctx=st)
        vG = k.sb("l0_vG", [8, D], BF16, ctx=st)
        for c in range(NCH):
            k.op("dve", lambda e, c=c: e.tensor_copy(
                xnG[:, c * 8:(c + 1) * 8],
                xnT.ap[:, c * T:(c + 1) * T][:, GPOS[0]::GSTEP]),
                reads=[xnT], writes=[xnG])
        for (dst, w, b, scale) in ((qG, wq, bq, 0.125), (kG, wk, bk, None),
                                   (vGf, wv, bv, None)):
            proj_ff(dst, lambda mc: slice(mc * 8, (mc + 1) * 8), w, b,
                    xnG, lambda c: slice(c * 8, (c + 1) * 8), 8, scale)
        for c in range(NCH):
            k.tr(cm.psT[0:8, 0:128], vGf[:, c * 8:(c + 1) * 8],
                 cm.ident[:, :], reads=[vGf, cm.ident], writes=[cm.psT])
            k.op("dve", lambda e, c=c: e.tensor_copy(
                vG[0:8, c * 128:(c + 1) * 128], cm.psT[0:8, 0:128]),
                reads=[cm.psT], writes=[vG])

        msk = k.sb("l0_msk", [128, 128], F32, ctx=st)
        sgst = k.sb("l0_sgst", [8, 512], F32, ctx=st)
        sstage = k.sb("l0_sst", [128, 512], F32, ctx=st)
        st2 = contextlib.ExitStack()
        qT = k.sb("l0_qT", [128, NCH * 512], BF16, ctx=st2)
        kT = k.sb("l0_kT", [128, NCH * 512], BF16, ctx=st2)
        vg = k.sb("l0_vg", [128, 4 * D], BF16, ctx=st2)
        aoT = k.sb("l0_aoT", [128, NCH * 512], BF16, ctx=st2)
        Ssb = k.sb("l0_S", [128, 136], F32, ctx=st2)
        Psb = k.sb("l0_P", [128, 136], BF16, ctx=st2)
        PT = k.sb("l0_PT", [128, 128], BF16, ctx=st2)
        PTg = k.sb("l0_PTg", [8, 128], BF16, ctx=st2)
        nmax = k.sb("l0_nmax", [128, 1], F32, ctx=st2)
        zrow = k.sb("l0_z", [128, 1], F32, ctx=st2)
        rz = k.sb("l0_rz", [128, 1], F32, ctx=st2)

        for g in range(NTT):
            gsl = slice(g * 512, (g + 1) * 512)
            proj_ff(qT, lambda mc: slice(mc * 512, (mc + 1) * 512), wq, bq,
                    xnT, lambda c, g=g: slice(c * T + g * 512,
                                              c * T + (g + 1) * 512),
                    512, 0.125)
            proj_ff(kT, lambda mc: slice(mc * 512, (mc + 1) * 512), wk, bk,
                    xnT, lambda c, g=g: slice(c * T + g * 512,
                                              c * T + (g + 1) * 512),
                    512, None)
            for ts in range(4):
                t0 = g * 512 + ts * 128
                for mt in range(2):
                    for c in range(NCH):
                        k.mm(cm.ps1[:, :],
                             xnT[:, c * T + t0: c * T + t0 + 128],
                             wv[:, c * D + mt * 512: c * D + (mt + 1) * 512],
                             reads=[xnT, wv], writes=[cm.ps1],
                             start=(c == 0), stop=False)
                    k.mm(cm.ps1[:, :], cm.ones1[0:1, 0:128],
                         bv[0:1, mt * 512:(mt + 1) * 512],
                         reads=[cm.ones1, bv], writes=[cm.ps1],
                         start=False, stop=True)
                    k.op("dve", lambda e, ts=ts, mt=mt: e.tensor_copy(
                        vg[:, ts * D + mt * 512: ts * D + (mt + 1) * 512],
                        cm.ps1[:, :]),
                        reads=[cm.ps1], writes=[vg])
            k.dma("pool",
                  v_scratch.ap[g * 512:(g + 1) * 512, :].rearrange(
                      "(ts p) m -> p ts m", p=128),
                  vg[:, :].rearrange("p (ts m) -> p ts m", ts=4),
                  reads=[vg], writes=[v_scratch])

            for h in range(16):
                c0, r0 = h // 2, (h % 2) * 64
                k.mm(cm.ps2[0:8, :], qG[r0:r0 + 64, c0 * 8:(c0 + 1) * 8],
                     kT[r0:r0 + 64, c0 * 512:(c0 + 1) * 512],
                     reads=[qG, kT], writes=[cm.ps2])
                k.op("dve", lambda e: e.tensor_copy(
                    sgst[0:8, :], cm.ps2[0:8, :]),
                    reads=[cm.ps2], writes=[sgst])
                k.dma("pool", sg_dram.ap[h, :, g * 512:(g + 1) * 512],
                      sgst[0:8, :], reads=[sgst], writes=[sg_dram])

            for qt in range(4):
                qtile = g * 4 + qt
                k.dma("sp", msk[:, :], fin["c_l0_mask"][qtile, :, :],
                      writes=[msk])
                for h in range(16):
                    c0, r0 = h // 2, (h % 2) * 64
                    hs = slice(c0 * 512 + qt * 128, c0 * 512 + (qt + 1) * 128)
                    k.mm(cm.psS[:, 0:128], qT[r0:r0 + 64, hs],
                         kT[r0:r0 + 64, hs],
                         reads=[qT, kT], writes=[cm.psS])
                    k.mm(cm.psS[:, 128:136], qT[r0:r0 + 64, hs],
                         kG[r0:r0 + 64, c0 * 8:(c0 + 1) * 8],
                         reads=[qT, kG], writes=[cm.psS])
                    k.op("dve", lambda e: e.tensor_add(
                        Ssb[:, 0:128], cm.psS[:, 0:128], msk[:, :]),
                        reads=[cm.psS, msk], writes=[Ssb])
                    k.op("dve", lambda e: e.tensor_copy(
                        Ssb[:, 128:136], cm.psS[:, 128:136]),
                        reads=[cm.psS], writes=[Ssb])
                    k.op("dve", lambda e: e.tensor_reduce(
                        nmax[:, :], Ssb[:, :], AX.X, ALU.max, negate=True),
                        reads=[Ssb], writes=[nmax])
                    k.op("act", lambda e: e.activation(
                        Psb[:, :], Ssb[:, :], AF.Exp, bias=nmax[:, 0:1],
                        accum_out=zrow[:, :]),
                        reads=[Ssb, nmax], writes=[Psb, zrow])
                    k.op("dve", lambda e: e.reciprocal(rz[:, :], zrow[:, :]),
                         reads=[zrow], writes=[rz])
                    k.op("pool", lambda e: e.tensor_scalar_mul(
                        Psb[:, :], Psb[:, :], rz[:, 0:1]),
                        reads=[Psb, rz], writes=[Psb])
                    k.tr(cm.psT[:, 0:128], Psb[:, 0:128], cm.ident[:, :],
                         reads=[Psb, cm.ident], writes=[cm.psT])
                    k.tr(cm.psT[0:8, 128:256], Psb[:, 128:136],
                         cm.ident[:, :],
                         reads=[Psb, cm.ident], writes=[cm.psT])
                    k.op("dve", lambda e: e.tensor_copy(
                        PT[:, :], cm.psT[:, 0:128]),
                        reads=[cm.psT], writes=[PT])
                    k.op("dve", lambda e: e.tensor_copy(
                        PTg[0:8, :], cm.psT[0:8, 128:256]),
                        reads=[cm.psT], writes=[PTg])
                    k.mm(cm.ps2[r0:r0 + 64, 0:128],
                         vg[:, qt * D + h * 64: qt * D + (h + 1) * 64],
                         PT[:, :], reads=[vg, PT], writes=[cm.ps2],
                         start=True, stop=False)
                    k.mm(cm.ps2[r0:r0 + 64, 0:128],
                         vG[0:8, h * 64:(h + 1) * 64], PTg[0:8, :],
                         reads=[vG, PTg], writes=[cm.ps2],
                         start=False, stop=True)
                    k.op("dve", lambda e, c0=c0, r0=r0, qt=qt: e.tensor_copy(
                        aoT[r0:r0 + 64,
                            c0 * 512 + qt * 128: c0 * 512 + (qt + 1) * 128],
                        cm.ps2[r0:r0 + 64, 0:128]),
                        reads=[cm.ps2], writes=[aoT])

            for mc in range(NCH):
                for c in range(NCH):
                    k.mm(cm.ps0[:, :],
                         wo[:, c * D + mc * 128: c * D + (mc + 1) * 128],
                         aoT[:, c * 512:(c + 1) * 512],
                         reads=[wo, aoT], writes=[cm.ps0],
                         start=(c == 0), stop=False)
                k.mm(cm.ps0[:, :], bo[0:1, mc * 128:(mc + 1) * 128],
                     onesw[0:1, 0:512],
                     reads=[bo, onesw], writes=[cm.ps0],
                     start=False, stop=True)
                k.op("dve", lambda e, mc=mc: e.tensor_copy(
                    sstage[:, :], cm.ps0[:, :]),
                    reads=[cm.ps0], writes=[sstage])
                k.dma("pool", s_dram.ap[mc * 128:(mc + 1) * 128, gsl],
                      sstage[:, :], reads=[sstage], writes=[s_dram])

        k.barrier()
        st2.close()
        SgP = k.sb("l0_SgP", [128, T], F32, ctx=st)
        PnG = k.sb("l0_PnG", [128, T], BF16, ctx=st)
        gng = k.sb("l0_gng", [128, 1], F32, ctx=st)
        gz = k.sb("l0_gz", [128, 1], F32, ctx=st)
        grz = k.sb("l0_grz", [128, 1], F32, ctx=st)
        vtile = k.sb("l0_vt", [128, D], BF16, ctx=st)
        PGT = k.sb("l0_PGT", [128, 128], BF16, ctx=st)
        aoG = k.sb("l0_aoG", [128, 64], BF16, ctx=st)
        psAVG = k.psum("l0_psavg", [128, 512], F32, ctx=st)
        k.dma("sp", SgP[:, :],
              sg_dram.ap.rearrange("h g t -> (h g) t"),
              reads=[sg_dram], writes=[SgP])
        k.op("dve", lambda e: e.tensor_reduce(
            gng[:, :], SgP[:, :], AX.X, ALU.max, negate=True),
            reads=[SgP], writes=[gng])
        k.op("act", lambda e: e.activation(
            PnG[:, :], SgP[:, :], AF.Exp, bias=gng[:, 0:1],
            accum_out=gz[:, :]),
            reads=[SgP, gng], writes=[PnG, gz])
        k.op("dve", lambda e: e.reciprocal(grz[:, :], gz[:, :]),
             reads=[gz], writes=[grz])
        k.op("pool", lambda e: e.tensor_scalar_mul(
            PnG[:, :], PnG[:, :], grz[:, 0:1]),
            reads=[PnG, grz], writes=[PnG])
        k.dma("pool", png_dram.ap[:, :], PnG[:, :],
              reads=[PnG], writes=[png_dram])
        for kt in range(NQT):
            k.dma("sp", vtile[:, :], v_scratch.ap[kt * 128:(kt + 1) * 128, :],
                  reads=[v_scratch], writes=[vtile])
            k.dmat(PGT[:, :], png_dram.ap[:, kt * 128:(kt + 1) * 128],
                   reads=[png_dram], writes=[PGT])
            for h in range(16):
                c0, r0 = h // 2, (h % 2) * 64
                k.mm(psAVG[r0:r0 + 64, c0 * 8:(c0 + 1) * 8],
                     vtile[:, h * 64:(h + 1) * 64],
                     PGT[:, h * 8:(h + 1) * 8],
                     reads=[vtile, PGT], writes=[psAVG],
                     start=(kt == 0 and h == 0), stop=(kt == NQT - 1),
                     skip=True, inc=1)
        k.op("dve", lambda e: e.tensor_copy(aoG[:, :], psAVG[:, 0:64]),
             reads=[psAVG], writes=[aoG])
        for mc in range(NCH):
            for c in range(NCH):
                k.mm(cm.psS[:, 0:8],
                     wo[:, c * D + mc * 128: c * D + (mc + 1) * 128],
                     aoG[:, c * 8:(c + 1) * 8],
                     reads=[wo, aoG], writes=[cm.psS],
                     start=(c == 0), stop=False)
            k.mm(cm.psS[:, 0:8], bo[0:1, mc * 128:(mc + 1) * 128],
                 onesw[0:1, 0:8],
                 reads=[bo, onesw], writes=[cm.psS],
                 start=False, stop=True)
            k.op("dve", lambda e, mc=mc: e.tensor_copy(
                sG[:, mc * 8:(mc + 1) * 8], cm.psS[:, 0:8]),
                reads=[cm.psS], writes=[sG])



def emit_l0_ffn(k, cm, xnT, g_dram, wbf, fin, gates):
    H = 2048
    NHC = H // 128
    with k.scope() as st:
        b1r = k.sb("l0_b1r", [1, 10 * H], BF16, ctx=st)
        b2r = k.sb("l0_b2r", [1, 10 * D], BF16, ctx=st)
        k.dma("pool", b1r[0:1, 0:8 * H],
              fin["L0_ffn_b1"].rearrange("e h -> (e h)").unsqueeze(0),
              writes=[b1r])
        k.dma("pool", b2r[0:1, 0:8 * D],
              fin["L0_ffn_b2"].rearrange("e d -> (e d)").unsqueeze(0),
              writes=[b2r])
        k.dma("pool", b1r[0:1, 8 * H:10 * H],
              fin["L0_ffn_sb1"].rearrange("e h -> (e h)").unsqueeze(0),
              writes=[b1r])
        k.dma("pool", b2r[0:1, 8 * D:10 * D],
              fin["L0_ffn_sb2"].rearrange("e d -> (e d)").unsqueeze(0),
              writes=[b2r])

        comb = k.sb("l0_comb", [128, NQT * 10], F32, ctx=st)
        gts = k.sb("l0_gts", [128, 8], F32, ctx=st)
        m1 = k.sb("l0_m1", [128, 1], F32, ctx=st)
        m2 = k.sb("l0_m2", [128, 1], F32, ctx=st)
        mk1 = k.sb("l0_mk1", [128, 8], F32, ctx=st)
        mk2 = k.sb("l0_mk2", [128, 8], F32, ctx=st)
        gg2 = k.sb("l0_gg2", [128, 8], F32, ctx=st)
        p1 = k.sb("l0_p1", [128, 1], F32, ctx=st)
        p1m = k.sb("l0_p1m", [128, 1], F32, ctx=st)
        dd = k.sb("l0_dd", [128, 1], F32, ctx=st)
        k.op("dve", lambda e: e.memset(
            comb[:, :].rearrange("p (t e) -> p t e", e=10)[:, :, 8:10], 0.5),
            writes=[comb])
        for tt in range(NQT):
            k.op("dve", lambda e, tt=tt: e.tensor_copy(
                gts[:, :], gates.ap[:, tt * 8:(tt + 1) * 8]),
                reads=[gates], writes=[gts])
            k.op("dve", lambda e: e.tensor_reduce(
                m1[:, :], gts[:, :], AX.X, ALU.max), reads=[gts], writes=[m1])
            k.op("dve", lambda e: e.tensor_scalar(
                mk1[:, :], gts[:, :], m1[:, 0:1], None, ALU.is_equal),
                reads=[gts, m1], writes=[mk1])
            k.op("dve", lambda e: e.scalar_tensor_tensor(
                gg2[:, :], mk1[:, :], NEG, gts[:, :], ALU.mult, ALU.add),
                reads=[mk1, gts], writes=[gg2])
            k.op("dve", lambda e: e.tensor_reduce(
                m2[:, :], gg2[:, :], AX.X, ALU.max), reads=[gg2], writes=[m2])
            k.op("dve", lambda e: e.tensor_scalar(
                mk2[:, :], gg2[:, :], m2[:, 0:1], None, ALU.is_equal),
                reads=[gg2, m2], writes=[mk2])
            k.op("dve", lambda e: e.tensor_sub(dd[:, :], m1[:, :], m2[:, :]),
                 reads=[m1, m2], writes=[dd])
            k.op("act", lambda e: e.activation(p1[:, :], dd[:, :],
                                               AF.Sigmoid),
                 reads=[dd], writes=[p1])
            k.op("dve", lambda e: e.tensor_scalar(
                p1m[:, :], p1[:, :], -1.0, 1.0, ALU.mult, ALU.add),
                reads=[p1], writes=[p1m])
            k.op("dve", lambda e: e.tensor_scalar(
                mk1[:, :], mk1[:, :], p1[:, 0:1], None, ALU.mult),
                reads=[mk1, p1], writes=[mk1])
            k.op("dve", lambda e, tt=tt: e.scalar_tensor_tensor(
                comb[:, tt * 10: tt * 10 + 8], mk2[:, :], p1m[:, 0:1],
                mk1[:, :], ALU.mult, ALU.add),
                reads=[mk1, mk2, p1m], writes=[comb])

        k.dump("l0_comb", comb.ap[:, :], [128, NQT * 10], [comb])
        h_sb = k.sb("l0_h", [128, NHC * 512], BF16, ctx=st)
        w1a = k.sb("l0_w1a", [128, NCH * 128], BF16, ctx=st)
        w1b = k.sb("l0_w1b", [128, NCH * 128], BF16, ctx=st)
        w2a = k.sb("l0_w2a", [128, NHC * 512], BF16, ctx=st)
        w2b = k.sb("l0_w2b", [128, NHC * 512], BF16, ctx=st)
        acc = k.sb("l0_acc", [128, 4 * D], F32, ctx=st)
        sil = k.sb("l0_sil", [128, 4 * D], BF16, ctx=st)
        gst = k.sb("l0_gst", [128, 512], BF16, ctx=st)

        def w1_ap(e, hc):
            src = (wbf["L0_ffn_w1"].ap[e] if e < 8
                   else wbf["L0_ffn_sw1"].ap[e - 8])
            return src[:, hc * 128:(hc + 1) * 128].rearrange(
                "(kc p) h -> p kc h", p=128)

        def w2_ap(e, mt):
            src = (wbf["L0_ffn_w2"].ap[e] if e < 8
                   else wbf["L0_ffn_sw2"].ap[e - 8])
            return src[:, mt * 512:(mt + 1) * 512].rearrange(
                "(hc p) m -> p hc m", p=128)

        for tt in range(NTT):
            tsl = slice(tt * 512, (tt + 1) * 512)
            for e in range(10):
                for hc in range(NHC):
                    wbuf = w1a if hc % 2 == 0 else w1b
                    psx = cm.ps0 if hc % 2 == 0 else cm.psS
                    k.dma("sp", wbuf[:, :].rearrange("p (kc h) -> p kc h",
                                                     kc=NCH),
                          w1_ap(e, hc),
                          reads=[wbf["L0_ffn_w1"] if e < 8
                                 else wbf["L0_ffn_sw1"]], writes=[wbuf])
                    for c in range(NCH):
                        k.mm(psx[:, :],
                             wbuf[:, c * 128:(c + 1) * 128],
                             xnT[:, c * T + tt * 512: c * T + (tt + 1) * 512],
                             reads=[wbuf, xnT], writes=[psx],
                             start=(c == 0), stop=False)
                    k.mm(psx[:, :],
                         b1r[0:1, e * H + hc * 128: e * H + (hc + 1) * 128],
                         cm.ones1[0:1, 0:512],
                         reads=[b1r, cm.ones1], writes=[psx],
                         start=False, stop=True)
                    k.op("act", lambda e2, hc=hc, psx=psx: e2.activation(
                        h_sb[:, hc * 512:(hc + 1) * 512], psx[:, :],
                        AF.Gelu),
                        reads=[psx], writes=[h_sb])
                if tt == 0 and e == 0:
                    k.dump("l0_h00", h_sb.ap[:, :], [128, NHC * 512],
                           [h_sb], dt=BF16)
                for mt in range(2):
                    wbuf = w2a if mt == 0 else w2b
                    k.dma("sp", wbuf[:, :].rearrange("p (hc m) -> p hc m",
                                                     hc=NHC),
                          w2_ap(e, mt),
                          reads=[wbf["L0_ffn_w2"] if e < 8
                                 else wbf["L0_ffn_sw2"]], writes=[wbuf])
                    for ts in range(4):
                        tsg = tt * 4 + ts
                        psy = cm.ps1 if ts % 2 == 0 else cm.ps2
                        for hc in range(NHC):
                            k.mm(psy[:, :],
                                 h_sb[:, hc * 512 + ts * 128:
                                      hc * 512 + (ts + 1) * 128],
                                 wbuf[:, hc * 512:(hc + 1) * 512],
                                 reads=[h_sb, wbuf], writes=[psy],
                                 start=(hc == 0), stop=False)
                        k.mm(psy[:, :], cm.ones1[0:1, 0:128],
                             b2r[0:1, e * D + mt * 512: e * D + (mt + 1) * 512],
                             reads=[cm.ones1, b2r], writes=[psy],
                             start=False, stop=True)
                        out_sl = slice(ts * D + mt * 512,
                                       ts * D + (mt + 1) * 512)
                        if e == 0:
                            k.op("dve", lambda e2, out_sl=out_sl, tsg=tsg,
                                 e=e, psy=psy: e2.tensor_scalar(
                                     acc[:, out_sl], psy[:, :],
                                     comb[:, tsg * 10 + e: tsg * 10 + e + 1],
                                     None, ALU.mult),
                                 reads=[psy, comb], writes=[acc])
                        else:
                            k.op("dve", lambda e2, out_sl=out_sl, tsg=tsg,
                                 e=e, psy=psy: e2.scalar_tensor_tensor(
                                     acc[:, out_sl], psy[:, :],
                                     comb[:, tsg * 10 + e: tsg * 10 + e + 1],
                                     acc[:, out_sl], ALU.mult, ALU.add),
                                 reads=[psy, comb, acc], writes=[acc])
            if tt == 0:
                k.dump("l0_acc0", acc.ap[:, :], [128, 4 * D], [acc])
            k.op("act", lambda e: e.activation(sil[:, :], acc[:, :], AF.Silu),
                 reads=[acc], writes=[sil])
            for c in range(NCH):
                for ts in range(4):
                    k.tr(cm.psT[:, 0:128],
                         sil[:, ts * D + c * 128: ts * D + (c + 1) * 128],
                         cm.ident[:, :],
                         reads=[sil, cm.ident], writes=[cm.psT])
                    k.op("dve", lambda e, ts=ts: e.tensor_copy(
                        gst[:, ts * 128:(ts + 1) * 128], cm.psT[:, 0:128]),
                        reads=[cm.psT], writes=[gst])
                k.dma("pool",
                      g_dram.ap[c * 128:(c + 1) * 128,
                                tt * 512:(tt + 1) * 512],
                      gst[:, :], reads=[gst], writes=[g_dram])





TBLK = 256
NBLK = T // TBLK


def _mix_coeff_cols(k, st, fin, pre, tm_name, cm_name):
    tm = k.sb(pre + "_tm", [128, NCH], F32, ctx=st)
    cmx = k.sb(pre + "_cm", [128, NCH], F32, ctx=st)
    a1 = k.sb(pre + "_a1", [128, NCH], F32, ctx=st)
    a2 = k.sb(pre + "_a2", [128, NCH], F32, ctx=st)
    load_cols(k, tm, fin[tm_name].rearrange("a b c -> (a b c)"), NCH)
    load_cols(k, cmx, fin[cm_name].rearrange("a b c -> (a b c)"), NCH)
    k.op("dve", lambda e: e.tensor_scalar(a1[:, :], tm[:, :], -1.0, 1.0,
                                          ALU.mult, ALU.add),
         reads=[tm], writes=[a1])
    k.op("dve", lambda e: e.tensor_add(a1[:, 0:NCH // 2], a1[:, 0:NCH // 2],
                                       cmx[:, 0:NCH // 2]),
         reads=[a1, cmx], writes=[a1])
    k.op("dve", lambda e: e.memset(a2[:, 0:NCH // 2], 0.0), writes=[a2])
    k.op("dve", lambda e: e.tensor_copy(a2[:, NCH // 2:NCH],
                                        cmx[:, NCH // 2:NCH]),
         reads=[cmx], writes=[a2])
    return tm, a1, a2


def _emit_mix_block(k, xsrc_pad, xm_out, tm, a1, a2, ncols, src0, dst_sl,
                    chunks=None):
    for c in (chunks if chunks is not None else range(NCH)):
        s0 = src0(c)
        x_ap = xsrc_pad.ap[:, s0:s0 + ncols]
        xm1_ap = xsrc_pad.ap[:, s0 - 1:s0 - 1 + ncols]
        xp1_ap = xsrc_pad.ap[:, s0 + 1:s0 + 1 + ncols]
        d = dst_sl(c)
        k.op("dve", lambda e, x_ap=x_ap, d=d, c=c: e.tensor_scalar(
            xm_out.ap[:, d], x_ap, tm[:, c:c + 1], None, ALU.mult),
            reads=[xsrc_pad, tm], writes=[xm_out])
        k.op("dve", lambda e, xm1_ap=xm1_ap, d=d, c=c:
             e.scalar_tensor_tensor(
                 xm_out.ap[:, d], xm1_ap, a1[:, c:c + 1], xm_out.ap[:, d],
                 ALU.mult, ALU.add),
             reads=[xsrc_pad, a1, xm_out], writes=[xm_out])
        k.op("dve", lambda e, xp1_ap=xp1_ap, d=d, c=c:
             e.scalar_tensor_tensor(
                 xm_out.ap[:, d], xp1_ap, a2[:, c:c + 1], xm_out.ap[:, d],
                 ALU.mult, ALU.add),
             reads=[xsrc_pad, a2, xm_out], writes=[xm_out])


def emit_l1_attn(k, cm, xnT, s_dram, wbf, fin, scratch):
    x0_dram = scratch["x0_dram"]
    xa_dram = scratch["xa_dram"]
    xb_dram = scratch["xb_dram"]

    with k.scope() as st:
        g1 = k.sb("l1_g1", [128, NCH], F32, ctx=st)
        b1 = k.sb("l1_b1", [128, NCH], F32, ctx=st)
        g2 = k.sb("l1_g2", [128, NCH], F32, ctx=st)
        b2 = k.sb("l1_b2", [128, NCH], F32, ctx=st)
        load_cols(k, g1, fin["L1_attn_ln1_g"], NCH)
        load_cols(k, b1, fin["L1_attn_ln1_b"], NCH)
        load_cols(k, g2, fin["L1_attn_ln2_g"], NCH)
        load_cols(k, b2, fin["L1_attn_ln2_b"], NCH)
        tmA, a1A, a2A = _mix_coeff_cols(k, st, fin, "l1a",
                                        "L1_attn_att_time_mix",
                                        "L1_attn_att_combined_mix")
        tmF, a1F, a2F = _mix_coeff_cols(k, st, fin, "l1f",
                                        "L1_attn_ffn_time_mix",
                                        "L1_attn_ffn_combined_mix")
        edc = k.sb("l1_ed", [128, NCH], F32, ctx=st)
        euc = k.sb("l1_eu", [128, NCH], F32, ctx=st)
        load_cols(k, edc, fin["L1_attn_time_decay"], NCH)
        load_cols(k, euc, fin["L1_attn_time_first"], NCH)
        k.op("act", lambda e: e.activation(edc[:, :], edc[:, :], AF.Exp),
             reads=[edc], writes=[edc])
        k.op("act", lambda e: e.activation(edc[:, :], edc[:, :], AF.Exp,
                                           scale=-1.0),
             reads=[edc], writes=[edc])
        k.op("act", lambda e: e.activation(euc[:, :], euc[:, :], AF.Exp),
             reads=[euc], writes=[euc])

        st_ln = contextlib.ExitStack()
        x0buf = k.sb("l1_x0buf", [128, T], BF16, ctx=st_ln)

        def _x0_out(c):
            k.dma("pool", x0_dram.ap[c * 128:(c + 1) * 128, :],
                  x0buf[:, :], reads=[x0buf], writes=[x0_dram])
        emit_ln_ff(k, cm, x0buf, x_sb=xnT, gamma_col=g1, beta_col=b1,
                   dst0=lambda c: 0, on_chunk=_x0_out)
        k.barrier()
        st_ln.close()

        st_tm = contextlib.ExitStack()
        wkT = k.sb("l1_wkT", [128, NCH * D], BF16, ctx=st_tm)
        wvT = k.sb("l1_wvT", [128, NCH * D], BF16, ctx=st_tm)
        wrT = k.sb("l1_wrT", [128, NCH * D], BF16, ctx=st_tm)
        woT = k.sb("l1_woT", [128, NCH * D], BF16, ctx=st_tm)
        for w, nm in ((wkT, "L1_attn_Wk"), (wvT, "L1_attn_Wv"),
                      (wrT, "L1_attn_Wr"), (woT, "L1_attn_Wo")):
            for c in range(NCH):
                k.dmat(w[:, c * D:(c + 1) * D],
                       wbf[nm].ap[0:D, c * 128:(c + 1) * 128],
                       reads=[wbf[nm]], writes=[w])
        x0b = k.sb("l1_x0b", [128, NCH * (TBLK + 2)], BF16, ctx=st_tm)
        xmb = k.sb("l1_xmb", [128, NCH * TBLK], BF16, ctx=st_tm)
        kb = k.sb("l1_kb", [128, NCH * TBLK], F32, ctx=st_tm)
        kvb = k.sb("l1_kvb", [128, NCH * TBLK], F32, ctx=st_tm)
        Ab = k.sb("l1_Ab", [128, NCH * (TBLK + 1)], F32, ctx=st_tm)
        Bb = k.sb("l1_Bb", [128, NCH * (TBLK + 1)], F32, ctx=st_tm)
        sigr = k.sb("l1_sigr", [128, NCH * TBLK], BF16, ctx=st_tm)
        syb = k.sb("l1_syb", [128, NCH * TBLK], BF16, ctx=st_tm)
        edb = k.sb("l1_edb", [128, NCH * TBLK], F32, ctx=st_tm)
        carA = k.sb("l1_carA", [128, NCH], F32, ctx=st_tm)
        carB = k.sb("l1_carB", [128, NCH], F32, ctx=st_tm)
        xab = k.sb("l1_xab", [128, NCH * TBLK], BF16, ctx=st_tm)
        k.op("dve", lambda e: e.memset(carA[:, :], 0.0), writes=[carA])
        k.op("dve", lambda e: e.memset(carB[:, :], 0.0), writes=[carB])
        k.op("dve", lambda e: e.memset(edb[:, :], 0.0), writes=[edb])
        for c in range(NCH):
            k.op("dve", lambda e, c=c: e.tensor_scalar(
                edb[:, c * TBLK:(c + 1) * TBLK],
                edb[:, c * TBLK:(c + 1) * TBLK], edc[:, c:c + 1], None,
                ALU.add),
                reads=[edb, edc], writes=[edb])

        for blk in range(NBLK):
            t0 = blk * TBLK
            if blk == 0:
                k.op("dve", lambda e: e.memset(
                    x0b[:, :].rearrange("p (c w) -> p c w", c=NCH)[:, :, 0:1],
                    0.0), writes=[x0b])
            if blk == NBLK - 1:
                k.op("dve", lambda e: e.memset(
                    x0b[:, :].rearrange("p (c w) -> p c w", c=NCH)
                    [:, :, TBLK + 1:TBLK + 2], 0.0), writes=[x0b])
            lo = max(t0 - 1, 0)
            hi = min(t0 + TBLK + 1, T)
            off = 0 if blk > 0 else 1
            k.dma("sp",
                  x0b[:, :].rearrange("p (c w) -> p c w", c=NCH)
                  [:, :, off:off + (hi - lo)],
                  x0_dram.ap.rearrange("(c p) t -> p c t", p=128)[:, :, lo:hi],
                  reads=[x0_dram], writes=[x0b])
            _emit_mix_block(k, x0b, xmb, tmA, a1A, a2A, TBLK,
                            lambda c: c * (TBLK + 2) + 1,
                            lambda c: slice(c * TBLK, (c + 1) * TBLK))
            for mc in range(NCH):
                for c in range(NCH):
                    k.mm(cm.ps0[:, 0:TBLK],
                         wkT[:, c * D + mc * 128: c * D + (mc + 1) * 128],
                         xmb[:, c * TBLK:(c + 1) * TBLK],
                         reads=[wkT, xmb], writes=[cm.ps0],
                         start=(c == 0), stop=(c == NCH - 1))
                k.op("dve", lambda e, mc=mc: e.tensor_scalar_min(
                    kb[:, mc * TBLK:(mc + 1) * TBLK], cm.ps0[:, 0:TBLK],
                    60.0), reads=[cm.ps0], writes=[kb])
            k.op("act", lambda e: e.activation(kb[:, :], kb[:, :], AF.Exp),
                 reads=[kb], writes=[kb])
            for mc in range(NCH):
                for c in range(NCH):
                    k.mm(cm.ps1[:, 0:TBLK],
                         wvT[:, c * D + mc * 128: c * D + (mc + 1) * 128],
                         xmb[:, c * TBLK:(c + 1) * TBLK],
                         reads=[wvT, xmb], writes=[cm.ps1],
                         start=(c == 0), stop=(c == NCH - 1))
                k.op("dve", lambda e, mc=mc: e.tensor_mul(
                    kvb[:, mc * TBLK:(mc + 1) * TBLK],
                    kb[:, mc * TBLK:(mc + 1) * TBLK], cm.ps1[:, 0:TBLK]),
                    reads=[kb, cm.ps1], writes=[kvb])
            for mc in range(NCH):
                for c in range(NCH):
                    k.mm(cm.ps2[:, 0:TBLK],
                         wrT[:, c * D + mc * 128: c * D + (mc + 1) * 128],
                         xmb[:, c * TBLK:(c + 1) * TBLK],
                         reads=[wrT, xmb], writes=[cm.ps2],
                         start=(c == 0), stop=(c == NCH - 1))
                k.op("act", lambda e, mc=mc: e.activation(
                    sigr[:, mc * TBLK:(mc + 1) * TBLK], cm.ps2[:, 0:TBLK],
                    AF.Sigmoid), reads=[cm.ps2], writes=[sigr])
            for c in range(NCH):
                k.op("dve", lambda e, c=c: e.tensor_copy(
                    Ab[:, c * (TBLK + 1): c * (TBLK + 1) + 1],
                    carA[:, c:c + 1]), reads=[carA], writes=[Ab])
                k.op("dve", lambda e, c=c: e.tensor_copy(
                    Bb[:, c * (TBLK + 1): c * (TBLK + 1) + 1],
                    carB[:, c:c + 1]), reads=[carB], writes=[Bb])
                k.op("dve", lambda e, c=c: e.tensor_tensor_scan(
                    Ab[:, c * (TBLK + 1) + 1: (c + 1) * (TBLK + 1)],
                    edb[:, c * TBLK:(c + 1) * TBLK],
                    kvb[:, c * TBLK:(c + 1) * TBLK],
                    carA[:, c:c + 1], ALU.mult, ALU.add),
                    reads=[edb, kvb, carA], writes=[Ab])
                k.op("dve", lambda e, c=c: e.tensor_tensor_scan(
                    Bb[:, c * (TBLK + 1) + 1: (c + 1) * (TBLK + 1)],
                    edb[:, c * TBLK:(c + 1) * TBLK],
                    kb[:, c * TBLK:(c + 1) * TBLK],
                    carB[:, c:c + 1], ALU.mult, ALU.add),
                    reads=[edb, kb, carB], writes=[Bb])
                k.op("dve", lambda e, c=c: e.tensor_copy(
                    carA[:, c:c + 1],
                    Ab[:, (c + 1) * (TBLK + 1) - 1: (c + 1) * (TBLK + 1)]),
                    reads=[Ab], writes=[carA])
                k.op("dve", lambda e, c=c: e.tensor_copy(
                    carB[:, c:c + 1],
                    Bb[:, (c + 1) * (TBLK + 1) - 1: (c + 1) * (TBLK + 1)]),
                    reads=[Bb], writes=[carB])
                ash = Ab.ap[:, c * (TBLK + 1): c * (TBLK + 1) + TBLK]
                bsh = Bb.ap[:, c * (TBLK + 1): c * (TBLK + 1) + TBLK]
                csl = slice(c * TBLK, (c + 1) * TBLK)
                k.op("dve", lambda e, c=c, ash=ash, csl=csl:
                     e.scalar_tensor_tensor(
                         kvb[:, csl], kvb[:, csl], euc[:, c:c + 1], ash,
                         ALU.mult, ALU.add),
                     reads=[kvb, euc, Ab], writes=[kvb])
                k.op("dve", lambda e, c=c, bsh=bsh, csl=csl:
                     e.scalar_tensor_tensor(
                         kb[:, csl], kb[:, csl], euc[:, c:c + 1], bsh,
                         ALU.mult, ALU.add),
                     reads=[kb, euc, Bb], writes=[kb])
                k.op("dve", lambda e, csl=csl: e.reciprocal(
                    kb[:, csl], kb[:, csl]), reads=[kb], writes=[kb])
                k.op("dve", lambda e, csl=csl: e.tensor_mul(
                    kvb[:, csl], kvb[:, csl], kb[:, csl]),
                    reads=[kvb, kb], writes=[kvb])
                k.op("pool", lambda e, csl=csl: e.tensor_mul(
                    syb[:, csl], sigr[:, csl], kvb[:, csl]),
                    reads=[sigr, kvb], writes=[syb])
            for mc in range(NCH):
                for c in range(NCH):
                    k.mm(cm.ps0[:, 0:TBLK],
                         woT[:, c * D + mc * 128: c * D + (mc + 1) * 128],
                         syb[:, c * TBLK:(c + 1) * TBLK],
                         reads=[woT, syb], writes=[cm.ps0],
                         start=(c == 0), stop=(c == NCH - 1))
                x0sl = x0b.ap[:, mc * (TBLK + 2) + 1:
                              mc * (TBLK + 2) + 1 + TBLK]
                k.op("dve", lambda e, mc=mc, x0sl=x0sl: e.tensor_add(
                    xab[:, mc * TBLK:(mc + 1) * TBLK], cm.ps0[:, 0:TBLK],
                    x0sl), reads=[cm.ps0, x0b], writes=[xab])
            k.dma("pool",
                  xa_dram.ap.rearrange("(c p) t -> p c t", p=128)
                  [:, :, t0:t0 + TBLK],
                  xab[:, :].rearrange("p (c w) -> p c w", c=NCH),
                  reads=[xab], writes=[xa_dram])
        k.barrier()
        st_tm.close()

        st_cm = contextlib.ExitStack()
        mixbuf = k.sb("l1_mix", [128, NCH * T], BF16, ctx=st_cm)
        emit_ln_ff(k, cm, mixbuf, xin_dram=xa_dram.ap, in_bf16=True,
                   gamma_col=g2, beta_col=b2)
        for c in range(NCH):
            k.dma("pool", xb_dram.ap[c * 128:(c + 1) * 128, :],
                  mixbuf[:, c * T:(c + 1) * T],
                  reads=[mixbuf], writes=[xb_dram])
        xbt = k.sb("l1_xbt", [128, T + 2], BF16, ctx=st_cm)
        k.op("dve", lambda e: e.memset(xbt[:, 0:1], 0.0), writes=[xbt])
        k.op("dve", lambda e: e.memset(xbt[:, T + 1:T + 2], 0.0),
             writes=[xbt])
        for c in range(NCH):
            k.op("dve", lambda e, c=c: e.tensor_copy(
                xbt[:, 1:T + 1], mixbuf[:, c * T:(c + 1) * T]),
                reads=[mixbuf], writes=[xbt])
            _emit_mix_block(k, xbt, mixbuf, tmF, a1F, a2F, T,
                            lambda cc: 1,
                            lambda cc: slice(cc * T, (cc + 1) * T),
                            chunks=[c])
        H2 = 4096
        NH2 = H2 // 128
        fkbA = k.sb("l1_fkA", [128, NCH * 128], BF16, ctx=st_cm)
        fkbB = k.sb("l1_fkB", [128, NCH * 128], BF16, ctx=st_cm)
        k2t = k.sb("l1_k2", [128, NH2 * 512], BF16, ctx=st_cm)
        fvbA = k.sb("l1_fvA", [128, 128], BF16, ctx=st_cm)
        fvbB = k.sb("l1_fvB", [128, 128], BF16, ctx=st_cm)
        frbA = k.sb("l1_frA", [128, NCH * 128], BF16, ctx=st_cm)
        kvt = k.sb("l1_kvt", [128, 512], F32, ctx=st_cm)
        xbl = k.sb("l1_xbl", [128, 512], BF16, ctx=st_cm)
        sst = k.sb("l1_sst", [128, 512], F32, ctx=st_cm)
        for tt in range(NTT):
            tsl = slice(tt * 512, (tt + 1) * 512)
            for hc in range(NH2):
                fkb = fkbA if hc % 2 == 0 else fkbB
                k.dma("sp", fkb[:, :].rearrange("p (c m) -> p c m", c=NCH),
                      wbf["fkT"].ap[:, hc * 128:(hc + 1) * 128].rearrange(
                          "(c p) m -> p c m", p=128),
                      reads=[wbf["fkT"]], writes=[fkb])
                for c in range(NCH):
                    k.mm(cm.ps0[:, :],
                         fkb[:, c * 128:(c + 1) * 128],
                         mixbuf[:, c * T + tt * 512: c * T + (tt + 1) * 512],
                         reads=[fkb, mixbuf], writes=[cm.ps0],
                         start=(c == 0), stop=(c == NCH - 1))
                k.op("dve", lambda e, hc=hc: e.tensor_scalar_max(
                    cm.ps0[:, :], cm.ps0[:, :], 0.0),
                    reads=[cm.ps0], writes=[cm.ps0])
                k.op("act", lambda e, hc=hc: e.activation(
                    k2t[:, hc * 512:(hc + 1) * 512], cm.ps0[:, :],
                    AF.Square), reads=[cm.ps0], writes=[k2t])
            for mc in range(NCH):
                k.dma("sp", frbA[:, :].rearrange("p (c m) -> p c m", c=NCH),
                      wbf["frT"].ap[:, mc * 128:(mc + 1) * 128].rearrange(
                          "(c p) m -> p c m", p=128),
                      reads=[wbf["frT"]], writes=[frbA])
                for c in range(NCH):
                    k.mm(cm.ps1[:, :],
                         frbA[:, c * 128:(c + 1) * 128],
                         mixbuf[:, c * T + tt * 512: c * T + (tt + 1) * 512],
                         reads=[frbA, mixbuf], writes=[cm.ps1],
                         start=(c == 0), stop=(c == NCH - 1))
                for hc in range(NH2):
                    fvb = fvbA if hc % 2 == 0 else fvbB
                    k.dma("act", fvb[:, :],
                          wbf["fvT"].ap[hc * 128:(hc + 1) * 128,
                                        mc * 128:(mc + 1) * 128],
                          reads=[wbf["fvT"]], writes=[fvb])
                    k.mm(cm.ps2[:, :], fvb[:, :],
                         k2t[:, hc * 512:(hc + 1) * 512],
                         reads=[fvb, k2t], writes=[cm.ps2],
                         start=(hc == 0), stop=(hc == NH2 - 1))
                k.op("dve", lambda e: e.tensor_copy(kvt[:, :], cm.ps2[:, :]),
                     reads=[cm.ps2], writes=[kvt])
                k.dma("sp", xbl[:, :],
                      xb_dram.ap[mc * 128:(mc + 1) * 128, tsl],
                      reads=[xb_dram], writes=[xbl])
                k.op("act", lambda e: e.activation(
                    sst[:, :], cm.ps1[:, :], AF.Sigmoid),
                    reads=[cm.ps1], writes=[sst])
                k.op("dve", lambda e: e.tensor_mul(sst[:, :], sst[:, :],
                                                   kvt[:, :]),
                     reads=[sst, kvt], writes=[sst])
                k.op("dve", lambda e: e.tensor_add(sst[:, :], sst[:, :],
                                                   xbl[:, :]),
                     reads=[sst, xbl], writes=[sst])
                k.dma("pool", s_dram.ap[mc * 128:(mc + 1) * 128, tsl],
                      sst[:, :], reads=[sst], writes=[s_dram])
        k.barrier()
        st_cm.close()


def emit_l1_ffn(k, cm, xnT, g_dram, wbf, fin, gates, scratch):
    HF = 4096
    NHT = HF // 512
    with k.scope() as st:
        gcol = k.sb("l1f_gcol", [128, NQT * 8], F32, ctx=st)
        sc = k.sb("l1f_sc", [128, 8], F32, ctx=st)
        nmax = k.sb("l1f_nmax", [128, 1], F32, ctx=st)
        zz = k.sb("l1f_z", [128, 1], F32, ctx=st)
        rz = k.sb("l1f_rz", [128, 1], F32, ctx=st)
        m1 = k.sb("l1f_m1", [128, 1], F32, ctx=st)
        m2 = k.sb("l1f_m2", [128, 1], F32, ctx=st)
        mk1 = k.sb("l1f_mk1", [128, 8], F32, ctx=st)
        mk2 = k.sb("l1f_mk2", [128, 8], F32, ctx=st)
        gg2 = k.sb("l1f_gg2", [128, 8], F32, ctx=st)
        vs = k.sb("l1f_vs", [128, 1], F32, ctx=st)
        vs2 = k.sb("l1f_vs2", [128, NQT], F32, ctx=st)
        vs2r_d = scratch["row_dram"]
        for tt in range(NQT):
            gsl = slice(tt * 8, (tt + 1) * 8)
            k.op("dve", lambda e, gsl=gsl: e.tensor_reduce(
                nmax[:, :], gates.ap[:, gsl], AX.X, ALU.max, negate=True),
                reads=[gates], writes=[nmax])
            k.op("act", lambda e, gsl=gsl: e.activation(
                sc[:, :], gates.ap[:, gsl], AF.Exp, bias=nmax[:, 0:1],
                accum_out=zz[:, :]),
                reads=[gates, nmax], writes=[sc, zz])
            k.op("dve", lambda e: e.reciprocal(rz[:, :], zz[:, :]),
                 reads=[zz], writes=[rz])
            k.op("dve", lambda e: e.tensor_scalar(
                sc[:, :], sc[:, :], rz[:, 0:1], None, ALU.mult),
                reads=[sc, rz], writes=[sc])
            k.op("dve", lambda e: e.tensor_reduce(
                m1[:, :], sc[:, :], AX.X, ALU.max), reads=[sc], writes=[m1])
            k.op("dve", lambda e: e.tensor_scalar(
                mk1[:, :], sc[:, :], m1[:, 0:1], None, ALU.is_equal),
                reads=[sc, m1], writes=[mk1])
            k.op("dve", lambda e: e.scalar_tensor_tensor(
                gg2[:, :], mk1[:, :], NEG, sc[:, :], ALU.mult, ALU.add),
                reads=[mk1, sc], writes=[gg2])
            k.op("dve", lambda e: e.tensor_reduce(
                m2[:, :], gg2[:, :], AX.X, ALU.max), reads=[gg2], writes=[m2])
            k.op("dve", lambda e: e.tensor_scalar(
                mk2[:, :], gg2[:, :], m2[:, 0:1], None, ALU.is_equal),
                reads=[gg2, m2], writes=[mk2])
            k.op("dve", lambda e: e.tensor_add(vs[:, :], m1[:, :], m2[:, :]),
                 reads=[m1, m2], writes=[vs])
            k.op("dve", lambda e: e.tensor_add(mk1[:, :], mk1[:, :],
                                               mk2[:, :]),
                 reads=[mk1, mk2], writes=[mk1])
            k.op("dve", lambda e, tt=tt: e.tensor_scalar(
                gcol[:, tt * 8:(tt + 1) * 8], mk1[:, :], vs[:, 0:1], None,
                ALU.mult), reads=[mk1, vs], writes=[gcol])
            k.op("dve", lambda e, tt=tt: e.tensor_scalar(
                vs2[:, tt:tt + 1], vs[:, :], 2.0, None, ALU.mult),
                reads=[vs], writes=[vs2])
        for tb in range(NQT):
            k.dma("pool", vs2r_d.ap[tb, :].unsqueeze(0).rearrange(
                "a p -> (a p)").unsqueeze(1),
                vs2[:, tb:tb + 1], reads=[vs2], writes=[vs2r_d])
        vsrow_b = k.sb("l1f_vsrb", [1, T], BF16, ctx=st)
        k.dma("pool", vsrow_b[0:1, :],
              vs2r_d.ap.rearrange("t p -> (t p)").unsqueeze(0),
              reads=[vs2r_d], writes=[vsrow_b])

        k.dump("l1f_gcol", gcol.ap[:, :], [128, NQT * 8], [gcol])
        k.dump("l1f_vsrow", vsrow_b.ap[0:1, :], [1, T], [vsrow_b], dt=BF16)
        aeT = k.sb("l1f_aeT", [128, 8 * NCH * 16], BF16, ctx=st)
        atmp = k.sb("l1f_atmp", [16, D], BF16, ctx=st)
        for e_ in range(8):
            k.dma("sp", atmp[0:16, :], wbf["L1_ffn_lora_A"].ap[e_],
                  reads=[wbf["L1_ffn_lora_A"]], writes=[atmp])
            for c in range(NCH):
                k.tr(cm.psT[0:128, 0:16],
                     atmp[0:16, c * 128:(c + 1) * 128], cm.ident[0:16, 0:16],
                     reads=[atmp, cm.ident], writes=[cm.psT])
                k.op("dve", lambda e, e_=e_, c=c: e.tensor_copy(
                    aeT[:, (e_ * NCH + c) * 16:(e_ * NCH + c + 1) * 16],
                    cm.psT[0:128, 0:16]),
                    reads=[cm.psT], writes=[aeT])
        beT_d = scratch["beT_dram"]
        btmp = k.sb("l1f_btmp", [128, 16], BF16, ctx=st)
        btr = k.sb("l1f_btr", [16, 128], BF16, ctx=st)
        for e_ in range(8):
            for hc in range(32):
                k.dma("sp", btmp[:, :],
                      wbf["L1_ffn_lora_B"].ap[e_][hc * 128:(hc + 1) * 128, :],
                      reads=[wbf["L1_ffn_lora_B"]], writes=[btmp])
                k.tr(cm.psT[0:16, 0:128], btmp[:, :], cm.ident[:, :],
                     reads=[btmp, cm.ident], writes=[cm.psT])
                k.op("dve", lambda e: e.tensor_copy(
                    btr[0:16, :], cm.psT[0:16, 0:128]),
                    reads=[cm.psT], writes=[btr])
                k.dma("pool", beT_d.ap[e_][:, hc * 128:(hc + 1) * 128],
                      btr[0:16, :], reads=[btr], writes=[beT_d])
        birow = k.sb("l1f_bi", [1, HF], BF16, ctx=st)
        borow = k.sb("l1f_bo", [1, D], BF16, ctx=st)
        load_row_bf16(k, birow, fin["L1_ffn_bi"], HF)
        load_row_bf16(k, borow, fin["L1_ffn_bo"], D)

        wit = k.sb("l1f_wit", [128, NCH * 512], BF16, ctx=st)
        wot = k.sb("l1f_wot", [128, 32 * 128], BF16, ctx=st)
        base_all = k.sb("l1f_base", [128, 4 * HF], BF16, ctx=st)
        h_e = k.sb("l1f_he", [128, 512], BF16, ctx=st)
        acc = k.sb("l1f_acc", [128, 4 * HF], BF16, ctx=st)
        hsT = base_all
        rA = k.sb("l1f_rA", [16, 8 * 512], BF16, ctx=st)
        beT = k.sb("l1f_beT", [16, HF], BF16, ctx=st)
        gtmp = k.sb("l1f_gtmp", [128, 512], F32, ctx=st)
        for tt in range(NTT):
            tsl = slice(tt * 512, (tt + 1) * 512)
            for e_ in range(8):
                for c in range(NCH):
                    k.mm(cm.psS[0:16, 0:512],
                         aeT[:, (e_ * NCH + c) * 16:(e_ * NCH + c + 1) * 16],
                         xnT[:, c * T + tt * 512: c * T + (tt + 1) * 512],
                         reads=[aeT, xnT], writes=[cm.psS],
                         start=(c == 0), stop=(c == NCH - 1))
                k.op("dve", lambda e, e_=e_: e.tensor_copy(
                    rA[0:16, e_ * 512:(e_ + 1) * 512], cm.psS[0:16, 0:512]),
                    reads=[cm.psS], writes=[rA])
            for ts4 in range(4):
                tb = tt * 4 + ts4
                for ht in range(NHT):
                    k.dma("sp",
                          wit[:, :].rearrange("p (c m) -> p c m", c=NCH),
                          wbf["wiT"].ap[:, ht * 512:(ht + 1) * 512]
                          .rearrange("(c p) m -> p c m", p=128),
                          reads=[wbf["wiT"]], writes=[wit])
                    for c in range(NCH):
                        k.mm(cm.ps0[:, :],
                             xnT[:, c * T + tb * 128: c * T + (tb + 1) * 128],
                             wit[:, c * 512:(c + 1) * 512],
                             reads=[xnT, wit], writes=[cm.ps0],
                             start=(c == 0), stop=False)
                    k.mm(cm.ps0[:, :], cm.ones1[0:1, 0:128],
                         birow[0:1, ht * 512:(ht + 1) * 512],
                         reads=[cm.ones1, birow], writes=[cm.ps0],
                         start=False, stop=True)
                    k.op("act", lambda e, ts4=ts4, ht=ht: e.activation(
                        base_all[:, ts4 * HF + ht * 512:
                                 ts4 * HF + (ht + 1) * 512],
                        cm.ps0[:, :], AF.Copy),
                        reads=[cm.ps0], writes=[base_all])
            for e_ in range(8):
                k.dma("sp", beT[0:16, :], beT_d.ap[e_],
                      reads=[beT_d], writes=[beT])
                for ts4 in range(4):
                    tb = tt * 4 + ts4
                    for ht in range(NHT):
                        bsl = slice(ts4 * HF + ht * 512,
                                    ts4 * HF + (ht + 1) * 512)
                        k.mm(cm.ps1[:, :],
                             rA[0:16, e_ * 512 + ts4 * 128:
                                e_ * 512 + (ts4 + 1) * 128],
                             beT[0:16, ht * 512:(ht + 1) * 512],
                             reads=[rA, beT], writes=[cm.ps1],
                             start=True, stop=False)
                        k.mm(cm.ps1[:, :], cm.ident[:, :],
                             base_all[:, bsl],
                             reads=[cm.ident, base_all], writes=[cm.ps1],
                             start=False, stop=True)
                        k.op("act", lambda e: e.activation(
                            h_e[:, :], cm.ps1[:, :], AF.Relu),
                            reads=[cm.ps1], writes=[h_e])
                        eng = "dve"
                        if e_ == 0:
                            k.op(eng, lambda e, bsl=bsl, tb=tb, e_=e_:
                                 e.tensor_scalar(
                                     acc[:, bsl], h_e[:, :],
                                     gcol[:, tb * 8 + e_:tb * 8 + e_ + 1],
                                     None, ALU.mult),
                                 reads=[h_e, gcol], writes=[acc])
                        else:
                            k.op(eng, lambda e, bsl=bsl, tb=tb, e_=e_:
                                 e.scalar_tensor_tensor(
                                     acc[:, bsl], h_e[:, :],
                                     gcol[:, tb * 8 + e_:tb * 8 + e_ + 1],
                                     acc[:, bsl], ALU.mult, ALU.add),
                                 reads=[h_e, gcol, acc], writes=[acc])
            if tt == 0:
                k.dump("l1f_acc0", acc.ap[:, :], [128, 4 * HF], [acc],
                       dt=BF16)
                k.dump("l1f_base0", base_all.ap[:, :], [128, 4 * HF],
                       [base_all], dt=BF16)
                k.dump("l1f_rA0", rA.ap[0:16, :], [16, 8 * 512], [rA],
                       dt=BF16)
            for ts4 in range(4):
                for hc in range(32):
                    k.tr(cm.psT[:, 0:128],
                         acc[:, ts4 * HF + hc * 128:
                             ts4 * HF + (hc + 1) * 128],
                         cm.ident[:, :],
                         reads=[acc, cm.ident], writes=[cm.psT])
                    k.op("dve", lambda e, hc=hc, ts4=ts4: e.tensor_copy(
                        hsT[:, hc * 512 + ts4 * 128:
                            hc * 512 + (ts4 + 1) * 128],
                        cm.psT[:, 0:128]),
                        reads=[cm.psT], writes=[hsT])
            for mc in range(NCH):
                k.dma("sp",
                      wot[:, :].rearrange("p (hc m) -> p hc m", hc=32),
                      wbf["woT"].ap[:, mc * 128:(mc + 1) * 128]
                      .rearrange("(hc p) m -> p hc m", p=128),
                      reads=[wbf["woT"]], writes=[wot])
                for hc in range(32):
                    k.mm(cm.ps2[:, :],
                         wot[:, hc * 128:(hc + 1) * 128],
                         hsT[:, hc * 512:(hc + 1) * 512],
                         reads=[wot, hsT], writes=[cm.ps2],
                         start=(hc == 0), stop=False)
                k.mm(cm.ps2[:, :], borow[0:1, mc * 128:(mc + 1) * 128],
                     vsrow_b[0:1, tsl],
                     reads=[borow, vsrow_b], writes=[cm.ps2],
                     start=False, stop=True)
                k.op("act", lambda e: e.activation(gtmp[:, :],
                                                   cm.ps2[:, :], AF.Silu),
                     reads=[cm.ps2], writes=[gtmp])
                k.dma("pool", g_dram.ap[mc * 128:(mc + 1) * 128, tsl],
                      gtmp[:, :], reads=[gtmp], writes=[g_dram])





def emit_l2_attn(k, cm, xnT, s_dram, wbf, fin):
    HS = 128
    with k.scope() as st:
        wqT = k.sb("l2_wqT", [128, 2 * NCH * 128], BF16, ctx=st)
        wkT = k.sb("l2_wkT", [128, NCH * 128], BF16, ctx=st)
        wvT = k.sb("l2_wvT", [128, NCH * 128], BF16, ctx=st)
        woT = k.sb("l2_woT", [128, 2 * NCH * 128], BF16, ctx=st)
        for h in range(2):
            for c in range(NCH):
                k.dmat(wqT[:, (h * NCH + c) * 128:(h * NCH + c + 1) * 128],
                       wbf["L2_attn_Wq"].ap[h][:, c * 128:(c + 1) * 128],
                       reads=[wbf["L2_attn_Wq"]], writes=[wqT])
        for c in range(NCH):
            k.dmat(wkT[:, c * 128:(c + 1) * 128],
                   wbf["L2_attn_Wk"].ap[:, c * 128:(c + 1) * 128],
                   reads=[wbf["L2_attn_Wk"]], writes=[wkT])
            k.dmat(wvT[:, c * 128:(c + 1) * 128],
                   wbf["L2_attn_Wv"].ap[:, c * 128:(c + 1) * 128],
                   reads=[wbf["L2_attn_Wv"]], writes=[wvT])
        for j in range(2):
            for c in range(NCH):
                k.dmat(woT[:, (j * NCH + c) * 128:(j * NCH + c + 1) * 128],
                       wbf["L2_attn_Wo"].ap[c * 128:(c + 1) * 128,
                                            j * 128:(j + 1) * 128],
                       reads=[wbf["L2_attn_Wo"]], writes=[woT])
        qrT = k.sb("l2_qrT", [128, 2 * T], BF16, ctx=st)
        krT = k.sb("l2_krT", [128, T], BF16, ctx=st)
        v_sb = k.sb("l2_v", [128, NQT * 128], BF16, ctx=st)
        aoT = k.sb("l2_aoT", [128, 2 * T], BF16, ctx=st)
        st2 = contextlib.ExitStack()
        ropeC = k.sb("l2_ropc", [128, T], F32, ctx=st2)
        ropeS = k.sb("l2_rops", [128, T], F32, ctx=st2)
        swp = k.sb("l2_swp", [128, 128], F32, ctx=st2)
        k.dma("sp", ropeC[:, :], fin["c_rope_cos"][:, :], writes=[ropeC])
        k.dma("sp", ropeS[:, :], fin["c_rope_sin"][:, :], writes=[ropeS])
        k.dma("sp", swp[:, :], fin["c_rope_swap"][:, :], writes=[swp])
        raw = k.sb("l2_raw", [128, T], F32, ctx=st2)
        tr1 = k.sb("l2_tr1", [128, T], F32, ctx=st2)

        def qk_proj(wT, w_off, dst, dst_off, scale):
            for tt in range(NTT):
                tsl = slice(tt * 512, (tt + 1) * 512)
                for c in range(NCH):
                    k.mm(cm.ps0[:, :],
                         wT.ap[:, w_off + c * 128: w_off + (c + 1) * 128],
                         xnT[:, c * T + tt * 512: c * T + (tt + 1) * 512],
                         reads=[wT, xnT], writes=[cm.ps0],
                         start=(c == 0), stop=(c == NCH - 1))
                k.op("dve", lambda e, tsl=tsl: e.tensor_copy(
                    raw[:, tsl], cm.ps0[:, :]), reads=[cm.ps0], writes=[raw])
            for tt in range(NTT):
                tsl = slice(tt * 512, (tt + 1) * 512)
                for q4 in range(4):
                    ssl = slice(tt * 512 + q4 * 128, tt * 512 + (q4 + 1) * 128)
                    k.mm(cm.ps1[:, 0:128], swp[:, :], raw.ap[:, ssl],
                         reads=[swp, raw], writes=[cm.ps1],
                         start=True, stop=True)
                    k.op("dve", lambda e, ssl=ssl: e.tensor_mul(
                        tr1[:, ssl], cm.ps1[:, 0:128], ropeS[:, ssl]),
                        reads=[cm.ps1, ropeS], writes=[tr1])
                k.op("dve", lambda e, tsl=tsl: e.tensor_mul(
                    raw[:, tsl], raw[:, tsl], ropeC[:, tsl]),
                    reads=[raw, ropeC], writes=[raw])
                k.op("dve", lambda e, tsl=tsl: e.tensor_add(
                    raw[:, tsl], raw[:, tsl], tr1[:, tsl]),
                    reads=[raw, tr1], writes=[raw])
                k.op("act", lambda e, tsl=tsl, dst=dst, dst_off=dst_off:
                     e.activation(dst.ap[:, dst_off + tsl.start:
                                         dst_off + tsl.stop],
                                  raw[:, tsl], AF.Copy, scale=scale),
                     reads=[raw], writes=[dst])

        qk_proj(wqT, 0, qrT, 0, HS ** -0.5)
        qk_proj(wqT, NCH * 128, qrT, T, HS ** -0.5)
        qk_proj(wkT, 0, krT, 0, 1.0)
        for kt in range(NQT):
            for c in range(NCH):
                k.mm(cm.ps1[:, 0:128],
                     xnT[:, c * T + kt * 128: c * T + (kt + 1) * 128],
                     wvT[:, c * 128:(c + 1) * 128],
                     reads=[xnT, wvT], writes=[cm.ps1],
                     start=(c == 0), stop=(c == NCH - 1))
            k.op("dve", lambda e, kt=kt: e.tensor_copy(
                v_sb[:, kt * 128:(kt + 1) * 128], cm.ps1[:, 0:128]),
                reads=[cm.ps1], writes=[v_sb])

        k.barrier()
        st2.close()
        Ssb = k.sb("l2_S", [128, T], F32, ctx=st)
        Psb = k.sb("l2_P", [128, T], BF16, ctx=st)
        PTs = k.sb("l2_PT", [128, 512], BF16, ctx=st)
        nmax = k.sb("l2_nmax", [128, 1], F32, ctx=st)
        zrow = k.sb("l2_zrow", [128, 1], F32, ctx=st)
        rz = k.sb("l2_rz", [128, 1], F32, ctx=st)
        cmask = k.sb("l2_cmask", [128, 4 * 512], F32, ctx=st)
        k.dma("sp", cmask[:, :].rearrange("p (r m) -> p r m", r=4),
              fin["c_causal"][:, :, :].rearrange("r p m -> p r m"),
              writes=[cmask])
        for h in range(2):
            for qt in range(NQT):
                nkt = qt // 4 + 1
                L = nkt * 512
                for j in range(nkt):
                    k.mm(cm.ps0[:, :],
                         qrT[:, h * T + qt * 128: h * T + (qt + 1) * 128],
                         krT[:, j * 512:(j + 1) * 512],
                         reads=[qrT, krT], writes=[cm.ps0],
                         start=True, stop=True)
                    if j == nkt - 1:
                        k.op("dve", lambda e, j=j, qt=qt: e.tensor_add(
                            Ssb[:, j * 512:(j + 1) * 512], cm.ps0[:, :],
                            cmask[:, (qt % 4) * 512:(qt % 4 + 1) * 512]),
                            reads=[cm.ps0, cmask], writes=[Ssb])
                    else:
                        k.op("dve", lambda e, j=j: e.tensor_copy(
                            Ssb[:, j * 512:(j + 1) * 512], cm.ps0[:, :]),
                            reads=[cm.ps0], writes=[Ssb])
                k.op("dve", lambda e, L=L: e.tensor_reduce(
                    nmax[:, :], Ssb[:, 0:L], AX.X, ALU.max, negate=True),
                    reads=[Ssb], writes=[nmax])
                k.op("act", lambda e, L=L: e.activation(
                    Psb[:, 0:L], Ssb[:, 0:L], AF.Exp, bias=nmax[:, 0:1],
                    accum_out=zrow[:, :]),
                    reads=[Ssb, nmax], writes=[Psb, zrow])
                k.op("dve", lambda e: e.reciprocal(rz[:, :], zrow[:, :]),
                     reads=[zrow], writes=[rz])
                k.op("dve", lambda e, L=L: e.tensor_scalar(
                    Psb[:, 0:L], Psb[:, 0:L], rz[:, 0:1], None, ALU.mult),
                    reads=[Psb, rz], writes=[Psb])
                for jj in range(nkt):
                    for q4 in range(4):
                        k.tr(cm.psT[:, q4 * 128:(q4 + 1) * 128],
                             Psb[:, jj * 512 + q4 * 128:
                                 jj * 512 + (q4 + 1) * 128],
                             cm.ident[:, :],
                             reads=[Psb, cm.ident], writes=[cm.psT])
                    k.op("dve", lambda e: e.tensor_copy(
                        PTs[:, :], cm.psT[:, 0:512]),
                        reads=[cm.psT], writes=[PTs])
                    for q4 in range(4):
                        kt = jj * 4 + q4
                        k.mm(cm.ps2[:, 0:128],
                             v_sb[:, kt * 128:(kt + 1) * 128],
                             PTs[:, q4 * 128:(q4 + 1) * 128],
                             reads=[v_sb, PTs], writes=[cm.ps2],
                             start=(kt == 0), stop=(kt == nkt * 4 - 1))
                k.op("dve", lambda e, h=h, qt=qt: e.tensor_copy(
                    aoT[:, h * T + qt * 128: h * T + (qt + 1) * 128],
                    cm.ps2[:, 0:128]),
                    reads=[cm.ps2], writes=[aoT])
        sstage = k.sb("l2_sst", [128, 512], F32, ctx=st)
        for mc in range(NCH):
            for tt in range(NTT):
                tsl = slice(tt * 512, (tt + 1) * 512)
                for j in range(2):
                    k.mm(cm.ps0[:, :],
                         woT[:, (j * NCH + mc) * 128:
                             (j * NCH + mc + 1) * 128],
                         aoT[:, j * T + tt * 512: j * T + (tt + 1) * 512],
                         reads=[woT, aoT], writes=[cm.ps0],
                         start=(j == 0), stop=(j == 1))
                k.op("dve", lambda e: e.tensor_copy(sstage[:, :],
                                                    cm.ps0[:, :]),
                     reads=[cm.ps0], writes=[sstage])
                k.dma("pool", s_dram.ap[mc * 128:(mc + 1) * 128, tsl],
                      sstage[:, :], reads=[sstage], writes=[s_dram])


def emit_l2_ffn(k, cm, xnT, g_dram, wbf, fin, gates):
    DD = 256
    with k.scope() as st:
        msk = k.sb("l2_msk", [128, NQT * 8], F32, ctx=st)
        m1 = k.sb("l2_m1", [128, 1], F32, ctx=st)
        for tt in range(NQT):
            gsl = slice(tt * 8, (tt + 1) * 8)
            k.op("dve", lambda e, gsl=gsl: e.tensor_reduce(
                m1[:, :], gates.ap[:, gsl], AX.X, ALU.max),
                reads=[gates], writes=[m1])
            k.op("dve", lambda e, gsl=gsl: e.tensor_scalar(
                msk[:, gsl], gates.ap[:, gsl], m1[:, 0:1], None,
                ALU.is_equal),
                reads=[gates, m1], writes=[msk])
        wg_sb = k.sb("l2_wg", [128, NCH * DD], BF16, ctx=st)
        wu_sb = k.sb("l2_wu", [128, NCH * DD], BF16, ctx=st)
        wd_sb = k.sb("l2_wd", [128, 2 * 1024], BF16, ctx=st)
        inter = k.sb("l2_int", [128, 2 * 512], BF16, ctx=st)
        sq = k.sb("l2_sq", [128, 512], F32, ctx=st)
        acc = k.sb("l2_acc", [128, 4 * D], F32, ctx=st)
        sil = k.sb("l2_sil", [128, 4 * D], BF16, ctx=st)
        gst = k.sb("l2_gst", [128, 512], BF16, ctx=st)
        for tt in range(NTT):
            for e_ in range(8):
                k.dma("sp", wg_sb[:, :].rearrange("p (c m) -> p c m", c=NCH),
                      wbf["L2_ffn_wg"].ap[e_].rearrange("(c p) m -> p c m",
                                                        p=128),
                      reads=[wbf["L2_ffn_wg"]], writes=[wg_sb])
                k.dma("sp", wu_sb[:, :].rearrange("p (c m) -> p c m", c=NCH),
                      wbf["L2_ffn_wu"].ap[e_].rearrange("(c p) m -> p c m",
                                                        p=128),
                      reads=[wbf["L2_ffn_wu"]], writes=[wu_sb])
                k.dma("sp", wd_sb[:, :].rearrange("p (c m) -> p c m", c=2),
                      wbf["L2_ffn_wd"].ap[e_].rearrange("(c p) m -> p c m",
                                                        p=128),
                      reads=[wbf["L2_ffn_wd"]], writes=[wd_sb])
                for ddc in range(2):
                    for c in range(NCH):
                        k.mm(cm.ps0[:, :],
                             wg_sb[:, c * DD + ddc * 128:
                                   c * DD + (ddc + 1) * 128],
                             xnT[:, c * T + tt * 512: c * T + (tt + 1) * 512],
                             reads=[wg_sb, xnT], writes=[cm.ps0],
                             start=(c == 0), stop=(c == NCH - 1))
                    for c in range(NCH):
                        k.mm(cm.ps1[:, :],
                             wu_sb[:, c * DD + ddc * 128:
                                   c * DD + (ddc + 1) * 128],
                             xnT[:, c * T + tt * 512: c * T + (tt + 1) * 512],
                             reads=[wu_sb, xnT], writes=[cm.ps1],
                             start=(c == 0), stop=(c == NCH - 1))
                    k.op("dve", lambda e: e.tensor_scalar_max(
                        sq[:, :], cm.ps0[:, :], 0.0),
                        reads=[cm.ps0], writes=[sq])
                    k.op("act", lambda e: e.activation(
                        sq[:, :], sq[:, :], AF.Square),
                        reads=[sq], writes=[sq])
                    k.op("dve", lambda e, ddc=ddc: e.tensor_mul(
                        inter[:, ddc * 512:(ddc + 1) * 512], sq[:, :],
                        cm.ps1[:, :]),
                        reads=[sq, cm.ps1], writes=[inter])
                for ts4 in range(4):
                    tb = tt * 4 + ts4
                    for mt in range(2):
                        for ddc in range(2):
                            k.mm(cm.ps2[:, :],
                                 inter[:, ddc * 512 + ts4 * 128:
                                       ddc * 512 + (ts4 + 1) * 128],
                                 wd_sb[:, ddc * 1024 + mt * 512:
                                       ddc * 1024 + (mt + 1) * 512],
                                 reads=[inter, wd_sb], writes=[cm.ps2],
                                 start=(ddc == 0), stop=(ddc == 1))
                        out_sl = slice(ts4 * D + mt * 512,
                                       ts4 * D + (mt + 1) * 512)
                        if e_ == 0:
                            k.op("dve", lambda e, out_sl=out_sl, tb=tb, e_=e_:
                                 e.tensor_scalar(
                                     acc[:, out_sl], cm.ps2[:, :],
                                     msk[:, tb * 8 + e_:tb * 8 + e_ + 1],
                                     None, ALU.mult),
                                 reads=[cm.ps2, msk], writes=[acc])
                        else:
                            k.op("dve", lambda e, out_sl=out_sl, tb=tb, e_=e_:
                                 e.scalar_tensor_tensor(
                                     acc[:, out_sl], cm.ps2[:, :],
                                     msk[:, tb * 8 + e_:tb * 8 + e_ + 1],
                                     acc[:, out_sl], ALU.mult, ALU.add),
                                 reads=[cm.ps2, msk, acc], writes=[acc])
            k.op("act", lambda e: e.activation(sil[:, :], acc[:, :], AF.Silu),
                 reads=[acc], writes=[sil])
            for c in range(NCH):
                for ts4 in range(4):
                    k.tr(cm.psT[:, 0:128],
                         sil[:, ts4 * D + c * 128: ts4 * D + (c + 1) * 128],
                         cm.ident[:, :],
                         reads=[sil, cm.ident], writes=[cm.psT])
                    k.op("dve", lambda e, ts4=ts4: e.tensor_copy(
                        gst[:, ts4 * 128:(ts4 + 1) * 128], cm.psT[:, 0:128]),
                        reads=[cm.psT], writes=[gst])
                k.dma("pool",
                      g_dram.ap[c * 128:(c + 1) * 128,
                                tt * 512:(tt + 1) * 512],
                      gst[:, :], reads=[gst], writes=[g_dram])





def emit_l3_attn(k, cm, xnT, s_dram, wbf, fin, scratch):
    from concourse.ap import AP as APc
    ao_dram = scratch["ao_dram"]
    with k.scope() as st:
        dall = k.sb("l3_dall", [128, 16 * 256], F32, ctx=st)
        wpT = k.sb("l3_wpT", [128, NCH * D], BF16, ctx=st)
        for c in range(NCH):
            k.dmat(wpT[:, c * D:(c + 1) * D],
                   wbf["L3_attn_W_proj"].ap[0:D, c * 128:(c + 1) * 128],
                   reads=[wbf["L3_attn_W_proj"]], writes=[wpT])
        barow = k.sb("l3_barow", [1, 3072], BF16, ctx=st)
        bprow = k.sb("l3_bprow", [1, D], BF16, ctx=st)
        load_row_bf16(k, barow, fin["L3_attn_b_attn"], 3072)
        load_row_bf16(k, bprow, fin["L3_attn_b_proj"], D)
        with k.scope() as stb:
            relmap = k.sb("l3_relmap", [32, 384], F32, ctx=stb)
            rele = k.sb("l3_rele", [32, 16], F32, ctx=stb)
            prev = k.sb("l3_prev", [128, 128], F32, ctx=stb)
            revp = k.sb("l3_revp", [128, 3 * 16], F32, ctx=stb)
            revt = k.sb("l3_revt", [16, 384], F32, ctx=stb)
            eps_t = k.sb("l3_eps", [128, 256], F32, ctx=stb)
            revt_d = k.dram("l3_revt_d", [16, 384], F32)
            k.dma("sp", relmap[0:32, :], fin["c_relmap"][:, :],
                  writes=[relmap])
            k.dma("sp", rele[0:32, :], fin["L3_attn_rel_emb"][:, :],
                  writes=[rele])
            k.dma("sp", prev[:, :], fin["c_prev"][:, :], writes=[prev])
            for i in range(3):
                k.mm(cm.psS[:, 0:16], relmap[0:32, i * 128:(i + 1) * 128],
                     rele[0:32, :], reads=[relmap, rele], writes=[cm.psS])
                k.op("dve", lambda e, i=i: e.tensor_copy(
                    revp[:, i * 16:(i + 1) * 16], cm.psS[:, 0:16]),
                    reads=[cm.psS], writes=[revp])
                k.tr(cm.psS[0:16, 128:256], revp[:, i * 16:(i + 1) * 16],
                     cm.identf[:, :], reads=[revp, cm.identf],
                     writes=[cm.psS])
                k.op("dve", lambda e, i=i: e.tensor_copy(
                    revt[0:16, i * 128:(i + 1) * 128], cm.psS[0:16, 128:256]),
                    reads=[cm.psS], writes=[revt])
            k.dma("pool", revt_d.ap[:, :], revt[0:16, :],
                  reads=[revt], writes=[revt_d])
            for h in range(16):
                win = APc(revt_d.ap.tensor, h * 384, [[1, 128], [1, 256]])
                k.dma("sp", eps_t[:, :], win, reads=[revt_d], writes=[eps_t])
                k.mm(cm.ps0[:, 0:256], prev[:, :], eps_t[:, :],
                     reads=[prev, eps_t], writes=[cm.ps0])
                k.op("dve", lambda e, h=h: e.tensor_copy(
                    dall[:, h * 256:(h + 1) * 256], cm.ps0[:, 0:256]),
                    reads=[cm.ps0], writes=[dall])

        cmask = k.sb("l3_cmask", [128, 4 * 512], F32, ctx=st)
        k.dma("sp", cmask[:, :].rearrange("p (r m) -> p r m", r=4),
              fin["c_causal"][:, :, :].rearrange("r p m -> p r m"),
              writes=[cmask])

        qT = k.sb("l3_qT", [128, T], BF16, ctx=st)
        kT = k.sb("l3_kT", [128, T], BF16, ctx=st)
        v_sb = k.sb("l3_v", [128, NQT * 128], BF16, ctx=st)
        wblk = k.sb("l3_wblk", [128, NCH * 128], BF16, ctx=st)
        Ssb = k.sb("l3_S", [128, T], F32, ctx=st)
        Psb = k.sb("l3_P", [128, T], BF16, ctx=st)
        PTs = k.sb("l3_PT", [128, 512], BF16, ctx=st)
        nmax = k.sb("l3_nmax", [128, 1], F32, ctx=st)
        zrow = k.sb("l3_zrow", [128, 1], F32, ctx=st)
        rz = k.sb("l3_rz", [128, 1], F32, ctx=st)
        aog = k.sb("l3_aog", [128, T], BF16, ctx=st)
        for g2 in range(8):
            for dst, base, scale in ((qT, 0, 0.125), (kT, D, 1.0)):
                k.dma("sp", wblk[:, :].rearrange("p (c m) -> p c m", c=NCH),
                      wbf["waT"].ap[:, base + g2 * 128: base + (g2 + 1) * 128]
                      .rearrange("(c p) m -> p c m", p=128),
                      reads=[wbf["waT"]], writes=[wblk])
                for tt in range(NTT):
                    for c in range(NCH):
                        k.mm(cm.ps0[:, :],
                             wblk[:, c * 128:(c + 1) * 128],
                             xnT[:, c * T + tt * 512: c * T + (tt + 1) * 512],
                             reads=[wblk, xnT], writes=[cm.ps0],
                             start=(c == 0), stop=False)
                    k.mm(cm.ps0[:, :],
                         barow[0:1, base + g2 * 128: base + (g2 + 1) * 128],
                         cm.ones1[0:1, 0:512],
                         reads=[barow, cm.ones1], writes=[cm.ps0],
                         start=False, stop=True)
                    k.op("act", lambda e, tt=tt, dst=dst, scale=scale:
                         e.activation(
                             dst.ap[:, tt * 512:(tt + 1) * 512],
                             cm.ps0[:, :], AF.Copy, scale=scale),
                         reads=[cm.ps0], writes=[dst])
            k.dma("sp", wblk[:, :].rearrange("p (c m) -> p c m", c=NCH),
                  wbf["waT"].ap[:, 2 * D + g2 * 128: 2 * D + (g2 + 1) * 128]
                  .rearrange("(c p) m -> p c m", p=128),
                  reads=[wbf["waT"]], writes=[wblk])
            for kt in range(NQT):
                for c in range(NCH):
                    k.mm(cm.ps1[:, 0:128],
                         xnT[:, c * T + kt * 128: c * T + (kt + 1) * 128],
                         wblk[:, c * 128:(c + 1) * 128],
                         reads=[xnT, wblk], writes=[cm.ps1],
                         start=(c == 0), stop=False)
                k.mm(cm.ps1[:, 0:128], cm.ones1[0:1, 0:128],
                     barow[0:1, 2 * D + g2 * 128: 2 * D + (g2 + 1) * 128],
                     reads=[cm.ones1, barow], writes=[cm.ps1],
                     start=False, stop=True)
                k.op("dve", lambda e, kt=kt: e.tensor_copy(
                    v_sb[:, kt * 128:(kt + 1) * 128], cm.ps1[:, 0:128]),
                    reads=[cm.ps1], writes=[v_sb])
            for hh in range(2):
                h = g2 * 2 + hh
                r0 = hh * 64
                for qt in range(NQT):
                    nkt = qt // 4 + 1
                    L = nkt * 512
                    for j in range(nkt):
                        psx = cm.ps0 if j % 2 == 0 else cm.ps1
                        k.mm(psx[:, :],
                             qT[r0:r0 + 64, qt * 128:(qt + 1) * 128],
                             kT[r0:r0 + 64, j * 512:(j + 1) * 512],
                             reads=[qT, kT], writes=[psx],
                             start=True, stop=True)
                        if j == nkt - 1:
                            k.op("dve", lambda e, j=j, qt=qt, psx=psx:
                                 e.tensor_add(
                                     Ssb[:, j * 512:(j + 1) * 512],
                                     psx[:, :],
                                     cmask[:, (qt % 4) * 512:
                                           (qt % 4 + 1) * 512]),
                                 reads=[psx, cmask], writes=[Ssb])
                        else:
                            k.op("act", lambda e, j=j, psx=psx: e.activation(
                                Ssb[:, j * 512:(j + 1) * 512], psx[:, :],
                                AF.Copy),
                                reads=[psx], writes=[Ssb])
                    klo = qt * 128 - 128
                    c0 = -klo if klo < 0 else 0
                    klo = max(klo, 0)
                    wid = qt * 128 + 128 - klo
                    k.op("dve", lambda e, klo=klo, c0=c0, wid=wid, h=h:
                         e.tensor_add(
                             Ssb[:, klo:klo + wid], Ssb[:, klo:klo + wid],
                             dall[:, h * 256 + c0: h * 256 + c0 + wid]),
                         reads=[Ssb, dall], writes=[Ssb])
                    k.op("dve", lambda e, L=L: e.tensor_reduce(
                        nmax[:, :], Ssb[:, 0:L], AX.X, ALU.max, negate=True),
                        reads=[Ssb], writes=[nmax])
                    k.op("act", lambda e, L=L: e.activation(
                        Psb[:, 0:L], Ssb[:, 0:L], AF.Exp, bias=nmax[:, 0:1],
                        accum_out=zrow[:, :]),
                        reads=[Ssb, nmax], writes=[Psb, zrow])
                    k.op("dve", lambda e: e.reciprocal(rz[:, :], zrow[:, :]),
                         reads=[zrow], writes=[rz])
                    k.op("dve", lambda e, L=L: e.tensor_scalar(
                        Psb[:, 0:L], Psb[:, 0:L], rz[:, 0:1], None,
                        ALU.mult),
                        reads=[Psb, rz], writes=[Psb])
                    for jj in range(nkt):
                        for q4 in range(4):
                            k.tr(cm.psT[:, q4 * 128:(q4 + 1) * 128],
                                 Psb[:, jj * 512 + q4 * 128:
                                     jj * 512 + (q4 + 1) * 128],
                                 cm.ident[:, :],
                                 reads=[Psb, cm.ident], writes=[cm.psT])
                        eng_e = "dve" if jj % 2 == 0 else "act"
                        if eng_e == "dve":
                            k.op("dve", lambda e: e.tensor_copy(
                                PTs[:, :], cm.psT[:, 0:512]),
                                reads=[cm.psT], writes=[PTs])
                        else:
                            k.op("act", lambda e: e.activation(
                                PTs[:, :], cm.psT[:, 0:512], AF.Copy),
                                reads=[cm.psT], writes=[PTs])
                        for q4 in range(4):
                            kt = jj * 4 + q4
                            k.mm(cm.ps2[r0:r0 + 64, 0:128],
                                 v_sb[:, kt * 128 + r0: kt * 128 + r0 + 64],
                                 PTs[:, q4 * 128:(q4 + 1) * 128],
                                 reads=[v_sb, PTs], writes=[cm.ps2],
                                 start=(kt == 0), stop=(kt == nkt * 4 - 1))
                    k.op("dve", lambda e, r0=r0, qt=qt: e.tensor_copy(
                        aog[r0:r0 + 64, qt * 128:(qt + 1) * 128],
                        cm.ps2[r0:r0 + 64, 0:128]),
                        reads=[cm.ps2], writes=[aog])
            k.dma("pool", ao_dram.ap[g2 * 128:(g2 + 1) * 128, :],
                  aog[:, :], reads=[aog], writes=[ao_dram])
        aoblk = k.sb("l3_aoblk", [128, NCH * 512], BF16, ctx=st)
        sstage = k.sb("l3_sst", [128, 512], F32, ctx=st)
        for tt in range(NTT):
            tsl = slice(tt * 512, (tt + 1) * 512)
            k.dma("sp", aoblk[:, :].rearrange("p (c m) -> p c m", c=NCH),
                  ao_dram.ap[:, tsl].rearrange("(c p) m -> p c m", p=128),
                  reads=[ao_dram], writes=[aoblk])
            for mc in range(NCH):
                for c in range(NCH):
                    k.mm(cm.ps0[:, :],
                         wpT[:, c * D + mc * 128: c * D + (mc + 1) * 128],
                         aoblk[:, c * 512:(c + 1) * 512],
                         reads=[wpT, aoblk], writes=[cm.ps0],
                         start=(c == 0), stop=False)
                k.mm(cm.ps0[:, :], bprow[0:1, mc * 128:(mc + 1) * 128],
                     cm.ones1[0:1, 0:512],
                     reads=[bprow, cm.ones1], writes=[cm.ps0],
                     start=False, stop=True)
                k.op("dve", lambda e: e.tensor_copy(sstage[:, :],
                                                    cm.ps0[:, :]),
                     reads=[cm.ps0], writes=[sstage])
                k.dma("pool", s_dram.ap[mc * 128:(mc + 1) * 128, tsl],
                      sstage[:, :], reads=[sstage], writes=[s_dram])




def _rms_scale_row(k, cm, st, n, read_bf, srow, onesc):
    sqb = k.sb("rms_sqb", [128, 512], BF16, ctx=st)
    epsc = k.sb("rms_eps", [1, 1], F32, ctx=st)
    k.op("dve", lambda e: e.memset(epsc[:, :], 0.0), writes=[epsc])
    for tt in range(n // 512):
        for c in range(NCH):
            src_ap, src_tile = read_bf(c, tt)
            k.op("act", lambda e, a=src_ap: e.activation(
                sqb[:, :], a, AF.Square), reads=[src_tile], writes=[sqb])
            k.mm(cm.psR1[0:1, :], onesc[:, 0:1], sqb[:, :],
                 reads=[onesc, sqb], writes=[cm.psR1],
                 start=(c == 0), stop=(c == NCH - 1))
        k.op("dve", lambda e, tt=tt: e.tensor_copy(
            srow[0:1, tt * 512:(tt + 1) * 512], cm.psR1[0:1, :]),
            reads=[cm.psR1], writes=[srow])
    k.op("act", lambda e, n=n: e.activation(
        srow[0:1, 0:n], srow[0:1, 0:n], AF.Sqrt, bias=epsc[0:1, 0:1]),
        reads=[srow, epsc], writes=[srow])
    k.op("dve", lambda e, n=n: e.tensor_scalar_max(
        srow[0:1, 0:n], srow[0:1, 0:n], 1e-8), reads=[srow], writes=[srow])
    k.op("dve", lambda e, n=n: e.reciprocal(srow[0:1, 0:n], srow[0:1, 0:n]),
         reads=[srow], writes=[srow])
    k.op("dve", lambda e, n=n: e.tensor_scalar_mul(
        srow[0:1, 0:n], srow[0:1, 0:n], 32.0), reads=[srow], writes=[srow])


def emit_l3_ffn(k, cm, xnT, g_dram, wbf, fin, scratch):
    G, KK, M, E1, S1, E2, HID = 64, 64, 16, 8, 2, 4, 4096
    L2N = G * M
    xtok_dram = scratch["xtok_dram"]
    seq_dram = scratch["seq_dram"]
    with k.scope() as st:
        onesc = k.sb("l3f_ones", [128, 1], BF16, ctx=st)
        k.op("dve", lambda e: e.memset(onesc[:, :], 1.0), writes=[onesc])
        g1c = k.sb("l3f_g1c", [128, NCH], F32, ctx=st)
        gs1c = k.sb("l3f_gs1c", [128, NCH], F32, ctx=st)
        g2c = k.sb("l3f_g2c", [128, NCH], F32, ctx=st)
        gs2c = k.sb("l3f_gs2c", [128, NCH], F32, ctx=st)
        load_cols(k, g1c, fin["L3_ffn_gamma1"], NCH)
        load_cols(k, gs1c, fin["L3_ffn_gamma_slot1"], NCH)
        load_cols(k, g2c, fin["L3_ffn_gamma2"], NCH)
        load_cols(k, gs2c, fin["L3_ffn_gamma_slot2"], NCH)
        comb_all = k.sb("l3f_comb", [128, (G // 2) * 16], F32, ctx=st)
        gb128 = k.sb("l3f_gb128", [128, 128], F32, ctx=st)
        egcol = k.sb("l3f_egcol", [128, 8], F32, ctx=st)

        with k.scope() as stx:
            xts = k.sb("l3f_xts", [128, D], BF16, ctx=stx)
            for kt in range(NQT):
                for c in range(NCH):
                    k.tr(cm.psT[:, 0:128],
                         xnT[:, c * T + kt * 128: c * T + (kt + 1) * 128],
                         cm.ident[:, :],
                         reads=[xnT, cm.ident], writes=[cm.psT])
                    k.op("dve", lambda e, c=c: e.tensor_copy(
                        xts[:, c * 128:(c + 1) * 128], cm.psT[:, 0:128]),
                        reads=[cm.psT], writes=[xts])
                k.dma("pool", xtok_dram.ap[kt * 128:(kt + 1) * 128, :],
                      xts[:, :], reads=[xts], writes=[xtok_dram])

        st1 = contextlib.ExitStack()
        gmT = k.sb("l3f_gmT", [128, NCH * G], BF16, ctx=st1)
        gmTf = k.sb("l3f_gmTf", [128, G], F32, ctx=st1)
        gates_g = k.sb("l3f_gates", [128, 1], F32, ctx=st1)
        for c in range(NCH):
            k.op("dve", lambda e, c=c: e.tensor_reduce(
                gmTf[:, :],
                xnT.ap[:, c * T:(c + 1) * T].rearrange(
                    "p (g kk) -> p g kk", kk=KK),
                AX.X, ALU.add), reads=[xnT], writes=[gmTf])
            k.op("dve", lambda e, c=c: e.tensor_scalar(
                gmT[:, c * G:(c + 1) * G], gmTf[:, :], 1.0 / KK, None,
                ALU.mult), reads=[gmTf], writes=[gmT])
        wgcb = k.sb("l3f_wgcb", [128, NCH], BF16, ctx=st1)
        load_cols(k, wgcb, fin["L3_ffn_Wg"].rearrange("a b -> (a b)"), NCH)
        bgrow = k.sb("l3f_bgrow", [1, 1], BF16, ctx=st1)
        load_row_bf16(k, bgrow, fin["L3_ffn_bg"], 1)
        for c in range(NCH):
            k.mm(cm.psS[0:64, 0:1], gmT[:, c * G:(c + 1) * G],
                 wgcb[:, c:c + 1],
                 reads=[gmT, wgcb], writes=[cm.psS],
                 start=(c == 0), stop=False)
        k.mm(cm.psS[0:64, 0:1], cm.ones1[0:1, 0:64], bgrow[0:1, 0:1],
             reads=[cm.ones1, bgrow], writes=[cm.psS],
             start=False, stop=True)
        k.op("act", lambda e: e.activation(gates_g[0:64, 0:1],
                                           cm.psS[0:64, 0:1], AF.Sigmoid),
             reads=[cm.psS], writes=[gates_g])
        k.op("dve", lambda e: e.tensor_scalar(
            gates_g[0:64, 0:1], gates_g[0:64, 0:1], 0.01, 1.0,
            ALU.max, ALU.min),
            reads=[gates_g], writes=[gates_g])
        k.tr(cm.psS[0:1, 0:64], gates_g[0:64, 0:1], cm.identf[0:64, 0:64],
             reads=[gates_g, cm.identf], writes=[cm.psS])
        grow = k.sb("l3f_grow", [1, 128], F32, ctx=st1)
        k.op("dve", lambda e: e.tensor_copy(grow[0:1, 0:64],
                                            cm.psS[0:1, 0:64]),
             reads=[cm.psS], writes=[grow])
        k.op("dve", lambda e: e.tensor_copy(grow[0:1, 64:128],
                                            grow[0:1, 0:64]),
             reads=[grow], writes=[grow])
        k.mm(cm.ps0[:, 0:128], cm.ones1f[0:1, 0:128], grow[0:1, :],
             reads=[cm.ones1f, grow], writes=[cm.ps0])
        k.op("dve", lambda e: e.tensor_copy(gb128[:, :], cm.ps0[:, 0:128]),
             reads=[cm.ps0], writes=[gb128])
        egrow = k.sb("l3f_egrow", [1, 8], F32, ctx=st1)
        k.dma("sp", egrow[0:1, :], fin["L3_ffn_expert_gates"].unsqueeze(0),
              writes=[egrow])
        k.mm(cm.ps0[:, 0:8], cm.ones1f[0:1, 0:128], egrow[0:1, :],
             reads=[cm.ones1f, egrow], writes=[cm.ps0])
        k.op("dve", lambda e: e.tensor_copy(egcol[:, :], cm.ps0[:, 0:8]),
             reads=[cm.ps0], writes=[egcol])

        seT = k.sb("l3f_seT", [128, NCH * 1024], BF16, ctx=st1)
        w1blk = k.sb("l3f_w1blk", [128, NCH * 128], BF16, ctx=st1)
        serow = k.sb("l3f_serow", [1, 1024], F32, ctx=st1)
        seb = k.sb("l3f_seb", [128, 1024], F32, ctx=st1)
        for m in range(M):
            for ddc in range(NCH):
                k.dma("sp",
                      w1blk[:, :].rearrange("p (c m2) -> p c m2", c=NCH),
                      wbf["L3_ffn_W_slot1"].ap[:, m * D + ddc * 128:
                                               m * D + (ddc + 1) * 128]
                      .rearrange("(c p) m2 -> p c m2", p=128),
                      reads=[wbf["L3_ffn_W_slot1"]], writes=[w1blk])
                for c in range(NCH):
                    k.mm(cm.ps0[:, 0:G],
                         w1blk[:, c * 128:(c + 1) * 128],
                         gmT[:, c * G:(c + 1) * G],
                         reads=[w1blk, gmT], writes=[cm.ps0],
                         start=(c == 0), stop=(c == NCH - 1))
                k.op("dve", lambda e, m=m, ddc=ddc: e.tensor_copy(
                    seT.ap[:, ddc * 1024 + m: ddc * 1024 + m + 1009: 16],
                    cm.ps0[:, 0:G]),
                    reads=[cm.ps0], writes=[seT])
        _rms_scale_row(k, cm, st1, 1024,
                       lambda c, tt: (seT.ap[:, c * 1024 + tt * 512:
                                             c * 1024 + (tt + 1) * 512],
                                      seT),
                       serow, onesc)
        for tt2 in range(2):
            k.mm(cm.ps0[:, :], cm.ones1f[0:1, 0:128],
                 serow[0:1, tt2 * 512:(tt2 + 1) * 512],
                 reads=[cm.ones1f, serow], writes=[cm.ps0])
            k.op("dve", lambda e, tt2=tt2: e.tensor_copy(
                seb[:, tt2 * 512:(tt2 + 1) * 512], cm.ps0[:, :]),
                reads=[cm.ps0], writes=[seb])
        for c in range(NCH):
            k.op("dve", lambda e, c=c: e.scalar_tensor_tensor(
                seT[:, c * 1024:(c + 1) * 1024],
                seT[:, c * 1024:(c + 1) * 1024],
                gs1c[:, c:c + 1], seb[:, :], ALU.mult, ALU.mult),
                reads=[seT, gs1c, seb], writes=[seT])

        xnrow = k.sb("l3f_xnrow", [1, T], F32, ctx=st1)
        _rms_scale_row(k, cm, st1, T,
                       lambda c, tt: (xnT.ap[:, c * T + tt * 512:
                                             c * T + (tt + 1) * 512], xnT),
                       xnrow, onesc)
        slotsT = k.sb("l3f_slotsT", [128, NCH * 1024], BF16, ctx=st1)
        xn1p = k.sb("l3f_xn1p", [128, NCH * 128], BF16, ctx=st1)
        nsb = k.sb("l3f_nsb", [128, 128], F32, ctx=st1)
        xtokp = k.sb("l3f_xtokp", [128, D], BF16, ctx=st1)
        comb_bf = k.sb("l3f_combbf", [128, 16], BF16, ctx=st1)
        lgm = k.sb("l3f_lgm", [128, 1], F32, ctx=st1)
        lgz = k.sb("l3f_lgz", [128, 1], F32, ctx=st1)
        lgr = k.sb("l3f_lgr", [128, 1], F32, ctx=st1)
        for gp in range(G // 2):
            t0 = gp * 128
            k.mm(cm.ps1[:, 0:128], cm.ones1f[0:1, 0:128],
                 xnrow[0:1, t0:t0 + 128],
                 reads=[cm.ones1f, xnrow], writes=[cm.ps1])
            k.op("dve", lambda e: e.tensor_copy(nsb[:, :],
                                                cm.ps1[:, 0:128]),
                 reads=[cm.ps1], writes=[nsb])
            for c in range(NCH):
                k.op("dve", lambda e, c=c, t0=t0: e.scalar_tensor_tensor(
                    xn1p[:, c * 128:(c + 1) * 128],
                    xnT[:, c * T + t0: c * T + t0 + 128],
                    g1c[:, c:c + 1], nsb[:, :], ALU.mult, ALU.mult),
                    reads=[xnT, g1c, nsb], writes=[xn1p])
            k.dma("sp", xtokp[:, :], xtok_dram.ap[t0:t0 + 128, :],
                  reads=[xtok_dram], writes=[xtokp])
            for gg in range(2):
                g = gp * 2 + gg
                r0 = gg * 64
                csl = slice(gp * 16, (gp + 1) * 16)
                for c in range(NCH):
                    k.mm(cm.psS[r0:r0 + 64, 0:16],
                         xn1p[:, c * 128 + r0: c * 128 + r0 + 64],
                         seT[:, c * 1024 + g * 16: c * 1024 + (g + 1) * 16],
                         reads=[xn1p, seT], writes=[cm.psS],
                         start=(c == 0), stop=(c == NCH - 1))
                k.op("dve", lambda e, r0=r0, csl=csl: e.tensor_copy(
                    comb_all.ap[r0:r0 + 64, csl], cm.psS[r0:r0 + 64, 0:16]),
                    reads=[cm.psS], writes=[comb_all])
                k.op("dve", lambda e, r0=r0, csl=csl: e.tensor_reduce(
                    lgm[r0:r0 + 64, :], comb_all.ap[r0:r0 + 64, csl],
                    AX.X, ALU.max, negate=True),
                    reads=[comb_all], writes=[lgm])
                k.op("act", lambda e, r0=r0, csl=csl: e.activation(
                    comb_all.ap[r0:r0 + 64, csl],
                    comb_all.ap[r0:r0 + 64, csl], AF.Exp,
                    bias=lgm[r0:r0 + 64, 0:1],
                    accum_out=lgz[r0:r0 + 64, :]),
                    reads=[comb_all, lgm], writes=[comb_all, lgz])
                k.op("dve", lambda e, r0=r0: e.reciprocal(
                    lgr[r0:r0 + 64, :], lgz[r0:r0 + 64, :]),
                    reads=[lgz], writes=[lgr])
                k.op("dve", lambda e, r0=r0, csl=csl: e.tensor_scalar(
                    comb_all.ap[r0:r0 + 64, csl],
                    comb_all.ap[r0:r0 + 64, csl],
                    lgr[r0:r0 + 64, 0:1], None, ALU.mult),
                    reads=[comb_all, lgr], writes=[comb_all])
                k.op("dve", lambda e, r0=r0, csl=csl: e.tensor_copy(
                    comb_bf.ap[r0:r0 + 64, 0:16],
                    comb_all.ap[r0:r0 + 64, csl]),
                    reads=[comb_all], writes=[comb_bf])
                for ddc in range(NCH):
                    k.mm(cm.ps2[:, 0:16],
                         xtokp.ap[r0:r0 + 64, ddc * 128:(ddc + 1) * 128],
                         comb_bf.ap[r0:r0 + 64, 0:16],
                         reads=[xtokp, comb_bf], writes=[cm.ps2],
                         start=True, stop=True)
                    k.op("dve", lambda e, ddc=ddc, g=g: e.tensor_copy(
                        slotsT.ap[:, ddc * 1024 + g:
                                  ddc * 1024 + g + 961: 64],
                        cm.ps2[:, 0:16]),
                        reads=[cm.ps2], writes=[slotsT])

        b1f = k.sb("l3f_b1f", [1, HID], BF16, ctx=st1)
        b2f = k.sb("l3f_b2f", [1, D], BF16, ctx=st1)
        h1T = k.sb("l3f_h1T", [128, 32 * 128], BF16, ctx=st1)
        w2blk = k.sb("l3f_w2blk", [128, 32 * 128], BF16, ctx=st1)
        o1tmp = k.sb("l3f_o1tmp", [128, 128], BF16, ctx=st1)
        for e_ in range(E1):
            k.dma("pool", b1f[0:1, :],
                  fin["L3_ffn_b1_first"][e_].unsqueeze(0), writes=[b1f])
            k.dma("pool", b2f[0:1, :],
                  fin["L3_ffn_b2_first"][e_].unsqueeze(0), writes=[b2f])
            for hc in range(32):
                k.dma("sp",
                      w1blk[:, :].rearrange("p (c m2) -> p c m2", c=NCH),
                      wbf["L3_ffn_W1_first"].ap[e_][:, hc * 128:
                                                    (hc + 1) * 128]
                      .rearrange("(c p) m2 -> p c m2", p=128),
                      reads=[wbf["L3_ffn_W1_first"]], writes=[w1blk])
                for c in range(NCH):
                    k.mm(cm.ps0[:, 0:128],
                         w1blk[:, c * 128:(c + 1) * 128],
                         slotsT[:, c * 1024 + e_ * 128:
                                c * 1024 + (e_ + 1) * 128],
                         reads=[w1blk, slotsT], writes=[cm.ps0],
                         start=(c == 0), stop=False)
                k.mm(cm.ps0[:, 0:128],
                     b1f[0:1, hc * 128:(hc + 1) * 128],
                     cm.ones1[0:1, 0:128],
                     reads=[b1f, cm.ones1], writes=[cm.ps0],
                     start=False, stop=True)
                k.op("act", lambda e, hc=hc: e.activation(
                    h1T[:, hc * 128:(hc + 1) * 128], cm.ps0[:, 0:128],
                    AF.Gelu), reads=[cm.ps0], writes=[h1T])
            for ddc in range(NCH):
                k.dma("sp",
                      w2blk[:, :].rearrange("p (hc m2) -> p hc m2", hc=32),
                      wbf["L3_ffn_W2_first"].ap[e_][:, ddc * 128:
                                                    (ddc + 1) * 128]
                      .rearrange("(hc p) m2 -> p hc m2", p=128),
                      reads=[wbf["L3_ffn_W2_first"]], writes=[w2blk])
                for hc in range(32):
                    k.mm(cm.ps1[:, 0:128],
                         w2blk[:, hc * 128:(hc + 1) * 128],
                         h1T[:, hc * 128:(hc + 1) * 128],
                         reads=[w2blk, h1T], writes=[cm.ps1],
                         start=(hc == 0), stop=False)
                k.mm(cm.ps1[:, 0:128],
                     b2f[0:1, ddc * 128:(ddc + 1) * 128],
                     cm.ones1[0:1, 0:128],
                     reads=[b2f, cm.ones1], writes=[cm.ps1],
                     start=False, stop=True)
                k.op("dve", lambda e, e_=e_: e.scalar_tensor_tensor(
                    o1tmp[:, :], cm.ps1[:, 0:128], egcol[:, e_:e_ + 1],
                    gb128[:, :], ALU.mult, ALU.mult),
                    reads=[cm.ps1, egcol, gb128], writes=[o1tmp])
                k.dma("pool",
                      seq_dram.ap[ddc * 128:(ddc + 1) * 128,
                                  e_ * 128:(e_ + 1) * 128],
                      o1tmp[:, :], reads=[o1tmp], writes=[seq_dram])
        k.dump("l3f_comb", comb_all.ap[:, :], [128, (G // 2) * 16],
               [comb_all])
        k.dump("l3f_slotsT", slotsT.ap[:, :], [128, NCH * 1024], [slotsT],
               dt=BF16)
        k.dump("l3f_gb", gb128.ap[:, :], [128, 128], [gb128])
        k.dump("l3f_seq", seq_dram.ap[:, :], [D, L2N], [seq_dram], dt=BF16)
        k.barrier()
        st1.close()

        xsT = k.sb("l3f_xsT", [128, NCH * L2N], BF16, ctx=st)
        comb2R = k.sb("l3f_c2R", [128, 8 * L2N], BF16, ctx=st)
        dispT = k.sb("l3f_dispT", [128, 8 * L2N], BF16, ctx=st)
        lgm2 = k.sb("l3f_lgm2", [128, 1], F32, ctx=st)
        lgz2 = k.sb("l3f_lgz2", [128, 1], F32, ctx=st)
        lgr2 = k.sb("l3f_lgr2", [128, 1], F32, ctx=st)
        stD = contextlib.ExitStack()
        se2T = k.sb("l3f_se2T", [128, NCH * L2N], BF16, ctx=stD)
        cmeanT = k.sb("l3f_cmean", [128, NCH * 256], BF16, ctx=stD)
        st2 = contextlib.ExitStack()
        xsrow = k.sb("l3f_xsrow", [1, L2N], F32, ctx=st2)
        xsb = k.sb("l3f_xsb", [128, L2N], F32, ctx=st2)
        seqb = k.sb("l3f_seqb", [128, 512], BF16, ctx=st2)

        def seq_read(c, tt):
            k.dma("sp", seqb[:, :],
                  seq_dram.ap[c * 128:(c + 1) * 128,
                              tt * 512:(tt + 1) * 512],
                  reads=[seq_dram], writes=[seqb])
            return seqb.ap[:, :], seqb
        _rms_scale_row(k, cm, st2, L2N, seq_read, xsrow, onesc)
        for tt2 in range(2):
            k.mm(cm.ps0[:, :], cm.ones1f[0:1, 0:128],
                 xsrow[0:1, tt2 * 512:(tt2 + 1) * 512],
                 reads=[cm.ones1f, xsrow], writes=[cm.ps0])
            k.op("dve", lambda e, tt2=tt2: e.tensor_copy(
                xsb[:, tt2 * 512:(tt2 + 1) * 512], cm.ps0[:, :]),
                reads=[cm.ps0], writes=[xsb])
        for c in range(NCH):
            k.dma("sp", seqb[:, :], seq_dram.ap[c * 128:(c + 1) * 128, 0:512],
                  reads=[seq_dram], writes=[seqb])
            k.op("dve", lambda e, c=c: e.scalar_tensor_tensor(
                xsT[:, c * L2N: c * L2N + 512], seqb[:, :],
                g2c[:, c:c + 1], xsb[:, 0:512], ALU.mult, ALU.mult),
                reads=[seqb, g2c, xsb], writes=[xsT])
            k.dma("sp", seqb[:, :],
                  seq_dram.ap[c * 128:(c + 1) * 128, 512:1024],
                  reads=[seq_dram], writes=[seqb])
            k.op("dve", lambda e, c=c: e.scalar_tensor_tensor(
                xsT[:, c * L2N + 512: c * L2N + 1024], seqb[:, :],
                g2c[:, c:c + 1], xsb[:, 512:1024], ALU.mult, ALU.mult),
                reads=[seqb, g2c, xsb], writes=[xsT])
        cmf = k.sb("l3f_cmf", [128, 256], F32, ctx=st2)
        for c in range(NCH):
            k.op("dve", lambda e, c=c: e.tensor_reduce(
                cmf[:, :],
                APc3(xsT, c * L2N),
                AX.X, ALU.add), reads=[xsT], writes=[cmf])
            k.op("dve", lambda e, c=c: e.tensor_scalar(
                cmeanT[:, c * 256:(c + 1) * 256], cmf[:, :], 0.25, None,
                ALU.mult), reads=[cmf], writes=[cmeanT])
        w1blk2 = k.sb("l3f_w1blk2", [128, NCH * 128], BF16, ctx=st2)
        se2row = k.sb("l3f_se2row", [1, L2N], F32, ctx=st2)
        se2b = k.sb("l3f_se2b", [128, L2N], F32, ctx=st2)
        for e2 in range(E2):
            for ddc in range(NCH):
                k.dma("sp",
                      w1blk2[:, :].rearrange("p (c m2) -> p c m2", c=NCH),
                      wbf["L3_ffn_W_slot2"].ap[:, e2 * D + ddc * 128:
                                               e2 * D + (ddc + 1) * 128]
                      .rearrange("(c p) m2 -> p c m2", p=128),
                      reads=[wbf["L3_ffn_W_slot2"]], writes=[w1blk2])
                for c in range(NCH):
                    k.mm(cm.ps0[:, 0:256],
                         w1blk2[:, c * 128:(c + 1) * 128],
                         cmeanT[:, c * 256:(c + 1) * 256],
                         reads=[w1blk2, cmeanT], writes=[cm.ps0],
                         start=(c == 0), stop=(c == NCH - 1))
                k.op("dve", lambda e, e2=e2, ddc=ddc: e.tensor_copy(
                    se2T[:, ddc * L2N + e2 * 256: ddc * L2N + (e2 + 1) * 256],
                    cm.ps0[:, 0:256]),
                    reads=[cm.ps0], writes=[se2T])
        _rms_scale_row(k, cm, st2, L2N,
                       lambda c, tt: (se2T.ap[:, c * L2N + tt * 512:
                                              c * L2N + (tt + 1) * 512],
                                      se2T),
                       se2row, onesc)
        for tt2 in range(2):
            k.mm(cm.ps0[:, :], cm.ones1f[0:1, 0:128],
                 se2row[0:1, tt2 * 512:(tt2 + 1) * 512],
                 reads=[cm.ones1f, se2row], writes=[cm.ps0])
            k.op("dve", lambda e, tt2=tt2: e.tensor_copy(
                se2b[:, tt2 * 512:(tt2 + 1) * 512], cm.ps0[:, :]),
                reads=[cm.ps0], writes=[se2b])
        for c in range(NCH):
            k.op("dve", lambda e, c=c: e.scalar_tensor_tensor(
                se2T[:, c * L2N:(c + 1) * L2N],
                se2T[:, c * L2N:(c + 1) * L2N],
                gs2c[:, c:c + 1], se2b[:, :], ALU.mult, ALU.mult),
                reads=[se2T, gs2c, se2b], writes=[se2T])
        k.barrier()
        st2.close()

        st3 = contextlib.ExitStack()
        l2x = k.sb("l3f_l2x", [128, 8 * L2N], F32, ctx=st3)
        pbf = k.sb("l3f_pbf", [128, L2N], BF16, ctx=st3)
        for lt in range(8):
            for est2 in range(2):
                for c in range(NCH):
                    k.mm(cm.ps0[:, :],
                         xsT[:, c * L2N + lt * 128: c * L2N + (lt + 1) * 128],
                         se2T[:, c * L2N + est2 * 512:
                              c * L2N + (est2 + 1) * 512],
                         reads=[xsT, se2T], writes=[cm.ps0],
                         start=(c == 0), stop=(c == NCH - 1))
                k.op("dve", lambda e, lt=lt, est2=est2: e.tensor_copy(
                    l2x[:, lt * L2N + est2 * 512:
                        lt * L2N + (est2 + 1) * 512], cm.ps0[:, :]),
                    reads=[cm.ps0], writes=[l2x])
            lsl = slice(lt * L2N, (lt + 1) * L2N)
            k.op("dve", lambda e, lsl=lsl: e.tensor_reduce(
                lgm2[:, :], l2x.ap[:, lsl], AX.X, ALU.max, negate=True),
                reads=[l2x], writes=[lgm2])
            k.op("act", lambda e, lsl=lsl: e.activation(
                pbf[:, :], l2x.ap[:, lsl], AF.Exp, bias=lgm2[:, 0:1],
                accum_out=lgz2[:, :]),
                reads=[l2x, lgm2], writes=[pbf, lgz2])
            k.op("dve", lambda e: e.reciprocal(lgr2[:, :], lgz2[:, :]),
                 reads=[lgz2], writes=[lgr2])
            k.op("dve", lambda e: e.tensor_scalar(
                pbf[:, :], pbf[:, :], lgr2[:, 0:1], None, ALU.mult),
                reads=[pbf, lgr2], writes=[pbf])
            for e128 in range(8):
                k.tr(cm.psT[:, 0:128],
                     pbf[:, e128 * 128:(e128 + 1) * 128], cm.ident[:, :],
                     reads=[pbf, cm.ident], writes=[cm.psT])
                k.op("dve", lambda e, e128=e128, lt=lt: e.tensor_copy(
                    comb2R[:, e128 * L2N + lt * 128:
                           e128 * L2N + (lt + 1) * 128], cm.psT[:, 0:128]),
                    reads=[cm.psT], writes=[comb2R])
        for est in range(8):
            for lt2 in range(2):
                for c in range(NCH):
                    k.mm(cm.ps0[:, :],
                         se2T[:, c * L2N + est * 128:
                              c * L2N + (est + 1) * 128],
                         xsT[:, c * L2N + lt2 * 512:
                             c * L2N + (lt2 + 1) * 512],
                         reads=[se2T, xsT], writes=[cm.ps0],
                         start=(c == 0), stop=(c == NCH - 1))
                k.op("dve", lambda e, est=est, lt2=lt2: e.tensor_copy(
                    l2x[:, est * L2N + lt2 * 512:
                        est * L2N + (lt2 + 1) * 512], cm.ps0[:, :]),
                    reads=[cm.ps0], writes=[l2x])
            esl = slice(est * L2N, (est + 1) * L2N)
            k.op("dve", lambda e, esl=esl: e.tensor_reduce(
                lgm2[:, :], l2x.ap[:, esl], AX.X, ALU.max, negate=True),
                reads=[l2x], writes=[lgm2])
            k.op("act", lambda e, esl=esl: e.activation(
                pbf[:, :], l2x.ap[:, esl], AF.Exp, bias=lgm2[:, 0:1],
                accum_out=lgz2[:, :]),
                reads=[l2x, lgm2], writes=[pbf, lgz2])
            k.op("dve", lambda e: e.reciprocal(lgr2[:, :], lgz2[:, :]),
                 reads=[lgz2], writes=[lgr2])
            k.op("dve", lambda e: e.tensor_scalar(
                pbf[:, :], pbf[:, :], lgr2[:, 0:1], None, ALU.mult),
                reads=[pbf, lgr2], writes=[pbf])
            for lt in range(8):
                k.tr(cm.psT[:, 0:128],
                     pbf[:, lt * 128:(lt + 1) * 128], cm.ident[:, :],
                     reads=[pbf, cm.ident], writes=[cm.psT])
                k.op("dve", lambda e, lt=lt, est=est: e.tensor_copy(
                    dispT[:, lt * L2N + est * 128:
                          lt * L2N + (est + 1) * 128], cm.psT[:, 0:128]),
                    reads=[cm.psT], writes=[dispT])
        k.dump("l3f_xsT", xsT.ap[:, :], [128, NCH * L2N], [xsT], dt=BF16)
        k.dump("l3f_cmean", cmeanT.ap[:, :], [128, NCH * 256], [cmeanT],
               dt=BF16)
        k.dump("l3f_se2T", se2T.ap[:, :], [128, NCH * L2N], [se2T], dt=BF16)
        k.dump("l3f_c2R", comb2R.ap[:, :], [128, 8 * L2N], [comb2R], dt=BF16)
        k.dump("l3f_disp", dispT.ap[:, :], [128, 8 * L2N], [dispT], dt=BF16)
        k.barrier()
        st3.close()
        stD.close()

        st4 = contextlib.ExitStack()
        xst_a = k.sb("l3f_xsta", [128, D], BF16, ctx=st4)
        xst_b = k.sb("l3f_xstb", [128, D], BF16, ctx=st4)
        slots2T = k.sb("l3f_s2T", [128, NCH * L2N], BF16, ctx=st4)
        for lt in range(8):
            xb = xst_a if lt % 2 == 0 else xst_b
            for c in range(NCH):
                k.tr(cm.psT[:, 0:128],
                     xsT[:, c * L2N + lt * 128: c * L2N + (lt + 1) * 128],
                     cm.ident[:, :], reads=[xsT, cm.ident], writes=[cm.psT])
                k.op("dve", lambda e, c=c, xb=xb: e.tensor_copy(
                    xb.ap[:, c * 128:(c + 1) * 128], cm.psT[:, 0:128]),
                    reads=[cm.psT], writes=[xb])
            k.dma("pool", xtok_dram.ap[lt * 128:(lt + 1) * 128, :],
                  xb[:, :], reads=[xb], writes=[xtok_dram])
        for ddc in range(NCH):
            for est2 in range(2):
                for lt in range(8):
                    xb = xst_a if lt % 2 == 0 else xst_b
                    k.dma("sp", xb[:, :],
                          xtok_dram.ap[lt * 128:(lt + 1) * 128, :],
                          reads=[xtok_dram], writes=[xb])
                    k.mm(cm.ps0[:, :],
                         xb.ap[:, ddc * 128:(ddc + 1) * 128],
                         dispT[:, lt * L2N + est2 * 512:
                               lt * L2N + (est2 + 1) * 512],
                         reads=[xb, dispT], writes=[cm.ps0],
                         start=(lt == 0), stop=(lt == 7))
                k.op("dve", lambda e, ddc=ddc, est2=est2: e.tensor_copy(
                    slots2T[:, ddc * L2N + est2 * 512:
                            ddc * L2N + (est2 + 1) * 512], cm.ps0[:, :]),
                    reads=[cm.ps0], writes=[slots2T])
        b1s = k.sb("l3f_b1s", [1, 128], BF16, ctx=st4)
        b2s = k.sb("l3f_b2s", [1, D], BF16, ctx=st4)
        h2T = k.sb("l3f_h2T", [128, 32 * 256], BF16, ctx=st4)
        w1b2 = k.sb("l3f_w1b2", [128, NCH * 128], BF16, ctx=st4)
        w2b2 = k.sb("l3f_w2b2", [128, 32 * 128], BF16, ctx=st4)
        o2_tok = k.sb("l3f_o2tok", [128, 8 * D], BF16, ctx=st4)
        o2tmp = k.sb("l3f_o2tmp", [128, 256], BF16, ctx=st4)
        for e2 in range(E2):
            k.dma("pool", b2s[0:1, :],
                  fin["L3_ffn_b2_second"][e2].unsqueeze(0), writes=[b2s])
            for hc in range(32):
                k.dma("pool", b1s[0:1, :],
                      fin["L3_ffn_b1_second"][e2][hc * 128:(hc + 1) * 128]
                      .unsqueeze(0), writes=[b1s])
                k.dma("sp",
                      w1b2[:, :].rearrange("p (c m2) -> p c m2", c=NCH),
                      wbf["L3_ffn_W1_second"].ap[e2][:, hc * 128:
                                                     (hc + 1) * 128]
                      .rearrange("(c p) m2 -> p c m2", p=128),
                      reads=[wbf["L3_ffn_W1_second"]], writes=[w1b2])
                for c in range(NCH):
                    k.mm(cm.ps0[:, 0:256],
                         w1b2[:, c * 128:(c + 1) * 128],
                         slots2T[:, c * L2N + e2 * 256:
                                 c * L2N + (e2 + 1) * 256],
                         reads=[w1b2, slots2T], writes=[cm.ps0],
                         start=(c == 0), stop=False)
                k.mm(cm.ps0[:, 0:256],
                     b1s[0:1, 0:128],
                     cm.ones1[0:1, 0:256],
                     reads=[b1s, cm.ones1], writes=[cm.ps0],
                     start=False, stop=True)
                k.op("act", lambda e, hc=hc: e.activation(
                    h2T[:, hc * 256:(hc + 1) * 256], cm.ps0[:, 0:256],
                    AF.Gelu), reads=[cm.ps0], writes=[h2T])
            for ddc in range(NCH):
                k.dma("sp",
                      w2b2[:, :].rearrange("p (hc m2) -> p hc m2", hc=32),
                      wbf["L3_ffn_W2_second"].ap[e2][:, ddc * 128:
                                                     (ddc + 1) * 128]
                      .rearrange("(hc p) m2 -> p hc m2", p=128),
                      reads=[wbf["L3_ffn_W2_second"]], writes=[w2b2])
                for hc in range(32):
                    k.mm(cm.ps1[:, 0:256],
                         w2b2[:, hc * 128:(hc + 1) * 128],
                         h2T[:, hc * 256:(hc + 1) * 256],
                         reads=[w2b2, h2T], writes=[cm.ps1],
                         start=(hc == 0), stop=False)
                k.mm(cm.ps1[:, 0:256],
                     b2s[0:1, ddc * 128:(ddc + 1) * 128],
                     cm.ones1[0:1, 0:256],
                     reads=[b2s, cm.ones1], writes=[cm.ps1],
                     start=False, stop=True)
                k.op("dve", lambda e: e.tensor_copy(o2tmp[:, :],
                                                    cm.ps1[:, 0:256]),
                     reads=[cm.ps1], writes=[o2tmp])
                for ns in range(2):
                    k.tr(cm.psT[:, 0:128],
                         o2tmp[:, ns * 128:(ns + 1) * 128], cm.ident[:, :],
                         reads=[o2tmp, cm.ident], writes=[cm.psT])
                    esc = e2 * 2 + ns
                    k.op("dve", lambda e, esc=esc, ddc=ddc: e.tensor_copy(
                        o2_tok[:, esc * D + ddc * 128:
                               esc * D + (ddc + 1) * 128],
                        cm.psT[:, 0:128]),
                        reads=[cm.psT], writes=[o2_tok])
        out2g = k.sb("l3f_out2g", [128, NCH * L2N], BF16, ctx=st4)
        for ddc in range(NCH):
            for lt2 in range(2):
                for esc in range(8):
                    k.mm(cm.ps0[:, :],
                         o2_tok[:, esc * D + ddc * 128:
                                esc * D + (ddc + 1) * 128],
                         comb2R[:, esc * L2N + lt2 * 512:
                                esc * L2N + (lt2 + 1) * 512],
                         reads=[o2_tok, comb2R], writes=[cm.ps0],
                         start=(esc == 0), stop=(esc == 7)),
                k.op("dve", lambda e, ddc=ddc, lt2=lt2: e.tensor_copy(
                    out2g.ap[:, ddc * L2N:(ddc + 1) * L2N]
                    .rearrange("p (g m) -> p g m", m=16)
                    [:, :, lt2 * 8:(lt2 + 1) * 8],
                    cm.ps0.ap[:, :].rearrange("p (m g) -> p g m", g=64)),
                    reads=[cm.ps0], writes=[out2g])
        k.dump("l3f_out2g", out2g.ap[:, :], [128, NCH * L2N], [out2g],
               dt=BF16)
        combT = k.sb("l3f_combT", [16, 64], BF16, ctx=st4)
        o2s = k.sb("l3f_o2s", [16, 128], BF16, ctx=st4)
        gsg = k.sb("l3f_gsg", [128, NCH * 64], BF16, ctx=st4)
        for g in range(G):
            gp, gg = g // 2, g % 2
            r0 = gg * 64
            k.tr(cm.psS[0:16, 0:64],
                 comb_all.ap[r0:r0 + 64, gp * 16:(gp + 1) * 16],
                 cm.identf[r0:r0 + 64, r0:r0 + 64],
                 reads=[comb_all, cm.identf], writes=[cm.psS])
            k.op("dve", lambda e: e.tensor_copy(combT[0:16, :],
                                                cm.psS[0:16, 0:64]),
                 reads=[cm.psS], writes=[combT])
            for ddc in range(NCH):
                k.tr(cm.psT[0:16, 0:128],
                     out2g[:, ddc * L2N + g * 16: ddc * L2N + (g + 1) * 16],
                     cm.ident[:, :],
                     reads=[out2g, cm.ident], writes=[cm.psT])
                k.op("dve", lambda e: e.tensor_copy(o2s[0:16, :],
                                                    cm.psT[0:16, 0:128]),
                     reads=[cm.psT], writes=[o2s])
                k.mm(cm.ps1[:, 0:64], o2s[0:16, :], combT[0:16, :],
                     reads=[o2s, combT], writes=[cm.ps1],
                     start=True, stop=True)
                k.op("act", lambda e, ddc=ddc: e.activation(
                    gsg[:, ddc * 64:(ddc + 1) * 64], cm.ps1[:, 0:64],
                    AF.Silu), reads=[cm.ps1], writes=[gsg])
            k.dma("pool",
                  g_dram.ap[:, g * 64:(g + 1) * 64]
                  .rearrange("(c p) m2 -> p c m2", p=128),
                  gsg[:, :].rearrange("p (c m2) -> p c m2", c=NCH),
                  reads=[gsg], writes=[g_dram])
        k.barrier()
        st4.close()


def APc3(xsT, off):
    from concourse.ap import AP as _AP
    a = xsT.ap[:, :]
    return _AP(a.tensor, a.offset + off,
               [list(a.ap[0]), [1, 64], [256, 4], [64, 4]])



def emit_combine(k, cm, xin_dram, s_dram, sG, g_dram, xout_dram):
    with k.scope() as st:
        xc = k.sb("cb_x", [128, T], F32, ctx=st)
        sc = k.sb("cb_s", [128, T], F32, ctx=st)
        gc = k.sb("cb_g", [128, T], BF16, ctx=st)
        for c in range(NCH):
            k.dma("sp", xc[:, :], xin_dram[c * 128:(c + 1) * 128, :],
                  writes=[xc])
            k.dma("sp", sc[:, :], s_dram.ap[c * 128:(c + 1) * 128, :],
                  reads=[s_dram], writes=[sc])
            k.dma("sp", gc[:, :], g_dram.ap[c * 128:(c + 1) * 128, :],
                  reads=[g_dram], writes=[gc])
            if sG is not None:
                k.op("dve", lambda e, c=c: e.tensor_copy(
                    sc[:, GPOS[0]::GSTEP][:, 0:8],
                    sG[:, c * 8:(c + 1) * 8]),
                    reads=[sG], writes=[sc])
            k.op("dve", lambda e: e.tensor_mul(sc[:, :], sc[:, :], gc[:, :]),
                 reads=[sc, gc], writes=[sc])
            k.op("dve", lambda e: e.tensor_add(sc[:, :], sc[:, :], xc[:, :]),
                 reads=[sc, xc], writes=[sc])
            k.dma("pool", xout_dram[c * 128:(c + 1) * 128, :], sc[:, :],
                  reads=[sc], writes=[])



L0_CAST = ["L0_attn_Wq", "L0_attn_Wk", "L0_attn_Wv", "L0_attn_Wo",
           "L0_ffn_router_w", "L0_ffn_w1", "L0_ffn_w2", "L0_ffn_sw1",
           "L0_ffn_sw2"]

L1_CAST = ["L1_attn_Wk", "L1_attn_Wv", "L1_attn_Wr", "L1_attn_Wo",
           "L1_attn_Fk", "L1_attn_Fv", "L1_attn_Fr", "L1_ffn_wi",
           "L1_ffn_wo", "L1_ffn_lora_A", "L1_ffn_lora_B"]

L2_CAST = ["L2_attn_Wq", "L2_attn_Wk", "L2_attn_Wv", "L2_attn_Wo",
           "L2_ffn_wg", "L2_ffn_wu", "L2_ffn_wd"]

L3_CAST = ["L3_attn_W_attn", "L3_attn_W_proj", "L3_ffn_W_slot1",
           "L3_ffn_W1_first", "L3_ffn_W2_first", "L3_ffn_W_slot2",
           "L3_ffn_W1_second", "L3_ffn_W2_second"]

LAYER_CAST = {0: L0_CAST, 1: L1_CAST, 2: L2_CAST, 3: L3_CAST}

INPUT_SPECS = None

WSHAPES = {
    "L0_attn_Wq": [D, D], "L0_attn_bq": [D], "L0_attn_Wk": [D, D],
    "L0_attn_bk": [D], "L0_attn_Wv": [D, D], "L0_attn_bv": [D],
    "L0_attn_Wo": [D, D], "L0_attn_bo": [D],
    "L0_ffn_router_w": [D, 8], "L0_ffn_router_b": [8],
    "L0_ffn_w1": [8, D, 2048], "L0_ffn_b1": [8, 2048],
    "L0_ffn_w2": [8, 2048, D], "L0_ffn_b2": [8, D],
    "L0_ffn_sw1": [2, D, 2048], "L0_ffn_sb1": [2, 2048],
    "L0_ffn_sw2": [2, 2048, D], "L0_ffn_sb2": [2, D],
    "L1_attn_ln1_g": [D], "L1_attn_ln1_b": [D],
    "L1_attn_ln2_g": [D], "L1_attn_ln2_b": [D],
    "L1_attn_att_time_mix": [1, 1, D], "L1_attn_att_combined_mix": [1, 1, D],
    "L1_attn_time_decay": [D], "L1_attn_time_first": [D],
    "L1_attn_Wk": [D, D], "L1_attn_Wv": [D, D], "L1_attn_Wr": [D, D],
    "L1_attn_Wo": [D, D],
    "L1_attn_ffn_time_mix": [1, 1, D], "L1_attn_ffn_combined_mix": [1, 1, D],
    "L1_attn_Fk": [4096, D], "L1_attn_Fv": [D, 4096], "L1_attn_Fr": [D, D],
    "L1_ffn_wi": [4096, D], "L1_ffn_bi": [4096], "L1_ffn_wo": [D, 4096],
    "L1_ffn_bo": [D], "L1_ffn_lora_A": [8, 16, D],
    "L1_ffn_lora_B": [8, 4096, 16], "L1_ffn_gate_w": [8, D],
    "L1_ffn_gate_b": [8],
    "L2_attn_Wq": [2, 128, D], "L2_attn_Wk": [128, D], "L2_attn_Wv": [128, D],
    "L2_attn_Wo": [D, 256],
    "L2_ffn_gate_w": [8, D], "L2_ffn_wg": [8, D, 256],
    "L2_ffn_wu": [8, D, 256], "L2_ffn_wd": [8, 256, D],
    "L3_attn_W_attn": [3072, D], "L3_attn_b_attn": [3072],
    "L3_attn_W_proj": [D, D], "L3_attn_b_proj": [D],
    "L3_attn_rel_emb": [32, 16],
    "L3_ffn_gamma1": [D], "L3_ffn_W_slot1": [D, 16384],
    "L3_ffn_gamma_slot1": [D],
    "L3_ffn_W1_first": [8, D, 4096], "L3_ffn_b1_first": [8, 4096],
    "L3_ffn_W2_first": [8, 4096, D], "L3_ffn_b2_first": [8, D],
    "L3_ffn_expert_gates": [8], "L3_ffn_Wg": [D, 1], "L3_ffn_bg": [1],
    "L3_ffn_gamma2": [D], "L3_ffn_W_slot2": [D, 4096],
    "L3_ffn_gamma_slot2": [D],
    "L3_ffn_W1_second": [4, D, 4096], "L3_ffn_b1_second": [4, 4096],
    "L3_ffn_W2_second": [4, 4096, D], "L3_ffn_b2_second": [4, D],
}

LAYER_INPUTS = {
    i: [n for n in WSHAPES if n.startswith("L%d_" % i)] for i in range(4)
}


def emit_layer(k, cm, lid, xin_dram, xout_dram, fin, wbf, scratch):
    s_dram = scratch["s_dram"]
    g_dram = scratch["g_dram"]
    v_scratch = scratch["v_scratch"]
    with k.scope() as lst:
        xnT = k.sb("xnT", [128, NCH * T], BF16, ctx=lst)
        routers = []
        if lid == 0:
            rw32 = k.sb("l0_rw32", [128, NCH * 8], F32, ctx=lst)
            k.dma("sp", rw32[:, :].rearrange("p (c m) -> p c m", c=NCH),
                  fin["L0_ffn_router_w"].rearrange("(c p) m -> p c m", p=128),
                  writes=[rw32])
            rb32 = k.sb("l0_rb32", [1, 8], F32, ctx=lst)
            k.dma("sp", rb32[0:1, :], fin["L0_ffn_router_b"].unsqueeze(0),
                  writes=[rb32])
            gates = k.sb("l0_gates", [128, NQT * 8], F32, ctx=lst)
            routers.append(dict(rw=rw32, rb=rb32, ne=8, out=gates))
        if lid in (1, 2):
            gw_name = "L%d_ffn_gate_w" % lid
            rw32 = k.sb("l1_rw32", [128, NCH * 8], F32, ctx=lst)
            gwsb = k.sb("l1_gwsb", [8, D], F32, ctx=lst)
            k.dma("sp", gwsb[0:8, :], fin[gw_name][:, :], writes=[gwsb])
            for c in range(NCH):
                k.tr(cm.psS[0:128, 0:8],
                     gwsb[0:8, c * 128:(c + 1) * 128], cm.identf[0:8, 0:8],
                     reads=[gwsb, cm.identf], writes=[cm.psS])
                k.op("dve", lambda e, c=c: e.tensor_copy(
                    rw32[:, c * 8:(c + 1) * 8], cm.psS[0:128, 0:8]),
                    reads=[cm.psS], writes=[rw32])
            rb32 = None
            if lid == 1:
                rb32 = k.sb("l1_rb32", [1, 8], F32, ctx=lst)
                k.dma("sp", rb32[0:1, :], fin["L1_ffn_gate_b"].unsqueeze(0),
                      writes=[rb32])
            gates = k.sb("l1_gates", [128, NQT * 8], F32, ctx=lst)
            routers.append(dict(rw=rw32, rb=rb32, ne=8, out=gates))
        emit_ln_ff(k, cm, xnT, xin_dram=xin_dram, routers=routers)
        if lid == 0:
            sG = k.sb("l0_sG", [128, NCH * 8], F32, ctx=lst)
            emit_l0_attn(k, cm, xnT, s_dram, sG, wbf, fin, v_scratch,
                         scratch["sg_dram"], scratch["png_dram"])
            emit_l0_ffn(k, cm, xnT, g_dram, wbf, fin, gates)
        elif lid == 1:
            with k.scope() as stt_:
                for nm, srcn, R, C in (
                        ("fkT", "L1_attn_Fk", 4096, D),
                        ("fvT", "L1_attn_Fv", D, 4096),
                        ("frT", "L1_attn_Fr", D, D),
                        ("wiT", "L1_ffn_wi", 4096, D),
                        ("woT", "L1_ffn_wo", D, 4096)):
                    if nm not in wbf:
                        wbf[nm] = k.dram("t_" + nm, [C, R], BF16)
                    build_transposed_dram(k, stt_, wbf[nm], wbf[srcn], R, C)
            emit_l1_attn(k, cm, xnT, s_dram, wbf, fin, scratch)
            emit_l1_ffn(k, cm, xnT, g_dram, wbf, fin, gates, scratch)
        elif lid == 2:
            emit_l2_attn(k, cm, xnT, s_dram, wbf, fin)
            emit_l2_ffn(k, cm, xnT, g_dram, wbf, fin, gates)
        elif lid == 3:
            with k.scope() as stt_:
                if "waT" not in wbf:
                    wbf["waT"] = k.dram("t_waT", [D, 3072], BF16)
                build_transposed_dram(k, stt_, wbf["waT"],
                                      wbf["L3_attn_W_attn"], 3072, D)
            emit_l3_attn(k, cm, xnT, s_dram, wbf, fin, scratch)
            emit_l3_ffn(k, cm, xnT, g_dram, wbf, fin, scratch)
        else:
            raise NotImplementedError("layer %d" % lid)
        emit_combine(k, cm, xin_dram, s_dram, sG if lid == 0 else None,
                     g_dram, xout_dram)


def build_program(layers, dump_names=()):
    nc = bass.Bass("TRN2", target_bir_lowering=False)
    ctx = contextlib.ExitStack()
    k = K(nc, ctx)
    k.dump_names = set(dump_names)

    fin = {}
    for name in CONST_SHAPES:
        shp, dt = CONST_SHAPES[name]
        fin[name] = nc.dram_tensor(name, shp, dt, kind="ExternalInput").ap()
    for lid in layers:
        for name in LAYER_INPUTS[lid]:
            fin[name] = nc.dram_tensor(name, WSHAPES[name], F32,
                                       kind="ExternalInput").ap()
    xin = nc.dram_tensor("xin", [D, T], F32, kind="ExternalInput").ap()
    xout = nc.dram_tensor("xout", [D, T], F32, kind="ExternalOutput").ap()

    cm = Common(k, fin)

    wbf = {}
    for lid in layers:
        for name in LAYER_CAST[lid]:
            wbf[name] = k.dram("bf_" + name, WSHAPES[name], BF16)
            emit_cast(k, wbf[name], fin[name])

    scratch = dict(
        s_dram=k.dram("s_scr", [D, T], F32),
        g_dram=k.dram("g_scr", [D, T], BF16),
        v_scratch=k.dram("v_scr", [T, D], BF16),
        sg_dram=k.dram("sg_scr", [16, 8, T], F32),
        png_dram=k.dram("png_scr", [128, T], BF16),
        x0_dram=k.dram("x0_scr", [D, T], BF16),
        xa_dram=k.dram("xa_scr", [D, T], BF16),
        xb_dram=k.dram("xb_scr", [D, T], BF16),
        row_dram=k.dram("row_scr", [NQT, 128], F32),
        beT_dram=k.dram("beT_scr", [8, 16, 4096], BF16),
        ao_dram=k.dram("ao_scr", [D, T], BF16),
        xtok_dram=k.dram("xtok_scr", [T, D], BF16),
        seq_dram=k.dram("seq_scr", [D, 1024], BF16),
    )

    bufs = [xin]
    for i in range(len(layers) - 1):
        bufs.append(k.dram("xmid%d" % i, [D, T], F32).ap)
    bufs.append(xout)
    for j, lid in enumerate(layers):
        emit_layer(k, cm, lid, bufs[j], bufs[j + 1], fin, wbf, scratch)
    k.barrier()
    k.finalize()
    ctx.close()
    return nc, k


LAUNCH_PLAN = [[0, 1, 2, 3]]


def kernel(**inputs):
    consts = host_constants()
    x = np.ascontiguousarray(inputs["L0_attn_x"])
    xT = [np.ascontiguousarray(x[b].T) for b in range(8)]

    for group in LAUNCH_PLAN:
        nc, k = build_program(group)
        in_maps = []
        for b in range(8):
            m = dict(consts)
            m["xin"] = xT[b]
            for lid in group:
                for name in LAYER_INPUTS[lid]:
                    m[name] = inputs[name]
            in_maps.append(m)
        res = run_bass_kernel_spmd(nc, in_maps, core_ids=list(range(8)))
        xT = [np.asarray(res.results[b]["xout"]) for b in range(8)]

    out = np.stack([xT[b].T for b in range(8)], axis=0)
    return np.ascontiguousarray(out.astype(np.float32))
